# version 09
# baseline (speedup 1.0000x reference)
import numpy as np
from contextlib import ExitStack

import concourse.bass as bass
import concourse.mybir as mybir
from concourse.bass_utils import run_bass_kernel_spmd

F32 = mybir.dt.float32
BF16 = mybir.dt.bfloat16
F16 = mybir.dt.float16
U16 = mybir.dt.uint16
I32 = mybir.dt.int32
AF = mybir.ActivationFunctionType
ALU = mybir.AluOpType
AXX = mybir.AxisListType.X

H = 2048
HC = 16
S = 8192
NQ = 4096
SCALE = 1.0 / float(np.sqrt(2048.0))
NEG = -1.0e30
EPS1 = 1e-5
EPS2 = 1e-5
EPSR = 1e-6
I2 = 11264
IC = 44
ENGINES = ("sync", "gpsimd", "vector", "scalar", "tensor")
Z = ("vs", 0)


class Ch:
    def __init__(self, ctx, nc, name, step):
        self.sem = ctx.enter_context(nc.semaphore(name))
        self.name = name
        self.n = 0
        self.step = step


class Rec:
    COMPUTE = ("gp", "vs", "cs", "ts")

    def __init__(self, ctx, nc):
        self.nc = nc
        self.ctx = ctx
        self.q = {e: [] for e in ENGINES}
        self.ch = {n: Ch(ctx, nc, n, 1) for n in self.COMPUTE}

    def chan(self, tag):
        if tag not in self.ch:
            self.ch[tag] = Ch(self.ctx, self.nc, tag, 16)
        return self.ch[tag]

    AUTOCH = {"vector": "vs", "scalar": "cs"}

    def op(self, eng, fn, tag=None, auto=True):
        if eng in ("vector", "scalar") and auto:
            ch = self.AUTOCH[eng]
            assert tag in (None, ch), (eng, tag)
            c = self.ch[ch]
            prev = c.n
            c.n += c.step
            mark = (ch, c.n)
            sem, step = c.sem, c.step

            def emit(E, fn=fn, sem=sem, step=step, prev=prev):
                if prev > 0:
                    E.wait_ge(sem, prev)
                fn(E).then_inc(sem, step)
            self.q[eng].append(emit)
            return mark
        if tag is None:
            self.q[eng].append(fn)
            return None
        c = self.chan(tag)
        c.n += c.step
        mark = (tag, c.n)
        sem, step = c.sem, c.step

        def emit(E, fn=fn, sem=sem, step=step):
            fn(E).then_inc(sem, step)
        self.q[eng].append(emit)
        return mark

    def wt(self, eng, mark):
        if mark is None:
            return
        tag, v = mark
        if v <= 0:
            return
        sem = self.chan(tag).sem
        self.q[eng].append(lambda E, sem=sem, v=v: E.wait_ge(sem, v))

    def marks(self):
        return [(n, c.n) for n, c in self.ch.items()]

    def barrier(self, mks):
        for e in ENGINES:
            for m in mks:
                self.wt(e, m)


def build(dbg=False, upto=3, pipe=True):
    nc = bass.Bass("TRN2", target_bir_lowering=False)
    x_in = nc.dram_tensor("x", [S, H], F32, kind="ExternalInput").ap()
    wqkv = nc.dram_tensor("wqkv", [H, 3 * H], F32, kind="ExternalInput").ap()
    wo = nc.dram_tensor("wo", [H, H], F32, kind="ExternalInput").ap()
    gam = nc.dram_tensor("gam", [128, HC], F32, kind="ExternalInput").ap()
    bet = nc.dram_tensor("bet", [128, HC], F32, kind="ExternalInput").ap()
    onoff_in = nc.dram_tensor("onoff", [128, 2], F32, kind="ExternalInput").ap()
    gup = nc.dram_tensor("gup", [I2, 256], I32, kind="ExternalInput").ap()
    gus = nc.dram_tensor("gus", [16, I2], F16, kind="ExternalInput").ap()
    dnp = nc.dram_tensor("dnp", [H, 704], I32, kind="ExternalInput").ap()
    dns = nc.dram_tensor("dns", [IC, H], F16, kind="ExternalInput").ap()
    out = nc.dram_tensor("out", [NQ, H], F32, kind="ExternalOutput").ap()

    kind = "ExternalOutput" if dbg else "Internal"
    s_wq = nc.dram_tensor("s_wq", [32, HC, 128, 128], BF16, kind=kind).ap()
    s_wv = nc.dram_tensor("s_wv", [HC, 128, H], BF16, kind=kind).ap()
    s_wo = nc.dram_tensor("s_wo", [HC, 128, H], BF16, kind=kind).ap()
    s_qT = nc.dram_tensor("s_qT", [8, HC, 128, 512], BF16, kind=kind).ap()
    s_kT = nc.dram_tensor("s_kT", [4, 4, HC, 128, 512], BF16, kind=kind).ap()
    s_v = nc.dram_tensor("s_v", [4, 2, 8, 128, H], BF16, kind=kind).ap()
    s_x2 = nc.dram_tensor("s_x2", [NQ, H], F32, kind=kind).ap()
    s_x2t = nc.dram_tensor("s_x2t", [HC, 32, 128, 128], BF16, kind=kind).ap()
    s_wgu = nc.dram_tensor("s_wgu", [44, HC, 128, 256], BF16, kind=kind).ap()
    s_wdn = nc.dram_tensor("s_wdn", [HC, IC, 128, 128], BF16, kind=kind).ap()
    s_y = nc.dram_tensor("s_y", [32, HC, 128, 128], F32, kind=kind).ap()

    with ExitStack() as ctx:
        sb = lambda *a: ctx.enter_context(nc.sbuf_tensor(*a))
        R = Rec(ctx, nc)

        ident_b = sb("ident_b", [128, 128], BF16)
        ident_f = sb("ident_f", [128, 128], F32)
        gam_sb = sb("gam_sb", [128, HC], F32)
        bet_sb = sb("bet_sb", [128, HC], F32)
        bet_bf = sb("bet_bf", [128, HC], BF16)
        onoff = sb("onoff_sb", [128, 2], F32)
        bw_pp = sb("bw_pp", [128, 32], F32)
        dump = sb("dump", [128, H], BF16)

        m0 = R.op("gpsimd", lambda E: E.memset(ident_b[:, :], 0.0), tag="gp")
        R.wt("gpsimd", m0)
        m0 = R.op("gpsimd", lambda E: E.affine_select(
            out=ident_b[:, :], in_=ident_b[:, :], compare_op=ALU.not_equal,
            fill=1.0, base=0, pattern=[[-1, 128]], channel_multiplier=1), tag="gp")
        R.wt("gpsimd", m0)
        m0 = R.op("gpsimd", lambda E: E.memset(ident_f[:, :], 0.0), tag="gp")
        R.wt("gpsimd", m0)
        m_const = R.op("gpsimd", lambda E: E.affine_select(
            out=ident_f[:, :], in_=ident_f[:, :], compare_op=ALU.not_equal,
            fill=1.0, base=0, pattern=[[-1, 128]], channel_multiplier=1), tag="gp")

        R.op("sync", lambda E: E.dma_start(out=gam_sb[:, :], in_=gam), tag="cst")
        R.op("sync", lambda E: E.dma_start(out=onoff[:, :], in_=onoff_in), tag="cst")
        m_gb = R.op("sync", lambda E: E.dma_start(out=bet_sb[:, :], in_=bet), tag="cst")
        R.wt("vector", m_gb)
        m_bb = R.op("vector", lambda E: E.tensor_copy(bet_bf[:, :], bet_sb[:, :]), tag="vs")

        p01 = ExitStack()
        bwvrow = p01.enter_context(nc.sbuf_tensor("bwvrow", [1, H], BF16))
        ones_row = p01.enter_context(nc.sbuf_tensor("ones_row", [1, 128], BF16))
        with ExitStack() as p0:
            wtile2 = [p0.enter_context(nc.sbuf_tensor("wtile%d" % i, [128, 3 * H], BF16))
                      for i in range(2)]
            wotile = p0.enter_context(nc.sbuf_tensor("wotile", [128, H], BF16))
            ps_bp = p0.enter_context(nc.psum_tensor("ps_bp", [128, 32], F32))
            ps_bv = p0.enter_context(nc.psum_tensor("ps_bv", [1, H], F32))

            m_wst = [Z, Z]
            m_mm = Z
            m_bv = Z
            for dc in range(HC):
                wbi = dc % 2
                wtile = wtile2[wbi]
                R.wt("gpsimd", m_wst[wbi])
                m_wl = R.op("gpsimd", lambda E, dc=dc, wtile=wtile: E.dma_start(
                    out=wtile[:, :], in_=wqkv[dc * 128:(dc + 1) * 128, :]),
                    tag="p0w%d" % wbi)
                R.wt("tensor", m_wl)
                R.wt("tensor", m_bb)
                for oc in range(32):
                    m_mm = R.op("tensor", lambda E, dc=dc, oc=oc, wtile=wtile:
                                nc.tensor.matmul(
                                    ps_bp[:, oc:oc + 1],
                                    wtile[:, oc * 128:(oc + 1) * 128],
                                    bet_bf[:, dc:dc + 1],
                                    start=(dc == 0 and oc == 0),
                                    stop=(dc == HC - 1 and oc == 31)),
                                tag=("ts" if oc == 31 else None))
                for sl in range(4):
                    m_bv = R.op("tensor", lambda E, dc=dc, sl=sl, wtile=wtile:
                                nc.tensor.matmul(
                                    ps_bv[:, sl * 512:(sl + 1) * 512],
                                    bet_bf[:, dc:dc + 1],
                                    wtile[:, 2 * H + sl * 512:2 * H + (sl + 1) * 512],
                                    start=(dc == 0), stop=(dc == HC - 1)),
                                tag=("ts" if sl == 3 else None))
                R.wt("vector", m_mm)
                R.wt("vector", m_bv)
                m_f1 = R.op("vector", lambda E, dc=dc, wtile=wtile: E.tensor_scalar(
                    wtile[:, :], wtile[:, :], gam_sb[:, dc:dc + 1], None, ALU.mult),
                    tag="vs")
                R.wt("gpsimd", m_f1)
                R.op("gpsimd", lambda E, dc=dc, wtile=wtile: E.dma_start(
                    out=s_wq[:, dc, :, :].rearrange("oc p k -> p oc k"),
                    in_=wtile[:, 0:2 * H].rearrange("p (oc k) -> p oc k", oc=32)),
                    tag="p0st%d" % wbi)
                m_wst[wbi] = R.op("gpsimd", lambda E, dc=dc, wtile=wtile: E.dma_start(
                    out=s_wv[dc, :, :], in_=wtile[:, 2 * H:3 * H]), tag="p0st%d" % wbi)

            R.wt("vector", m_mm)
            m_bwpp = R.op("vector", lambda E: E.tensor_copy(bw_pp[:, :], ps_bp[:, :]),
                          tag="vs")
            R.wt("scalar", m_bv)
            m_bvrow = R.op("scalar", lambda E: E.copy(bwvrow[:, :], ps_bv[:, :]), tag="cs")
            R.op("vector", lambda E: E.memset(ones_row[:, :], 1.0), tag="vs")

            m_wost = Z
            for vc in range(HC):
                R.wt("gpsimd", m_wost)
                m_wol = R.op("gpsimd", lambda E, vc=vc: E.dma_start(
                    out=wotile[:, :], in_=wo[vc * 128:(vc + 1) * 128, :]), tag="p0wo")
                R.wt("gpsimd", m_wol)
                m_wost = R.op("gpsimd", lambda E, vc=vc: E.dma_start(
                    out=s_wo[vc, :, :], in_=wotile[:, :]), tag="p0wo")

        R.barrier(R.marks())

        dq_holder = {}
        dqs = {"u": 0, "pk_use": [Z] * 6, "scl_use": [Z] * 3, "ob_use": [Z, Z]}

        def dq_unit(kindu, t, slc):
            dq_pk, dq_scl = dq_holder["pk"], dq_holder["scl"]
            dq_nib, dq_mag = dq_holder["nib"], dq_holder["mag"]
            dq_bits, dq_sb, dq_pred = dq_holder["bits"], dq_holder["sb"], dq_holder["pred"]
            dq_obf = dq_holder["obf"]
            u = dqs["u"]
            dqs["u"] += 1
            bi = u % 6
            si = u % 3
            pk = dq_pk[bi]
            scl = dq_scl[si]
            W = 1024
            R.wt("sync", dqs["pk_use"][bi])
            if kindu == "gu":
                src = gup.bitcast(U16)[slc * 1024:(slc + 1) * 1024, t * 128:(t + 1) * 128]
            else:
                src = dnp.bitcast(U16)[slc * 1024:(slc + 1) * 1024, t * 128:(t + 1) * 128]
            m_pk = R.op("sync", lambda E, src=src, pk=pk, W=W:
                        E.dma_start_transpose(pk[:, 0:W], src), tag="pk%d" % bi)
            R.wt("gpsimd", dqs["scl_use"][si])
            if kindu == "gu":
                ssrc = gus[4 * t:4 * t + 4, slc * 1024:(slc + 1) * 1024]
            else:
                ssrc = dns[4 * t:4 * t + 4, slc * 1024:(slc + 1) * 1024]
            m_scl = R.op("gpsimd", lambda E, ssrc=ssrc, scl=scl, W=W:
                         E.dma_start(out=scl[:, 0:W],
                                     in_=ssrc.unsqueeze(1).broadcast_to([4, 32, W])),
                         tag="scl%d" % si)
            R.wt("vector", m_pk)
            R.wt("vector", m_scl)
            for j in range(4):
                ob = dq_obf[j % 2]
                m1 = R.op("vector", lambda E, j=j, pk=pk, W=W: E.tensor_scalar(
                    dq_nib[:, 0:W], pk[:, 0:W], 4 * j, 15,
                    ALU.logical_shift_right, ALU.bitwise_and), tag="vs")
                R.op("vector", lambda E, W=W: E.tensor_scalar(
                    dq_mag[:, 0:W], dq_nib[:, 0:W], 7, None, ALU.bitwise_and))
                R.op("vector", lambda E, W=W: E.tensor_scalar(
                    dq_sb[:, 0:W], dq_nib[:, 0:W], 8, 12,
                    ALU.bitwise_and, ALU.logical_shift_left))
                R.op("vector", lambda E, W=W: E.tensor_scalar(
                    dq_bits[:, 0:W], dq_mag[:, 0:W], 64, 16128, ALU.mult, ALU.add))
                R.op("vector", lambda E, W=W: E.tensor_scalar(
                    dq_pred[:, 0:W], dq_mag[:, 0:W], 1, None, ALU.is_le))
                R.op("vector", lambda E, W=W: E.tensor_scalar(
                    dq_mag[:, 0:W], dq_mag[:, 0:W], 16128, None, ALU.mult))
                R.op("vector", lambda E, W=W: E.copy_predicated(
                    dq_bits[:, 0:W], dq_pred[:, 0:W], dq_mag[:, 0:W]))
                R.op("vector", lambda E, W=W: E.tensor_tensor(
                    dq_nib[:, 0:W], dq_bits[:, 0:W], dq_sb[:, 0:W], ALU.bitwise_or))
                R.wt("vector", dqs["ob_use"][j % 2])
                if kindu == "gu":
                    m7 = R.op("vector", lambda E, W=W, scl=scl, ob=ob: E.tensor_tensor(
                        ob[:, 0:W]
                        .rearrange("p (qh j q7) -> p qh j q7", qh=2, j=4),
                        dq_nib.bitcast(BF16)[:, 0:W]
                        .rearrange("p (qh q7 j) -> p qh j q7", qh=2, q7=128),
                        scl[:, 0:W]
                        .rearrange("p (qh q7 j) -> p qh j q7", qh=2, q7=128),
                        ALU.mult), tag="vs")
                else:
                    m7 = R.op("vector", lambda E, W=W, scl=scl, ob=ob: E.tensor_tensor(
                        ob[:, 0:W], dq_nib.bitcast(BF16)[:, 0:W],
                        scl[:, 0:W], ALU.mult), tag="vs")
                R.wt("gpsimd", m7)
                if kindu == "gu":
                    n = t * 4 + j
                    m_st = R.op("gpsimd", lambda E, n=n, slc=slc, ob=ob: E.dma_start(
                        out=s_wgu[4 * slc:4 * slc + 4, n, :, :]
                        .rearrange("pp p k -> p pp k"),
                        in_=ob[:, 0:1024]
                        .rearrange("p (pp k) -> p pp k", pp=4)), tag="dqst%d" % (j % 2))
                else:
                    mm = t * 4 + j
                    m_st = R.op("gpsimd", lambda E, mm=mm, slc=slc, ob=ob: E.dma_start(
                        out=s_wdn[8 * slc:8 * slc + 8, mm, :, :]
                        .rearrange("hc p k -> p hc k"),
                        in_=ob[:, 0:1024]
                        .rearrange("p (hc k) -> p hc k", hc=8)), tag="dqst%d" % (j % 2))
                dqs["ob_use"][j % 2] = m_st
            dqs["pk_use"][bi] = ("vs", R.ch["vs"].n)
            dqs["scl_use"][si] = ("vs", R.ch["vs"].n)
            return m_st

        dq_list = ([("gu", t, s_) for t in range(4) for s_ in range(11)] +
                   [("dn", t, s_) for t in range(11) for s_ in range(2)])
        dq_i = [0]
        dq_laststore = [Z]

        def emit_dq(nun):
            while nun > 0 and dq_i[0] < len(dq_list):
                dq_laststore[0] = dq_unit(*dq_list[dq_i[0]])
                dq_i[0] += 1
                nun -= 1

        with ExitStack() as p1:
            psb = lambda *a: p1.enter_context(nc.sbuf_tensor(*a))
            dq_holder["pk"] = [psb("dq_pk%d" % i, [128, 1024], U16) for i in range(6)]
            dq_holder["nib"] = psb("dq_nib", [128, 1024], U16)
            dq_holder["mag"] = psb("dq_mag", [128, 1024], U16)
            dq_holder["bits"] = psb("dq_bits", [128, 1024], U16)
            dq_holder["sb"] = psb("dq_sb", [128, 1024], U16)
            dq_holder["pred"] = psb("dq_pred", [128, 1024], U16)
            dq_holder["scl"] = [psb("dq_scl%d" % i, [128, 1024], BF16) for i in range(3)]
            dq_holder["obf"] = [psb("dq_ob%d" % i, [128, 1024], BF16) for i in range(2)]
            x_raw = [psb("x_raw0", [128, H], F32), psb("x_raw1", [128, H], F32)]
            xhat = [psb("xhat%d" % i, [128, H], BF16) for i in range(8)]
            xnT = psb("xnT", [128, HC, 1024], BF16)
            wqk = [psb("wqk0", [128, HC, 128], BF16), psb("wqk1", [128, HC, 128], BF16)]
            wv_sb = psb("wv_sb", [128, HC, H], BF16)
            qkstg = [psb("qkstg0", [128, 1024], BF16), psb("qkstg1", [128, 1024], BF16)]
            vstg = [psb("vstg0", [128, H], BF16), psb("vstg1", [128, H], BF16)]
            sums8 = psb("p1su", [128, 8], F32)
            ssq8 = psb("p1sq", [128, 8], F32)
            mu8 = psb("p1mu", [128, 8], F32)
            var8 = psb("p1va", [128, 8], F32)
            rstd8 = psb("p1rs", [128, 8], F32)
            nmr8 = psb("p1nm", [128, 8], F32)
            t18 = psb("p1t1", [128, 8], F32)

            xr_use = [Z, Z]
            xh_use = [Z] * 8
            wqk_use = [Z, Z]
            qkstg_use = [Z, Z]
            vstg_use = [Z, Z]
            xnt_use = Z
            v_done = (Z, Z)
            m_wv = Z
            ln_marks = {}

            def rec_ln(T):
                marks = []
                for s_g in range(8):
                    bi = s_g % 2
                    hb = s_g
                    R.wt("sync", xr_use[bi])
                    rows = T * 1024 + s_g * 128
                    m_x = R.op("sync", lambda E, bi=bi, rows=rows: E.dma_start(
                        out=x_raw[bi][:, :],
                        in_=x_in[rows:rows + 128, :]), tag="x%d" % bi)
                    xb = x_raw[bi]
                    sg = s_g
                    sums = sums8[:, sg:sg + 1]
                    ssq = ssq8[:, sg:sg + 1]
                    mu = mu8[:, sg:sg + 1]
                    var = var8[:, sg:sg + 1]
                    rstd = rstd8[:, sg:sg + 1]
                    nmr = nmr8[:, sg:sg + 1]
                    t1 = t18[:, sg:sg + 1]
                    R.wt("vector", m_x)
                    R.wt("scalar", m_x)
                    m_sum = R.op("vector", lambda E, xb=xb, sums=sums: E.reduce_sum(
                        sums, xb[:, :], axis=AXX), tag="vs")
                    m_ssq = R.op("scalar", lambda E, xb=xb, ssq=ssq: E.activation(
                        dump[:, :], xb[:, :], AF.Square,
                        accum_out=ssq), tag="cs")
                    R.wt("vector", m_ssq)
                    R.wt("vector", m_sum)
                    m1 = R.op("vector", lambda E, mu=mu, sums=sums: E.tensor_scalar(
                        mu, sums, 1.0 / H, None, ALU.mult), tag="vs")
                    m2 = R.op("vector", lambda E, t1=t1, ssq=ssq: E.tensor_scalar(
                        t1, ssq, 1.0 / H, None, ALU.mult), tag="vs")
                    m3 = R.op("vector", lambda E, var=var, mu=mu: E.tensor_mul(
                        var, mu, mu), tag="vs")
                    m4 = R.op("vector", lambda E, var=var, t1=t1:
                              nc.vector.scalar_tensor_tensor(
                                  var, var, -1.0, t1, ALU.mult, ALU.add), tag="vs")
                    m5 = R.op("vector", lambda E, var=var: E.tensor_scalar(
                        var, var, EPS1, None, ALU.add), tag="vs")
                    R.wt("scalar", m5)
                    m6 = R.op("scalar", lambda E, rstd=rstd, var=var: E.activation(
                        rstd, var, AF.Sqrt), tag="cs")
                    R.wt("vector", m6)
                    m7 = R.op("vector", lambda E, rstd=rstd: E.reciprocal(
                        rstd, rstd), tag="vs")
                    m8 = R.op("vector", lambda E, nmr=nmr, mu=mu, rstd=rstd:
                              nc.vector.scalar_tensor_tensor(
                                  nmr, mu, -1.0, rstd, ALU.mult, ALU.mult), tag="vs")
                    R.wt("scalar", m8)
                    R.wt("scalar", xh_use[hb])
                    m_xh = R.op("scalar", lambda E, xb=xb, hb=hb, nmr=nmr, rstd=rstd:
                                E.activation(xhat[hb][:, :], xb[:, :], AF.Identity,
                                             bias=nmr, scale=rstd),
                                tag="cs")
                    xr_use[bi] = m_xh
                    marks.append(m_xh)
                ln_marks[T] = marks

            def rec_tr(T):
                nonlocal xnt_use, v_done, m_wv, xnt_rdy
                with ExitStack() as pT:
                    ps_tr2 = [pT.enter_context(
                        nc.psum_tensor("p1tr%d_%d" % (T, i), [128, 1024], BF16))
                        for i in range(2)]
                    tr_evac = [Z, Z]
                    nbatch = 0
                    for s_g in range(8):
                        hb = s_g
                        R.wt("tensor", ln_marks[T][s_g])
                        R.wt("tensor", m_const)
                        R.wt("tensor", xnt_use)
                        R.wt("tensor", v_done[0])
                        R.wt("tensor", v_done[1])
                        for dcg in range(4):
                            tb = nbatch % 2
                            nbatch += 1
                            R.wt("tensor", tr_evac[tb])
                            for k in range(4):
                                dc = dcg * 4 + k
                                m_tr = R.op("tensor", lambda E, hb=hb, dc=dc, k=k, tb=tb:
                                            nc.tensor.transpose(
                                                ps_tr2[tb][:, k * 128:(k + 1) * 128],
                                                xhat[hb][:, dc * 128:(dc + 1) * 128],
                                                ident_b[:, :]),
                                            tag=("ts" if k == 3 else None))
                            R.wt("vector", m_tr)
                            tr_evac[tb] = R.op(
                                "vector", lambda E, dcg=dcg, s_g=s_g, tb=tb:
                                E.tensor_copy(
                                    xnT[:, 4 * dcg:4 * dcg + 4,
                                        s_g * 128:(s_g + 1) * 128],
                                    ps_tr2[tb][:, 0:512].rearrange(
                                        "p (d k) -> p d k", d=4)), tag="vs")
                        xh_use[hb] = ("ts", R.ch["ts"].n)
                    xnt_rdy = (tr_evac[0], tr_evac[1])
                    if T == 0:
                        m_wv = R.op("sync", lambda E: E.dma_start(
                            out=wv_sb[:, :, :],
                            in_=s_wv.rearrange("d p h -> p d h")), tag="wv")

            def rec_qkv(T):
                nonlocal xnt_use, v_done, m_wv, qk_done
                has_q = T < 4
                with ExitStack() as pQ:
                    ps_qk = [pQ.enter_context(nc.psum_tensor(
                        "p1qk%d_%d" % (T, i), [128, 1024], F32)) for i in range(2)]
                    psqk_use = [Z, Z]
                    oc_list = list(range(32)) if has_q else list(range(16, 32))
                    for idx, oc in enumerate(oc_list):
                        wb = idx % 2
                        pb = idx % 2
                        R.wt("sync", wqk_use[wb])
                        m_w = R.op("sync", lambda E, oc=oc, wb=wb: E.dma_start(
                            out=wqk[wb][:, :, :],
                            in_=s_wq[oc, :, :, :].rearrange("d p k -> p d k")),
                            tag="wqk%d" % wb)
                        R.wt("tensor", m_w)
                        R.wt("tensor", xnt_rdy[0])
                        R.wt("tensor", xnt_rdy[1])
                        R.wt("tensor", psqk_use[pb])
                        for dc in range(HC):
                            for h2 in range(2):
                                m_mm = R.op("tensor", lambda E, wb=wb, dc=dc, h2=h2, pb=pb:
                                            nc.tensor.matmul(
                                                ps_qk[pb][:, h2 * 512:(h2 + 1) * 512],
                                                wqk[wb][:, dc, :],
                                                xnT[:, dc, h2 * 512:(h2 + 1) * 512],
                                                start=(dc == 0), stop=(dc == HC - 1)),
                                            tag=("ts" if dc == HC - 1 and h2 == 1 else None))
                        wqk_use[wb] = ("ts", R.ch["ts"].n)
                        R.wt("scalar", m_mm)
                        R.wt("scalar", m_bwpp)
                        R.wt("scalar", qkstg_use[pb])
                        m_ev = R.op("scalar", lambda E, oc=oc, pb=pb: E.activation(
                            qkstg[pb][:, :], ps_qk[pb][:, :], AF.Identity,
                            bias=bw_pp[:, oc:oc + 1]), tag="cs")
                        psqk_use[pb] = m_ev
                        R.wt("sync", m_ev)
                        if oc < 16:
                            m_st = R.op("sync", lambda E, oc=oc, T=T, pb=pb: E.dma_start(
                                out=s_qT[2 * T:2 * T + 2, oc, :, :]
                                .rearrange("g p k -> p g k"),
                                in_=qkstg[pb][:, :].rearrange("p (g k) -> p g k", g=2)),
                                tag="qst%d" % pb)
                        else:
                            kb, hf = T >> 1, T & 1
                            m_st = R.op("sync", lambda E, oc=oc, kb=kb, hf=hf, pb=pb:
                                        E.dma_start(
                                            out=s_kT[kb, 2 * hf:2 * hf + 2, oc - 16, :, :]
                                            .rearrange("qq p k -> p qq k"),
                                            in_=qkstg[pb][:, :].rearrange(
                                                "p (qq k) -> p qq k", qq=2)),
                                        tag="qst%d" % pb)
                        qkstg_use[pb] = m_st
                    qk_done = (psqk_use[0], psqk_use[1])

                with ExitStack() as pV:
                    ps_v = [pV.enter_context(nc.psum_tensor(
                        "p1v%d_%d" % (T, i), [128, H], F32)) for i in range(2)]
                    psv_use = [Z, Z]
                    R.wt("tensor", m_wv)
                    R.wt("tensor", qk_done[0])
                    R.wt("tensor", qk_done[1])
                    for tc in range(8):
                        pb = tc % 2
                        R.wt("tensor", psv_use[pb])
                        for dc in range(HC):
                            for sl in range(4):
                                R.op("tensor", lambda E, dc=dc, sl=sl, tc=tc, pb=pb:
                                     nc.tensor.matmul(
                                         ps_v[pb][:, sl * 512:(sl + 1) * 512],
                                         xnT[:, dc, tc * 128:(tc + 1) * 128],
                                         wv_sb[:, dc, sl * 512:(sl + 1) * 512],
                                         start=(dc == 0), stop=False))
                        for sl in range(4):
                            m_mm = R.op("tensor", lambda E, sl=sl, pb=pb:
                                        nc.tensor.matmul(
                                            ps_v[pb][:, sl * 512:(sl + 1) * 512],
                                            ones_row[0:1, :],
                                            bwvrow[0:1, sl * 512:(sl + 1) * 512],
                                            start=False, stop=True),
                                        tag=("ts" if sl == 3 else None))
                        R.wt("scalar", m_mm)
                        R.wt("scalar", vstg_use[pb])
                        m_ev = R.op("scalar", lambda E, pb=pb: E.copy(
                            vstg[pb][:, :], ps_v[pb][:, :]), tag="cs")
                        psv_use[pb] = m_ev
                        R.wt("sync", m_ev)
                        gtc = T * 8 + tc
                        kb, hf, tcl = gtc >> 4, (gtc >> 3) & 1, gtc & 7
                        m_st = R.op("sync", lambda E, kb=kb, hf=hf, tcl=tcl, pb=pb:
                                    E.dma_start(out=s_v[kb, hf, tcl, :, :],
                                                in_=vstg[pb][:, :]), tag="vst%d" % pb)
                        vstg_use[pb] = m_st
                    xnt_use = ("ts", R.ch["ts"].n)
                    v_done = (psv_use[0], psv_use[1])
                emit_dq(7)

            qk_done = (Z, Z)
            xnt_rdy = (Z, Z)
            if pipe:
                rec_ln(0)
                for T in range(8):
                    rec_tr(T)
                    if T < 7:
                        rec_ln(T + 1)
                    rec_qkv(T)
            else:
                for T in range(8):
                    rec_ln(T)
                    rec_tr(T)
                    rec_qkv(T)

        emit_dq(100)
        p01.close()
        R.barrier(R.marks())

        if upto >= 2:
            with ExitStack() as p2:
                psb = lambda *a: p2.enter_context(nc.sbuf_tensor(*a))
                mdiag = psb("mdiag", [128, 3968], BF16)
                qarena = psb("qarena", [128, HC, 512], BF16)
                kv = [psb("kv0", [128, 8192], BF16), psb("kv1", [128, 8192], BF16)]
                parena = psb("parena", [128, 4, 4096], BF16)
                ptar = psb("ptar", [128, 32, 512], BF16)
                osb = psb("osb", [128, 4, H], F32)
                msk = [psb("msk0", [128, 1024], F32), psb("msk1", [128, 1024], F32)]
                wost = [psb("wost0", [128, H], BF16), psb("wost1", [128, H], BF16)]
                xres = psb("xres", [128, H], F32)
                r2 = psb("r2", [128, H], F32)
                x2f = psb("x2f", [128, H], F32)
                x2b = psb("x2b", [128, H], BF16)
                x2stg = psb("x2stg", [128, 512], BF16)
                mrun = psb("mrun", [128, 4], F32)
                mprev = psb("mprev", [128, 4], F32)
                mbuf = psb("mbuf", [128, 4, 4], F32)
                lbuf = psb("lbuf", [128, 4, 4], F32)
                gbuf = psb("gbuf", [128, 4, 4], F32)
                gx = psb("gx", [128, 4], F32)
                lsb = psb("lsb", [128, 4], F32)
                lpass = psb("lpass", [128, 1], F32)
                negb = psb("negb", [128, 1], F32)
                negb2g = psb("negb2", [128, 4], F32)
                rl = psb("rl", [128, 1], F32)
                st2g = psb("st2", [128, 4], F32)
                bmax = psb("bmax", [128, 1], F32)
                st3g = psb("st3", [128, 4], F32)
                st4g = psb("st4", [128, 4], F32)

                q_use = Z
                ge_done = (Z, Z)
                x2ev_done = Z
                rl_use = Z
                x2f_use = Z
                x2tx_prev = Z
                osb_use = [Z] * 4
                kv_use = [Z, Z]
                msk_use = [Z, Z]
                wost_use = [Z, Z]
                xres_use = Z
                negb_use = Z
                x2stg_use = Z
                parena_use = [Z] * 4

                m0b = R.op("gpsimd", lambda E: E.memset(mdiag[:, :], 0.0), tag="gp")
                R.wt("gpsimd", m0b)
                m_mdiag = R.op("gpsimd", lambda E: E.affine_select(
                    out=mdiag[:, :], in_=mdiag[:, :], compare_op=ALU.is_ge,
                    fill=NEG, base=1920, pattern=[[-1, 3968]], channel_multiplier=1), tag="gp")
                R.wt("vector", m_mdiag)

                GRP_PASSES = [[[0, 2]]] * 4 + [[[0, 1], [2, 3]]] * 4

                def slot_type(g, kb):
                    if g < 4:
                        return "diag" if kb == 0 else "on0"
                    return {0: "full", 1: "diag", 2: "full", 3: "on1"}[kb]

                for g in range(8):
                    R.wt("sync", q_use)
                    m_q = R.op("sync", lambda E, g=g: E.dma_start(
                        out=qarena[:, :, :],
                        in_=s_qT[g, :, :, :].rearrange("d p k -> p d k")), tag="q2")
                    m_mi = R.op("vector", lambda E: E.memset(mrun[:, :], -3.0e38), tag="vs")
                    passes = GRP_PASSES[g]
                    pv_last = [Z] * 4
                    o_fold = [Z] * 4
                    for pidx, kbs in enumerate(passes):
                        if pidx == 0:
                            R.wt("tensor", ge_done[0])
                            R.wt("tensor", ge_done[1])
                        else:
                            for s4 in range(4):
                                R.wt("tensor", o_fold[s4])
                        with ExitStack() as pqk:
                            ps_qk = [pqk.enter_context(nc.psum_tensor(
                                "p2qk%d_%d_%d" % (g, pidx, i), [128, 1024], F32))
                                for i in range(2)]
                            ps_pt = pqk.enter_context(nc.psum_tensor(
                                "p2pt%d_%d" % (g, pidx), [128, 1024], BF16))
                            psqk_use = [Z, Z]
                            pt_evac = Z
                            for ki, kb in enumerate(kbs):
                                styp = slot_type(g, kb)
                                for hf in range(2):
                                    hidx = ki * 2 + hf
                                    m_k = [Z, Z]
                                    for q2 in range(2):
                                        kbuf = q2
                                        R.wt("sync", kv_use[kbuf])
                                        m_k[q2] = R.op("sync",
                                                       lambda E, kb=kb, hf=hf, q2=q2, kbuf=kbuf:
                                                       E.dma_start(
                                                           out=kv[kbuf].rearrange(
                                                               "p (d k) -> p d k", d=HC),
                                                           in_=s_kT[kb, 2 * hf + q2, :, :, :]
                                                           .rearrange("d p k -> p d k")),
                                                       tag="kv%d" % kbuf)
                                    for s in range(4):
                                        pb = s % 2
                                        R.wt("tensor", m_q)
                                        R.wt("tensor", psqk_use[pb])
                                        for q2 in range(2):
                                            kbuf = q2
                                            R.wt("tensor", m_k[q2])
                                            for dc in range(HC):
                                                m_mm = R.op(
                                                    "tensor",
                                                    lambda E, dc=dc, q2=q2, s=s, pb=pb, kbuf=kbuf:
                                                    nc.tensor.matmul(
                                                        ps_qk[pb][:, q2 * 512:(q2 + 1) * 512],
                                                        qarena[:, dc, s * 128:(s + 1) * 128],
                                                        kv[kbuf].rearrange(
                                                            "p (d k) -> p d k", d=HC)[:, dc, :],
                                                        start=(dc == 0), stop=(dc == HC - 1)),
                                                    tag=("ts" if dc == HC - 1 else None))
                                            kv_use[kbuf] = ("ts", R.ch["ts"].n)
                                        q_use = ("ts", R.ch["ts"].n)
                                        mb = s % 2
                                        R.wt("vector", m_mm)
                                        R.wt("vector", msk_use[mb])
                                        r_q = (g % 4) * 4 + s
                                        if styp == "diag":
                                            off = 1920 - 128 * r_q + hf * 1024
                                            m_msk = R.op("vector", lambda E, pb=pb, mb=mb, off=off:
                                                         E.tensor_add(
                                                             msk[mb][:, :], ps_qk[pb][:, :],
                                                             mdiag[:, off:off + 1024]), tag="vs")
                                        elif styp == "full":
                                            m_msk = R.op("vector", lambda E, pb=pb, mb=mb:
                                                         E.tensor_copy(
                                                             msk[mb][:, :], ps_qk[pb][:, :]),
                                                         tag="vs")
                                        else:
                                            col = 0 if styp == "on0" else 1
                                            m_msk = R.op("vector", lambda E, pb=pb, mb=mb, col=col:
                                                         E.tensor_scalar(
                                                             msk[mb][:, :], ps_qk[pb][:, :],
                                                             onoff[:, col:col + 1], None,
                                                             ALU.add), tag="vs")
                                        psqk_use[pb] = m_msk
                                        R.wt("vector", m_msk)
                                        if pidx == 0 and hidx == 0:
                                            R.wt("vector", m_mi)
                                        m_bm = R.op("vector", lambda E, mb=mb: E.reduce_max(
                                            bmax[:, 0:1], msk[mb][:, :], axis=AXX), tag="vs")
                                        R.wt("vector", m_bm)
                                        m_mx = R.op("vector", lambda E, s=s: E.tensor_max(
                                            mrun[:, s:s + 1], mrun[:, s:s + 1], bmax[:, 0:1]),
                                            tag="vs")
                                        R.wt("vector", m_mx)
                                        m_sv = R.op("vector", lambda E, s=s, hidx=hidx:
                                                    E.tensor_copy(mbuf[:, s, hidx:hidx + 1],
                                                                  mrun[:, s:s + 1]), tag="vs")
                                        R.wt("vector", m_sv)
                                        R.wt("vector", negb_use)
                                        m_nb = R.op("vector", lambda E, s=s: E.tensor_scalar(
                                            negb[:, :], mrun[:, s:s + 1], -SCALE, None,
                                            ALU.mult), tag="vs")
                                        R.wt("scalar", m_nb)
                                        if pidx == 0 and hidx == 0:
                                            R.wt("scalar", parena_use[s])
                                        m_exp = R.op("scalar", lambda E, mb=mb, s=s, hidx=hidx:
                                                     E.activation(
                                                         parena[:, s,
                                                                hidx * 1024:(hidx + 1) * 1024],
                                                         msk[mb][:, :], AF.Exp,
                                                         bias=negb[:, 0:1], scale=SCALE,
                                                         accum_out=lbuf[:, s, hidx:hidx + 1]),
                                                     tag="cs")
                                        msk_use[mb] = m_exp
                                        negb_use = m_exp
                            for s in range(4):
                                R.wt("vector", negb_use)
                                m_nb = R.op("vector", lambda E, s=s: E.tensor_scalar(
                                    negb[:, :], mrun[:, s:s + 1], -SCALE, None, ALU.mult),
                                    tag="vs")
                                R.wt("scalar", m_nb)
                                R.wt("scalar", msk_use[0])
                                R.wt("scalar", msk_use[1])
                                m_gb2 = R.op("scalar", lambda E, s=s: E.activation(
                                    gbuf[:, s, :], mbuf[:, s, :], AF.Exp, bias=negb[:, 0:1],
                                    scale=SCALE), tag="cs")
                                negb_use = m_gb2
                                if pidx > 0:
                                    m_gx = R.op("scalar", lambda E, s=s: E.activation(
                                        gx[:, s:s + 1], mprev[:, s:s + 1], AF.Exp,
                                        bias=negb[:, 0:1], scale=SCALE), tag="cs")
                                    negb_use = m_gx
                                R.wt("vector", m_gb2)
                                m_rs = Z
                                for hidx in range(4):
                                    m_rs = R.op("vector", lambda E, s=s, hidx=hidx:
                                                E.tensor_scalar(
                                                    parena[:, s, hidx * 1024:(hidx + 1) * 1024],
                                                    parena[:, s, hidx * 1024:(hidx + 1) * 1024],
                                                    gbuf[:, s, hidx:hidx + 1], None, ALU.mult),
                                                tag="vs")
                                R.wt("vector", m_rs)
                                m_lg = R.op("vector", lambda E, s=s: E.tensor_mul(
                                    gbuf[:, s, :], lbuf[:, s, :], gbuf[:, s, :]), tag="vs")
                                R.wt("vector", m_lg)
                                m_lp = R.op("vector", lambda E, s=s: E.reduce_sum(
                                    lpass[:, 0:1], gbuf[:, s, :],
                                    axis=AXX), tag="vs")
                                R.wt("vector", m_lp)
                                if pidx == 0:
                                    R.op("vector", lambda E, s=s: E.tensor_copy(
                                        lsb[:, s:s + 1], lpass[:, 0:1]), tag="vs")
                                else:
                                    R.wt("vector", m_gx)
                                    R.op("vector", lambda E, s=s:
                                         nc.vector.scalar_tensor_tensor(
                                             lsb[:, s:s + 1], lsb[:, s:s + 1], gx[:, s:s + 1],
                                             lpass[:, 0:1], ALU.mult, ALU.add), tag="vs")
                                R.op("vector", lambda E, s=s: E.tensor_copy(
                                    mprev[:, s:s + 1], mrun[:, s:s + 1]), tag="vs")
                                R.wt("tensor", m_rs)
                                for kcg in range(8):
                                    R.wt("tensor", pt_evac)
                                    for k in range(4):
                                        kc = kcg * 4 + k
                                        m_tp = R.op("tensor", lambda E, s=s, kc=kc, k=k:
                                                    nc.tensor.transpose(
                                                        ps_pt[:, k * 256:k * 256 + 128],
                                                        parena[:, s, kc * 128:(kc + 1) * 128],
                                                        ident_b[:, :]),
                                                    tag=("ts" if k == 3 else None))
                                    R.wt("vector", m_tp)
                                    pt_evac = R.op("vector", lambda E, s=s, kcg=kcg:
                                                   E.tensor_copy(
                                                       ptar[:, 4 * kcg:4 * kcg + 4,
                                                            s * 128:(s + 1) * 128],
                                                       ps_pt[:, :].rearrange(
                                                           "p (k q) -> p k q", k=4)[:, :, 0:128]),
                                                   tag="vs")
                        with ExitStack() as ppv:
                            ps_o = [ppv.enter_context(nc.psum_tensor(
                                "p2o%d_%d_%d" % (g, pidx, i), [128, H], F32)) for i in range(2)]
                            R.wt("tensor", x2ev_done)
                            pso_use = [o_fold[0], o_fold[1]]
                            for pair in ((0, 1), (2, 3)):
                                for kcg in range(8):
                                    vbuf = kcg % 2
                                    R.wt("sync", kv_use[vbuf])
                                    kb0 = kbs[kcg // 4]
                                    hf0 = (kcg % 4) // 2
                                    tc0 = (kcg % 2) * 4
                                    m_v = R.op("sync", lambda E, kb0=kb0, hf0=hf0, tc0=tc0,
                                               vbuf=vbuf: E.dma_start(
                                                   out=kv[vbuf].rearrange(
                                                       "p (c v) -> p c v", c=4),
                                                   in_=s_v[kb0, hf0, tc0:tc0 + 4, :, :]
                                                   .rearrange("c p v -> p c v")),
                                           tag="kv%d" % vbuf)
                                    for k in range(4):
                                        kc = kcg * 4 + k
                                        for s in pair:
                                            R.wt("tensor", m_v)
                                            R.wt("tensor", pt_evac)
                                            if kc == 0:
                                                R.wt("tensor", pso_use[s % 2])
                                            for sl in range(4):
                                                m_pv = R.op(
                                                    "tensor",
                                                    lambda E, kc=kc, s=s, sl=sl, k=k, vbuf=vbuf:
                                                    nc.tensor.matmul(
                                                        ps_o[s % 2][:, sl * 512:(sl + 1) * 512],
                                                        ptar[:, kc, s * 128:(s + 1) * 128],
                                                        kv[vbuf].rearrange(
                                                            "p (c v) -> p c v", c=4)
                                                        [:, k, sl * 512:(sl + 1) * 512],
                                                        start=(kc == 0), stop=(kc == 31)),
                                                    tag=("ts" if sl == 3 else None))
                                            pv_last[s] = m_pv
                                    kv_use[vbuf] = ("ts", R.ch["ts"].n)
                                for s in pair:
                                    R.wt("vector", pv_last[s])
                                    R.wt("vector", osb_use[s])
                                    if pidx == 0:
                                        m_of = R.op("vector", lambda E, s=s: E.tensor_copy(
                                            osb[:, s, :], ps_o[s % 2][:, :]), tag="vs")
                                    else:
                                        R.wt("vector", negb_use)
                                        m_of = R.op("vector", lambda E, s=s:
                                                    nc.vector.scalar_tensor_tensor(
                                                        osb[:, s, :], osb[:, s, :],
                                                        gx[:, s:s + 1], ps_o[s % 2][:, :],
                                                        ALU.mult, ALU.add), tag="vs")
                                    o_fold[s] = m_of
                                    pso_use[s % 2] = m_of

                    with ExitStack() as pop:
                        ps_at = pop.enter_context(nc.psum_tensor("p3at%d" % g, [128, H], F32))
                        ps_ot = pop.enter_context(nc.psum_tensor("p3ot%d" % g, [128, 1024], BF16))
                        ps_pad = pop.enter_context(nc.psum_tensor("p3pd%d" % g, [128, 512], F32))
                        ps_xt = pop.enter_context(nc.psum_tensor("p3xt%d" % g, [128, 1024], BF16))
                        ot_evac = Z
                        m_ob = [Z] * 4
                        for s in range(4):
                            R.wt("vector", o_fold[s])
                            R.wt("vector", rl_use)
                            m_rl = R.op("vector", lambda E, s=s: E.reciprocal(
                                rl[:, 0:1], lsb[:, s:s + 1]), tag="vs")
                            R.wt("scalar", m_rl)
                            R.wt("scalar", pt_evac)
                            m_ob[s] = R.op("scalar", lambda E, s=s: E.activation(
                                parena[:, s, 0:H], osb[:, s, :], AF.Copy,
                                scale=rl[:, 0:1]), tag="cs")
                            rl_use = m_ob[s]
                            osb_use[s] = m_ob[s]
                        for s in range(4):
                            R.wt("tensor", m_ob[s])
                            for vcg in range(4):
                                R.wt("tensor", ot_evac)
                                for k in range(4):
                                    vc = vcg * 4 + k
                                    m_to = R.op("tensor", lambda E, s=s, vc=vc, k=k:
                                                nc.tensor.transpose(
                                                    ps_ot[:, k * 256:k * 256 + 128],
                                                    parena[:, s, vc * 128:(vc + 1) * 128],
                                                    ident_b[:, :]),
                                                tag=("ts" if k == 3 else None))
                                R.wt("vector", m_to)
                                ot_evac = R.op("vector", lambda E, s=s, vcg=vcg: E.tensor_copy(
                                    ptar[:, 4 * vcg:4 * vcg + 4, s * 128:(s + 1) * 128],
                                    ps_ot[:, :].rearrange("p (k q) -> p k q", k=4)[:, :, 0:128]),
                                    tag="vs")
                            parena_use[s] = ("ts", R.ch["ts"].n)
                        m_r2 = Z
                        for s in range(4):
                            R.wt("sync", xres_use)
                            m_xr = R.op("sync", lambda E, g=g, s=s: E.dma_start(
                                out=xres[:, :],
                                in_=x_in[g * 512 + s * 128:g * 512 + (s + 1) * 128, :]),
                                tag="xr")
                            for vc in range(HC):
                                wb = vc % 2
                                R.wt("sync", wost_use[wb])
                                m_w = R.op("sync", lambda E, vc=vc, wb=wb: E.dma_start(
                                    out=wost[wb][:, :], in_=s_wo[vc, :, :]), tag="wo%d" % wb)
                                R.wt("tensor", m_w)
                                R.wt("tensor", ot_evac)
                                if vc == 0:
                                    R.wt("tensor", m_r2)
                                for sl in range(4):
                                    m_op = R.op("tensor", lambda E, vc=vc, s=s, sl=sl, wb=wb:
                                                nc.tensor.matmul(
                                                    ps_at[:, sl * 512:(sl + 1) * 512],
                                                    ptar[:, vc, s * 128:(s + 1) * 128],
                                                    wost[wb][:, sl * 512:(sl + 1) * 512],
                                                    start=(vc == 0), stop=(vc == HC - 1)),
                                                tag=("ts" if sl == 3 else None))
                                wost_use[wb] = ("ts", R.ch["ts"].n)
                            st2 = st2g[:, s:s + 1]
                            st3 = st3g[:, s:s + 1]
                            st4 = st4g[:, s:s + 1]
                            negb2 = negb2g[:, s:s + 1]
                            R.wt("vector", m_op)
                            R.wt("vector", m_xr)
                            m_r2 = R.op("vector", lambda E: nc.vector.scalar_tensor_tensor(
                                r2[:, :], xres[:, :], 2.0, ps_at[:, :],
                                ALU.mult, ALU.add), tag="vs")
                            xres_use = m_r2
                            R.wt("vector", m_r2)
                            m_sum = R.op("vector", lambda E, st2=st2: E.reduce_sum(
                                st2, r2[:, :], axis=AXX), tag="vs")
                            R.wt("scalar", m_r2)
                            m_ssq = R.op("scalar", lambda E, st3=st3: E.activation(
                                dump[:, :], r2[:, :], AF.Square, accum_out=st3),
                                tag="cs")
                            R.wt("vector", m_sum)
                            R.wt("vector", m_ssq)
                            m1 = R.op("vector", lambda E, st2=st2: E.tensor_scalar(
                                st2, st2, 1.0 / H, None, ALU.mult), tag="vs")
                            R.wt("vector", m1)
                            m2 = R.op("vector", lambda E, st3=st3: E.tensor_scalar(
                                st3, st3, 1.0 / H, None, ALU.mult), tag="vs")
                            R.wt("vector", m2)
                            m3 = R.op("vector", lambda E, st4=st4, st2=st2: E.tensor_mul(
                                st4, st2, st2), tag="vs")
                            R.wt("vector", m3)
                            m4 = R.op("vector", lambda E, st4=st4, st3=st3: nc.vector.scalar_tensor_tensor(
                                st4, st4, -1.0, st3, ALU.mult, ALU.add),
                                tag="vs")
                            R.wt("vector", m4)
                            m5 = R.op("vector", lambda E, st4=st4: E.tensor_scalar(
                                st4, st4, EPS2, None, ALU.add), tag="vs")
                            R.wt("scalar", m5)
                            m6 = R.op("scalar", lambda E, st4=st4: E.activation(
                                st4, st4, AF.Sqrt), tag="cs")
                            R.wt("vector", m6)
                            m7 = R.op("vector", lambda E, st4=st4: E.reciprocal(st4, st4),
                                      tag="vs")
                            R.wt("vector", m7)
                            m8 = R.op("vector", lambda E, negb2=negb2, st2=st2, st4=st4: nc.vector.scalar_tensor_tensor(
                                negb2, st2, -1.0, st4, ALU.mult, ALU.mult),
                                tag="vs")
                            R.wt("scalar", m8)
                            R.wt("scalar", x2f_use)
                            m_x2 = R.op("scalar", lambda E, negb2=negb2, st4=st4: E.activation(
                                x2f[:, :], r2[:, :], AF.Identity, bias=negb2,
                                scale=st4), tag="cs")
                            R.wt("gpsimd", m_x2)
                            m_sx2 = R.op("gpsimd", lambda E, g=g, s=s: E.dma_start(
                                out=s_x2[g * 512 + s * 128:g * 512 + (s + 1) * 128, :],
                                in_=x2f[:, :]), tag="x2st")
                            x2f_use = m_sx2
                            R.wt("vector", m_x2)
                            R.wt("vector", x2tx_prev)
                            m_sg = R.op("vector", lambda E: E.tensor_copy(
                                x2b[:, :].rearrange("p (t j q) -> p t j q", t=4, j=4),
                                x2f[:, :].rearrange("p (t q j) -> p t j q", t=4, j=4)), tag="vs")
                            R.wt("tensor", m_sg)
                            x2_evac = Z
                            for ng in range(4):
                                R.wt("tensor", x2_evac)
                                for k in range(4):
                                    n = ng * 4 + k
                                    m_tx = R.op("tensor", lambda E, n=n, k=k:
                                                nc.tensor.transpose(
                                                    ps_xt[:, k * 256:k * 256 + 128],
                                                    x2b[:, n * 128:(n + 1) * 128],
                                                    ident_b[:, :]),
                                                tag=("ts" if k == 3 else None))
                                R.wt("vector", m_tx)
                                R.wt("vector", x2stg_use)
                                x2_evac = R.op("vector", lambda E: E.tensor_copy(
                                    x2stg[:, :].rearrange("p (k q) -> p k q", k=4),
                                    ps_xt[:, :].rearrange("p (k q) -> p k q", k=4)[:, :, 0:128]),
                                    tag="vs")
                                R.wt("gpsimd", x2_evac)
                                tc = g * 4 + s
                                x2stg_use = R.op("gpsimd", lambda E, ng=ng, tc=tc: E.dma_start(
                                    out=s_x2t[4 * ng:4 * ng + 4, tc, :, :]
                                    .rearrange("n p k -> p n k"),
                                    in_=x2stg[:, :].rearrange("p (n k) -> p n k", n=4)),
                                    tag="x2tst")
                            x2tx_prev = ("ts", R.ch["ts"].n)
                            ge_done = (m_r2, ot_evac)
                            x2ev_done = x2_evac

            R.barrier(R.marks())

        if upto >= 3:
            with ExitStack() as p4:
                psb = lambda *a: p4.enter_context(nc.sbuf_tensor(*a))
                harena = psb("harena", [128, IC, 1024], BF16)
                x2t = psb("x2t", [128, HC, 1024], BF16)
                wstr = [psb("wstr%d" % i, [128, HC, 256], BF16) for i in range(4)]
                wdnb = [psb("wdnb%d" % i, [128, IC * 128], BF16) for i in range(2)]
                silu_t = psb("silu_t", [128, 1024], BF16)
                ysb = psb("ysb", [128, 1024], F32)
                ytst = psb("ytst", [128, 1024], F32)
                ytile = [psb("ytile0", [128, H], F32)]
                x2res = psb("x2res", [128, H], F32)
                s52 = psb("s52", [128, 1], F32)
                s53 = psb("s53", [128, 1], F32)
                s54 = psb("s54", [128, 1], F32)
                n5 = psb("n5", [128, 1], F32)

                wstr_use = [Z, Z, Z, Z]
                wdnb_use = [Z, Z]
                b_done = (Z, Z, Z)
                x2t_use = Z
                h_use = Z
                ysb_use = Z
                ytst_use = Z
                ytile_use = [Z]
                x2r_use = Z
                out_use = Z

                for Q in range(4):
                    R.wt("tensor", b_done[0])
                    R.wt("tensor", b_done[1])
                    R.wt("tensor", b_done[2])
                    R.wt("sync", x2t_use)
                    for n_ in range(HC):
                        m_x2t = R.op("sync", lambda E, Q=Q, n_=n_: E.dma_start(
                            out=x2t[:, n_, :].rearrange("p (tc k) -> p tc k", tc=8),
                            in_=s_x2t[n_, Q * 8:(Q + 1) * 8, :, :]
                            .rearrange("tc p k -> p tc k")), tag="x2t")
                    with ExitStack() as pA:
                        ps_g = [pA.enter_context(nc.psum_tensor(
                            "p4g%d_%d" % (Q, i), [128, 1024], F32)) for i in range(2)]
                        ps_u = [pA.enter_context(nc.psum_tensor(
                            "p4u%d_%d" % (Q, i), [128, 1024], F32)) for i in range(2)]
                        psg_use = [Z, Z]
                        psu_use = [Z, Z]
                        m_h = Z
                        for p in range(22):
                            gb, ub = 2 * (p % 2), 2 * (p % 2) + 1
                            R.wt("sync", wstr_use[gb])
                            m_wg = R.op("sync", lambda E, p=p, gb=gb: E.dma_start(
                                out=wstr[gb][:, :, :],
                                in_=s_wgu[p, :, :, :].rearrange("d p k -> p d k")),
                            tag="ws%d" % gb)
                            R.wt("sync", wstr_use[ub])
                            m_wu = R.op("sync", lambda E, p=p, ub=ub: E.dma_start(
                                out=wstr[ub][:, :, :],
                                in_=s_wgu[22 + p, :, :, :].rearrange("d p k -> p d k")),
                                tag="ws%d" % ub)
                            for jm in range(2):
                                m = p * 2 + jm
                                pb = m % 2
                                R.wt("tensor", m_wg)
                                R.wt("tensor", m_wu)
                                R.wt("tensor", m_x2t)
                                R.wt("tensor", psg_use[pb])
                                R.wt("tensor", psu_use[pb])
                                for dc in range(HC):
                                    for h2 in range(2):
                                        m_g = R.op("tensor",
                                                   lambda E, dc=dc, h2=h2, jm=jm, pb=pb, gb=gb:
                                                   nc.tensor.matmul(
                                                       ps_g[pb][:, h2 * 512:(h2 + 1) * 512],
                                                       wstr[gb][:, dc, jm * 128:(jm + 1) * 128],
                                                       x2t[:, dc, h2 * 512:(h2 + 1) * 512],
                                                       start=(dc == 0), stop=(dc == HC - 1)),
                                                   tag=("ts" if dc == HC - 1 and h2 == 1
                                                        else None))
                                for dc in range(HC):
                                    for h2 in range(2):
                                        m_u = R.op("tensor",
                                                   lambda E, dc=dc, h2=h2, jm=jm, pb=pb, ub=ub:
                                                   nc.tensor.matmul(
                                                       ps_u[pb][:, h2 * 512:(h2 + 1) * 512],
                                                       wstr[ub][:, dc, jm * 128:(jm + 1) * 128],
                                                       x2t[:, dc, h2 * 512:(h2 + 1) * 512],
                                                       start=(dc == 0), stop=(dc == HC - 1)),
                                                   tag=("ts" if dc == HC - 1 and h2 == 1
                                                        else None))
                                wstr_use[gb] = ("ts", R.ch["ts"].n)
                                wstr_use[ub] = ("ts", R.ch["ts"].n)
                                x2t_use = ("ts", R.ch["ts"].n)
                                R.wt("scalar", m_g)
                                R.wt("scalar", psg_use[pb])
                                R.wt("scalar", m_h)
                                m_si = R.op("scalar", lambda E, pb=pb: E.activation(
                                    silu_t[:, :], ps_g[pb][:, :], AF.Silu), tag="cs")
                                psg_use[pb] = m_si
                                R.wt("vector", m_si)
                                R.wt("vector", m_u)
                                R.wt("vector", h_use)
                                m_h = R.op("vector", lambda E, m=m, pb=pb: E.tensor_mul(
                                    harena[:, m, :], silu_t[:, :], ps_u[pb][:, :]), tag="vs")
                                psu_use[pb] = m_h
                        h_rdy = m_h
                    with ExitStack() as pB:
                        ps_y = [pB.enter_context(nc.psum_tensor(
                            "p4y%d_%d" % (Q, i), [128, 1024], F32)) for i in range(2)]
                        ps_yt = [pB.enter_context(nc.psum_tensor(
                            "p4t%d_%d" % (Q, i), [128, 512], F32)) for i in range(2)]
                        psy_use = [Z, Z]
                        yt_ev = Z
                        m_yst = Z
                        for hc in range(HC):
                            wb = hc % 2
                            pb = hc % 2
                            R.wt("sync", wdnb_use[wb])
                            m_wd = R.op("sync", lambda E, hc=hc, wb=wb: E.dma_start(
                                out=wdnb[wb][:, :].rearrange("p (mm k) -> p mm k", mm=IC),
                                in_=s_wdn[hc, :, :, :].rearrange("mm p k -> p mm k")),
                                tag="wd%d" % wb)
                            R.wt("tensor", m_wd)
                            R.wt("tensor", h_rdy)
                            R.wt("tensor", psy_use[pb])
                            wflat = wdnb[wb][:, :]
                            for m in range(IC):
                                for h2 in range(2):
                                    m_dn = R.op("tensor", lambda E, m=m, h2=h2, wflat=wflat,
                                                pb=pb: nc.tensor.matmul(
                                                    ps_y[pb][:, h2 * 512:(h2 + 1) * 512],
                                                    wflat[:, m * 128:(m + 1) * 128],
                                                    harena[:, m, h2 * 512:(h2 + 1) * 512],
                                                    start=(m == 0), stop=(m == IC - 1)),
                                                tag=("ts" if m == IC - 1 and h2 == 1 else None))
                            wdnb_use[wb] = ("ts", R.ch["ts"].n)
                            if hc == HC - 1:
                                h_use = ("ts", R.ch["ts"].n)
                            R.wt("vector", m_dn)
                            R.wt("vector", ysb_use)
                            m_ye = R.op("vector", lambda E, pb=pb: E.tensor_copy(
                                ysb[:, :], ps_y[pb][:, :]), tag="vs")
                            psy_use[pb] = m_ye
                            R.wt("tensor", m_ye)
                            for tg2 in range(2):
                                ptb = tg2 % 2
                                R.wt("tensor", yt_ev)
                                for k in range(4):
                                    tloc = tg2 * 4 + k
                                    m_yt = R.op("tensor", lambda E, tloc=tloc, k=k, ptb=ptb:
                                                nc.tensor.transpose(
                                                    ps_yt[ptb][:, k * 128:(k + 1) * 128],
                                                    ysb[:, tloc * 128:(tloc + 1) * 128],
                                                    ident_f[:, :]),
                                                tag=("ts" if k == 3 else None))
                                ysb_use = ("ts", R.ch["ts"].n)
                                R.wt("vector", m_yt)
                                R.wt("vector", ytst_use)
                                yt_ev = R.op("vector", lambda E, tg2=tg2, ptb=ptb: E.tensor_copy(
                                    ytst[:, tg2 * 512:(tg2 + 1) * 512], ps_yt[ptb][:, :]),
                                    tag="vs")
                                R.wt("gpsimd", yt_ev)
                                m_yst = R.op("gpsimd", lambda E, Q=Q, hc=hc, tg2=tg2:
                                             E.dma_start(
                                                 out=s_y[Q * 8 + tg2 * 4:Q * 8 + tg2 * 4 + 4,
                                                         hc, :, :].rearrange("t p k -> p t k"),
                                                 in_=ytst[:, tg2 * 512:(tg2 + 1) * 512]
                                                 .rearrange("p (t k) -> p t k", t=4)), tag="syst")
                                ytst_use = m_yst
                        b_done = (psy_use[0], psy_use[1], yt_ev)
                    for tl in range(8):
                        tsub = Q * 8 + tl
                        yb = 0
                        R.wt("gpsimd", ytile_use[yb])
                        R.wt("gpsimd", m_yst)
                        m_y = R.op("gpsimd", lambda E, tsub=tsub, yb=yb: E.dma_start(
                            out=ytile[yb][:, :].rearrange("p (c k) -> p c k", c=HC),
                            in_=s_y[tsub, :, :, :].rearrange("c p k -> p c k")), tag="yld")
                        R.wt("gpsimd", x2r_use)
                        R.wt("gpsimd", out_use)
                        m_x2r = R.op("gpsimd", lambda E, tsub=tsub: E.dma_start(
                            out=x2res[:, :], in_=s_x2[tsub * 128:(tsub + 1) * 128, :]),
                            tag="x2r")
                        R.wt("scalar", m_y)
                        R.wt("scalar", ("vs", R.ch["vs"].n))
                        R.wt("vector", ("cs", R.ch["cs"].n))
                        m_sq = R.op("scalar", lambda E, yb=yb: E.activation(
                            dump[:, :], ytile[yb][:, :], AF.Square, accum_out=s52[:, 0:1]),
                            tag="cs")
                        R.wt("vector", m_sq)
                        m1 = R.op("vector", lambda E: E.tensor_scalar(
                            s52[:, :], s52[:, :], 1.0 / H, EPSR, ALU.mult, ALU.add), tag="vs")
                        R.wt("scalar", m1)
                        m2 = R.op("scalar", lambda E: E.activation(
                            s52[:, :], s52[:, :], AF.Sqrt), tag="cs")
                        R.wt("vector", m2)
                        m3 = R.op("vector", lambda E: E.reciprocal(s52[:, :], s52[:, :]),
                                  tag="vs")
                        R.wt("vector", m3)
                        R.wt("vector", m_x2r)
                        m_r3 = R.op("vector", lambda E, yb=yb: nc.vector.scalar_tensor_tensor(
                            ytile[yb][:, :], ytile[yb][:, :], s52[:, 0:1], x2res[:, :],
                            ALU.mult, ALU.add), tag="vs")
                        x2r_use = m_r3
                        R.wt("vector", m_r3)
                        m_sum = R.op("vector", lambda E, yb=yb: E.reduce_sum(
                            s53[:, 0:1], ytile[yb][:, :], axis=AXX), tag="vs")
                        R.wt("scalar", m_r3)
                        m_ssq = R.op("scalar", lambda E, yb=yb: E.activation(
                            dump[:, :], ytile[yb][:, :], AF.Square, accum_out=s54[:, 0:1]),
                            tag="cs")
                        R.wt("vector", m_sum)
                        R.wt("vector", m_ssq)
                        m4 = R.op("vector", lambda E: E.tensor_scalar(
                            s53[:, :], s53[:, :], 1.0 / H, None, ALU.mult), tag="vs")
                        R.wt("vector", m4)
                        m5 = R.op("vector", lambda E: E.tensor_scalar(
                            s54[:, :], s54[:, :], 1.0 / H, None, ALU.mult), tag="vs")
                        R.wt("vector", m5)
                        m6 = R.op("vector", lambda E: E.tensor_mul(
                            n5[:, :], s53[:, :], s53[:, :]), tag="vs")
                        R.wt("vector", m6)
                        m7 = R.op("vector", lambda E: nc.vector.scalar_tensor_tensor(
                            n5[:, :], n5[:, :], -1.0, s54[:, :], ALU.mult, ALU.add), tag="vs")
                        R.wt("vector", m7)
                        m8 = R.op("vector", lambda E: E.tensor_scalar(
                            n5[:, :], n5[:, :], EPS2, None, ALU.add), tag="vs")
                        R.wt("scalar", m8)
                        m9 = R.op("scalar", lambda E: E.activation(
                            n5[:, :], n5[:, :], AF.Sqrt), tag="cs")
                        R.wt("vector", m9)
                        m10 = R.op("vector", lambda E: E.reciprocal(n5[:, :], n5[:, :]),
                                   tag="vs")
                        R.wt("vector", m10)
                        m11 = R.op("vector", lambda E: nc.vector.scalar_tensor_tensor(
                            s53[:, :], s53[:, :], -1.0, n5[:, :], ALU.mult, ALU.mult), tag="vs")
                        R.wt("scalar", m11)
                        R.wt("scalar", out_use)
                        m_out = R.op("scalar", lambda E, yb=yb: E.activation(
                            x2res[:, :], ytile[yb][:, :], AF.Identity, bias=s53[:, 0:1],
                            scale=n5[:, 0:1]), tag="cs")
                        ytile_use[yb] = m_out
                        R.wt("gpsimd", m_out)
                        out_use = R.op("gpsimd", lambda E, tsub=tsub: E.dma_start(
                            out=out[tsub * 128:(tsub + 1) * 128, :], in_=x2res[:, :]),
                            tag="oust")

        for name, c in list(R.ch.items()):
            if c.step == 16:
                R.wt("gpsimd", (name, c.n))

        with nc.Block() as block:
            for e in ENGINES:
                getattr(block, e)(lambda E, e=e: [f(E) for f in R.q[e]])
    return nc



def kernel(L0_attn_x, L0_attn_qkv, L0_attn_o_proj, L0_attn_gamma, L0_attn_beta,
           L0_ffn_gate_up_packed, L0_ffn_gate_up_scales, L0_ffn_down_packed,
           L0_ffn_down_scales):
    x = np.ascontiguousarray(np.asarray(L0_attn_x, dtype=np.float32))
    B = x.shape[0]
    wqkv = np.ascontiguousarray(np.asarray(L0_attn_qkv, dtype=np.float32))
    wo_ = np.ascontiguousarray(np.asarray(L0_attn_o_proj, dtype=np.float32))
    gamma = np.asarray(L0_attn_gamma, dtype=np.float32)
    beta = np.asarray(L0_attn_beta, dtype=np.float32)
    gup_ = np.ascontiguousarray(np.asarray(L0_ffn_gate_up_packed, dtype=np.int32))
    gus_ = np.ascontiguousarray(np.asarray(L0_ffn_gate_up_scales, dtype=np.float16))
    dnp_ = np.ascontiguousarray(np.asarray(L0_ffn_down_packed, dtype=np.int32))
    dns_ = np.ascontiguousarray(np.asarray(L0_ffn_down_scales, dtype=np.float16))

    gam_pp = np.ascontiguousarray(gamma.reshape(HC, 128).T)
    bet_pp = np.ascontiguousarray(beta.reshape(HC, 128).T)

    PERM = {0: [0, 3, 1, 2], 1: [1, 2, 0, 3]}
    in_maps = []
    for b in range(B):
        for t in range(2):
            xl = np.ascontiguousarray(x[b].reshape(4, 2048, H)[PERM[t]].reshape(S, H))
            onoff = np.zeros((128, 2), np.float32)
            onoff[:, 0] = NEG if t == 0 else 0.0
            onoff[:, 1] = 0.0 if t == 0 else NEG
            in_maps.append(dict(x=xl, wqkv=wqkv, wo=wo_, gam=gam_pp, bet=bet_pp,
                                onoff=onoff, gup=gup_, gus=gus_, dnp=dnp_, dns=dns_))

    nc = build()
    res = run_bass_kernel_spmd(nc, in_maps, core_ids=list(range(8)))
    outp = np.empty((B, 4, 2048, H), np.float32)
    for b in range(B):
        oa = res.results[2 * b]["out"]
        ob = res.results[2 * b + 1]["out"]
        outp[b, 0] = oa[0:2048]
        outp[b, 3] = oa[2048:4096]
        outp[b, 1] = ob[0:2048]
        outp[b, 2] = ob[2048:4096]
    return outp.reshape(B, S, H)


# version 10
# speedup vs baseline: 1.2187x; 1.2187x over previous
import numpy as np
from contextlib import ExitStack

import concourse.bass as bass
import concourse.mybir as mybir
from concourse.bass_utils import run_bass_kernel_spmd

F32 = mybir.dt.float32
BF16 = mybir.dt.bfloat16
F16 = mybir.dt.float16
U16 = mybir.dt.uint16
I32 = mybir.dt.int32
AF = mybir.ActivationFunctionType
ALU = mybir.AluOpType
AXX = mybir.AxisListType.X

H = 2048
HC = 16
S = 8192
NQ = 4096
SCALE = 1.0 / float(np.sqrt(2048.0))
NEG = -1.0e30
EPS1 = 1e-5
EPS2 = 1e-5
EPSR = 1e-6
I2 = 11264
IC = 44
ENGINES = ("sync", "gpsimd", "vector", "scalar", "tensor")
Z = ("vs", 0)


class Ch:
    def __init__(self, ctx, nc, name, step):
        self.sem = ctx.enter_context(nc.semaphore(name))
        self.name = name
        self.n = 0
        self.step = step


class Rec:
    COMPUTE = ("gp", "vs", "cs", "ts")

    def __init__(self, ctx, nc):
        self.nc = nc
        self.ctx = ctx
        self.q = {e: [] for e in ENGINES}
        self.ch = {n: Ch(ctx, nc, n, 1) for n in self.COMPUTE}

    def chan(self, tag):
        if tag not in self.ch:
            self.ch[tag] = Ch(self.ctx, self.nc, tag, 16)
        return self.ch[tag]

    AUTOCH = {"vector": "vs", "scalar": "cs"}

    def op(self, eng, fn, tag=None, auto=True):
        if eng in ("vector", "scalar") and auto:
            ch = self.AUTOCH[eng]
            assert tag in (None, ch), (eng, tag)
            c = self.ch[ch]
            prev = c.n
            c.n += c.step
            mark = (ch, c.n)
            sem, step = c.sem, c.step

            def emit(E, fn=fn, sem=sem, step=step, prev=prev):
                if prev > 0:
                    E.wait_ge(sem, prev)
                fn(E).then_inc(sem, step)
            self.q[eng].append(emit)
            return mark
        if tag is None:
            self.q[eng].append(fn)
            return None
        c = self.chan(tag)
        c.n += c.step
        mark = (tag, c.n)
        sem, step = c.sem, c.step

        def emit(E, fn=fn, sem=sem, step=step):
            fn(E).then_inc(sem, step)
        self.q[eng].append(emit)
        return mark

    def wt(self, eng, mark):
        if mark is None:
            return
        tag, v = mark
        if v <= 0:
            return
        sem = self.chan(tag).sem
        self.q[eng].append(lambda E, sem=sem, v=v: E.wait_ge(sem, v))

    def marks(self):
        return [(n, c.n) for n, c in self.ch.items()]

    def barrier(self, mks):
        for e in ENGINES:
            for m in mks:
                self.wt(e, m)


def build(dbg=False, upto=3, pipe=True):
    nc = bass.Bass("TRN2", target_bir_lowering=False)
    x_in = nc.dram_tensor("x", [S, H], F32, kind="ExternalInput").ap()
    wqkv = nc.dram_tensor("wqkv", [H, 3 * H], F32, kind="ExternalInput").ap()
    wo = nc.dram_tensor("wo", [H, H], F32, kind="ExternalInput").ap()
    gam = nc.dram_tensor("gam", [128, HC], F32, kind="ExternalInput").ap()
    bet = nc.dram_tensor("bet", [128, HC], F32, kind="ExternalInput").ap()
    onoff_in = nc.dram_tensor("onoff", [128, 2], F32, kind="ExternalInput").ap()
    gup = nc.dram_tensor("gup", [I2, 256], I32, kind="ExternalInput").ap()
    gus = nc.dram_tensor("gus", [16, I2], F16, kind="ExternalInput").ap()
    dnp = nc.dram_tensor("dnp", [H, 704], I32, kind="ExternalInput").ap()
    dns = nc.dram_tensor("dns", [IC, H], F16, kind="ExternalInput").ap()
    out = nc.dram_tensor("out", [NQ, H], F32, kind="ExternalOutput").ap()

    kind = "ExternalOutput" if dbg else "Internal"
    s_wq = nc.dram_tensor("s_wq", [32, HC, 128, 128], BF16, kind=kind).ap()
    s_wv = nc.dram_tensor("s_wv", [HC, 128, H], BF16, kind=kind).ap()
    s_wo = nc.dram_tensor("s_wo", [HC, 128, H], BF16, kind=kind).ap()
    s_qT = nc.dram_tensor("s_qT", [8, HC, 128, 512], BF16, kind=kind).ap()
    s_kT = nc.dram_tensor("s_kT", [4, 4, HC, 128, 512], BF16, kind=kind).ap()
    s_v = nc.dram_tensor("s_v", [4, 2, 8, 128, H], BF16, kind=kind).ap()
    s_x2 = nc.dram_tensor("s_x2", [NQ, H], F32, kind=kind).ap()
    s_x2t = nc.dram_tensor("s_x2t", [HC, 32, 128, 128], BF16, kind=kind).ap()
    s_wgu = nc.dram_tensor("s_wgu", [44, HC, 128, 256], BF16, kind=kind).ap()
    s_wdn = nc.dram_tensor("s_wdn", [HC, IC, 128, 128], BF16, kind=kind).ap()
    s_y = nc.dram_tensor("s_y", [32, HC, 128, 128], F32, kind=kind).ap()

    with ExitStack() as ctx:
        sb = lambda *a: ctx.enter_context(nc.sbuf_tensor(*a))
        R = Rec(ctx, nc)

        ident_b = sb("ident_b", [128, 128], BF16)
        ident_f = sb("ident_f", [128, 128], F32)
        gam_sb = sb("gam_sb", [128, HC], F32)
        bet_sb = sb("bet_sb", [128, HC], F32)
        bet_bf = sb("bet_bf", [128, HC], BF16)
        onoff = sb("onoff_sb", [128, 2], F32)
        bw_pp = sb("bw_pp", [128, 32], F32)
        dump = sb("dump", [128, H], BF16)

        m0 = R.op("gpsimd", lambda E: E.memset(ident_b[:, :], 0.0), tag="gp")
        R.wt("gpsimd", m0)
        m0 = R.op("gpsimd", lambda E: E.affine_select(
            out=ident_b[:, :], in_=ident_b[:, :], compare_op=ALU.not_equal,
            fill=1.0, base=0, pattern=[[-1, 128]], channel_multiplier=1), tag="gp")
        R.wt("gpsimd", m0)
        m0 = R.op("gpsimd", lambda E: E.memset(ident_f[:, :], 0.0), tag="gp")
        R.wt("gpsimd", m0)
        m_const = R.op("gpsimd", lambda E: E.affine_select(
            out=ident_f[:, :], in_=ident_f[:, :], compare_op=ALU.not_equal,
            fill=1.0, base=0, pattern=[[-1, 128]], channel_multiplier=1), tag="gp")

        R.op("sync", lambda E: E.dma_start(out=gam_sb[:, :], in_=gam), tag="cst")
        R.op("sync", lambda E: E.dma_start(out=onoff[:, :], in_=onoff_in), tag="cst")
        m_gb = R.op("sync", lambda E: E.dma_start(out=bet_sb[:, :], in_=bet), tag="cst")
        R.wt("vector", m_gb)
        m_bb = R.op("vector", lambda E: E.tensor_copy(bet_bf[:, :], bet_sb[:, :]), tag="vs")

        p01 = ExitStack()
        bwvrow = p01.enter_context(nc.sbuf_tensor("bwvrow", [1, H], BF16))
        ones_row = p01.enter_context(nc.sbuf_tensor("ones_row", [1, 128], BF16))
        with ExitStack() as p0:
            wtile2 = [p0.enter_context(nc.sbuf_tensor("wtile%d" % i, [128, 3 * H], BF16))
                      for i in range(2)]
            wotile = p0.enter_context(nc.sbuf_tensor("wotile", [128, H], BF16))
            ps_bp = p0.enter_context(nc.psum_tensor("ps_bp", [128, 32], F32))
            ps_bv = p0.enter_context(nc.psum_tensor("ps_bv", [1, H], F32))

            m_wst = [Z, Z]
            m_mm = Z
            m_bv = Z
            for dc in range(HC):
                wbi = dc % 2
                wtile = wtile2[wbi]
                R.wt("gpsimd", m_wst[wbi])
                m_wl = R.op("gpsimd", lambda E, dc=dc, wtile=wtile: E.dma_start(
                    out=wtile[:, :], in_=wqkv[dc * 128:(dc + 1) * 128, :]),
                    tag="p0w%d" % wbi)
                R.wt("tensor", m_wl)
                R.wt("tensor", m_bb)
                for oc in range(32):
                    m_mm = R.op("tensor", lambda E, dc=dc, oc=oc, wtile=wtile:
                                nc.tensor.matmul(
                                    ps_bp[:, oc:oc + 1],
                                    wtile[:, oc * 128:(oc + 1) * 128],
                                    bet_bf[:, dc:dc + 1],
                                    start=(dc == 0 and oc == 0),
                                    stop=(dc == HC - 1 and oc == 31)),
                                tag=("ts" if oc == 31 else None))
                for sl in range(4):
                    m_bv = R.op("tensor", lambda E, dc=dc, sl=sl, wtile=wtile:
                                nc.tensor.matmul(
                                    ps_bv[:, sl * 512:(sl + 1) * 512],
                                    bet_bf[:, dc:dc + 1],
                                    wtile[:, 2 * H + sl * 512:2 * H + (sl + 1) * 512],
                                    start=(dc == 0), stop=(dc == HC - 1)),
                                tag=("ts" if sl == 3 else None))
                R.wt("vector", m_mm)
                R.wt("vector", m_bv)
                m_f1 = R.op("vector", lambda E, dc=dc, wtile=wtile: E.tensor_scalar(
                    wtile[:, :], wtile[:, :], gam_sb[:, dc:dc + 1], None, ALU.mult),
                    tag="vs")
                R.wt("gpsimd", m_f1)
                R.op("gpsimd", lambda E, dc=dc, wtile=wtile: E.dma_start(
                    out=s_wq[:, dc, :, :].rearrange("oc p k -> p oc k"),
                    in_=wtile[:, 0:2 * H].rearrange("p (oc k) -> p oc k", oc=32)),
                    tag="p0st%d" % wbi)
                m_wst[wbi] = R.op("gpsimd", lambda E, dc=dc, wtile=wtile: E.dma_start(
                    out=s_wv[dc, :, :], in_=wtile[:, 2 * H:3 * H]), tag="p0st%d" % wbi)

            R.wt("vector", m_mm)
            m_bwpp = R.op("vector", lambda E: E.tensor_copy(bw_pp[:, :], ps_bp[:, :]),
                          tag="vs")
            R.wt("scalar", m_bv)
            m_bvrow = R.op("scalar", lambda E: E.copy(bwvrow[:, :], ps_bv[:, :]), tag="cs")
            R.op("vector", lambda E: E.memset(ones_row[:, :], 1.0), tag="vs")

            m_wost = Z
            for vc in range(HC):
                R.wt("gpsimd", m_wost)
                m_wol = R.op("gpsimd", lambda E, vc=vc: E.dma_start(
                    out=wotile[:, :], in_=wo[vc * 128:(vc + 1) * 128, :]), tag="p0wo")
                R.wt("gpsimd", m_wol)
                m_wost = R.op("gpsimd", lambda E, vc=vc: E.dma_start(
                    out=s_wo[vc, :, :], in_=wotile[:, :]), tag="p0wo")

        R.barrier(R.marks())

        dq_holder = {}
        dqs = {"u": 0, "pk_use": [Z] * 6, "scl_use": [Z] * 3, "ob_use": [Z, Z]}

        def dq_unit(kindu, t, slc):
            dq_pk, dq_scl = dq_holder["pk"], dq_holder["scl"]
            dq_nib, dq_mag = dq_holder["nib"], dq_holder["mag"]
            dq_bits, dq_sb, dq_pred = dq_holder["bits"], dq_holder["sb"], dq_holder["pred"]
            dq_obf = dq_holder["obf"]
            u = dqs["u"]
            dqs["u"] += 1
            bi = u % 6
            si = u % 3
            pk = dq_pk[bi]
            scl = dq_scl[si]
            W = 1024
            R.wt("sync", dqs["pk_use"][bi])
            if kindu == "gu":
                src = gup.bitcast(U16)[slc * 1024:(slc + 1) * 1024, t * 128:(t + 1) * 128]
            else:
                src = dnp.bitcast(U16)[slc * 1024:(slc + 1) * 1024, t * 128:(t + 1) * 128]
            m_pk = R.op("sync", lambda E, src=src, pk=pk, W=W:
                        E.dma_start_transpose(pk[:, 0:W], src), tag="pk%d" % bi)
            R.wt("gpsimd", dqs["scl_use"][si])
            if kindu == "gu":
                ssrc = gus[4 * t:4 * t + 4, slc * 1024:(slc + 1) * 1024]
            else:
                ssrc = dns[4 * t:4 * t + 4, slc * 1024:(slc + 1) * 1024]
            m_scl = R.op("gpsimd", lambda E, ssrc=ssrc, scl=scl, W=W:
                         E.dma_start(out=scl[:, 0:W],
                                     in_=ssrc.unsqueeze(1).broadcast_to([4, 32, W])),
                         tag="scl%d" % si)
            R.wt("vector", m_pk)
            R.wt("vector", m_scl)
            for j in range(4):
                ob = dq_obf[j % 2]
                m1 = R.op("vector", lambda E, j=j, pk=pk, W=W: E.tensor_scalar(
                    dq_nib[:, 0:W], pk[:, 0:W], 4 * j, 15,
                    ALU.logical_shift_right, ALU.bitwise_and), tag="vs")
                R.op("vector", lambda E, W=W: E.tensor_scalar(
                    dq_mag[:, 0:W], dq_nib[:, 0:W], 7, None, ALU.bitwise_and))
                R.op("vector", lambda E, W=W: E.tensor_scalar(
                    dq_sb[:, 0:W], dq_nib[:, 0:W], 8, 12,
                    ALU.bitwise_and, ALU.logical_shift_left))
                R.op("vector", lambda E, W=W: E.tensor_scalar(
                    dq_bits[:, 0:W], dq_mag[:, 0:W], 64, 16128, ALU.mult, ALU.add))
                R.op("vector", lambda E, W=W: E.tensor_scalar(
                    dq_pred[:, 0:W], dq_mag[:, 0:W], 1, None, ALU.is_le))
                R.op("vector", lambda E, W=W: E.tensor_scalar(
                    dq_mag[:, 0:W], dq_mag[:, 0:W], 16128, None, ALU.mult))
                R.op("vector", lambda E, W=W: E.copy_predicated(
                    dq_bits[:, 0:W], dq_pred[:, 0:W], dq_mag[:, 0:W]))
                R.op("vector", lambda E, W=W: E.tensor_tensor(
                    dq_nib[:, 0:W], dq_bits[:, 0:W], dq_sb[:, 0:W], ALU.bitwise_or))
                R.wt("vector", dqs["ob_use"][j % 2])
                if kindu == "gu":
                    m7 = R.op("vector", lambda E, W=W, scl=scl, ob=ob: E.tensor_tensor(
                        ob[:, 0:W]
                        .rearrange("p (qh j q7) -> p qh j q7", qh=2, j=4),
                        dq_nib.bitcast(BF16)[:, 0:W]
                        .rearrange("p (qh q7 j) -> p qh j q7", qh=2, q7=128),
                        scl[:, 0:W]
                        .rearrange("p (qh q7 j) -> p qh j q7", qh=2, q7=128),
                        ALU.mult), tag="vs")
                else:
                    m7 = R.op("vector", lambda E, W=W, scl=scl, ob=ob: E.tensor_tensor(
                        ob[:, 0:W], dq_nib.bitcast(BF16)[:, 0:W],
                        scl[:, 0:W], ALU.mult), tag="vs")
                R.wt("gpsimd", m7)
                if kindu == "gu":
                    n = t * 4 + j
                    m_st = R.op("gpsimd", lambda E, n=n, slc=slc, ob=ob: E.dma_start(
                        out=s_wgu[4 * slc:4 * slc + 4, n, :, :]
                        .rearrange("pp p k -> p pp k"),
                        in_=ob[:, 0:1024]
                        .rearrange("p (pp k) -> p pp k", pp=4)), tag="dqst%d" % (j % 2))
                else:
                    mm = t * 4 + j
                    m_st = R.op("gpsimd", lambda E, mm=mm, slc=slc, ob=ob: E.dma_start(
                        out=s_wdn[8 * slc:8 * slc + 8, mm, :, :]
                        .rearrange("hc p k -> p hc k"),
                        in_=ob[:, 0:1024]
                        .rearrange("p (hc k) -> p hc k", hc=8)), tag="dqst%d" % (j % 2))
                dqs["ob_use"][j % 2] = m_st
            dqs["pk_use"][bi] = ("vs", R.ch["vs"].n)
            dqs["scl_use"][si] = ("vs", R.ch["vs"].n)
            return m_st

        dq_list = ([("gu", t, s_) for t in range(4) for s_ in range(11)] +
                   [("dn", t, s_) for t in range(11) for s_ in range(2)])
        dq_i = [0]
        dq_laststore = [Z]

        def emit_dq(nun):
            while nun > 0 and dq_i[0] < len(dq_list):
                dq_laststore[0] = dq_unit(*dq_list[dq_i[0]])
                dq_i[0] += 1
                nun -= 1

        with ExitStack() as p1:
            psb = lambda *a: p1.enter_context(nc.sbuf_tensor(*a))
            dq_holder["pk"] = [psb("dq_pk%d" % i, [128, 1024], U16) for i in range(6)]
            dq_holder["nib"] = psb("dq_nib", [128, 1024], U16)
            dq_holder["mag"] = psb("dq_mag", [128, 1024], U16)
            dq_holder["bits"] = psb("dq_bits", [128, 1024], U16)
            dq_holder["sb"] = psb("dq_sb", [128, 1024], U16)
            dq_holder["pred"] = psb("dq_pred", [128, 1024], U16)
            dq_holder["scl"] = [psb("dq_scl%d" % i, [128, 1024], BF16) for i in range(3)]
            dq_holder["obf"] = [psb("dq_ob%d" % i, [128, 1024], BF16) for i in range(2)]
            x_raw = [psb("x_raw0", [128, H], F32), psb("x_raw1", [128, H], F32)]
            xhat = [psb("xhat%d" % i, [128, H], BF16) for i in range(8)]
            xnT = psb("xnT", [128, HC, 1024], BF16)
            wqk = [psb("wqk0", [128, HC, 128], BF16), psb("wqk1", [128, HC, 128], BF16)]
            wv_sb = psb("wv_sb", [128, HC, H], BF16)
            qkstg = [psb("qkstg0", [128, 1024], BF16), psb("qkstg1", [128, 1024], BF16)]
            vstg = [psb("vstg0", [128, H], BF16), psb("vstg1", [128, H], BF16)]
            sums8 = psb("p1su", [128, 8], F32)
            ssq8 = psb("p1sq", [128, 8], F32)
            mu8 = psb("p1mu", [128, 8], F32)
            var8 = psb("p1va", [128, 8], F32)
            rstd8 = psb("p1rs", [128, 8], F32)
            nmr8 = psb("p1nm", [128, 8], F32)
            t18 = psb("p1t1", [128, 8], F32)

            xr_use = [Z, Z]
            xh_use = [Z] * 8
            wqk_use = [Z, Z]
            qkstg_use = [Z, Z]
            vstg_use = [Z, Z]
            xnt_use = Z
            v_done = (Z, Z)
            m_wv = Z
            ln_marks = {}

            def rec_ln(T):
                marks = []
                for s_g in range(8):
                    bi = s_g % 2
                    hb = s_g
                    R.wt("sync", xr_use[bi])
                    rows = T * 1024 + s_g * 128
                    m_x = R.op("sync", lambda E, bi=bi, rows=rows: E.dma_start(
                        out=x_raw[bi][:, :],
                        in_=x_in[rows:rows + 128, :]), tag="x%d" % bi)
                    xb = x_raw[bi]
                    sg = s_g
                    sums = sums8[:, sg:sg + 1]
                    ssq = ssq8[:, sg:sg + 1]
                    mu = mu8[:, sg:sg + 1]
                    var = var8[:, sg:sg + 1]
                    rstd = rstd8[:, sg:sg + 1]
                    nmr = nmr8[:, sg:sg + 1]
                    t1 = t18[:, sg:sg + 1]
                    R.wt("vector", m_x)
                    R.wt("scalar", m_x)
                    m_sum = R.op("vector", lambda E, xb=xb, sums=sums: E.reduce_sum(
                        sums, xb[:, :], axis=AXX), tag="vs")
                    m_ssq = R.op("scalar", lambda E, xb=xb, ssq=ssq: E.activation(
                        dump[:, :], xb[:, :], AF.Square,
                        accum_out=ssq), tag="cs")
                    R.wt("vector", m_ssq)
                    R.wt("vector", m_sum)
                    m1 = R.op("vector", lambda E, mu=mu, sums=sums: E.tensor_scalar(
                        mu, sums, 1.0 / H, None, ALU.mult), tag="vs")
                    m2 = R.op("vector", lambda E, t1=t1, ssq=ssq: E.tensor_scalar(
                        t1, ssq, 1.0 / H, None, ALU.mult), tag="vs")
                    m3 = R.op("vector", lambda E, var=var, mu=mu: E.tensor_mul(
                        var, mu, mu), tag="vs")
                    m4 = R.op("vector", lambda E, var=var, t1=t1:
                              nc.vector.scalar_tensor_tensor(
                                  var, var, -1.0, t1, ALU.mult, ALU.add), tag="vs")
                    m5 = R.op("vector", lambda E, var=var: E.tensor_scalar(
                        var, var, EPS1, None, ALU.add), tag="vs")
                    R.wt("scalar", m5)
                    m6 = R.op("scalar", lambda E, rstd=rstd, var=var: E.activation(
                        rstd, var, AF.Sqrt), tag="cs")
                    R.wt("vector", m6)
                    m7 = R.op("vector", lambda E, rstd=rstd: E.reciprocal(
                        rstd, rstd), tag="vs")
                    m8 = R.op("vector", lambda E, nmr=nmr, mu=mu, rstd=rstd:
                              nc.vector.scalar_tensor_tensor(
                                  nmr, mu, -1.0, rstd, ALU.mult, ALU.mult), tag="vs")
                    R.wt("scalar", m8)
                    R.wt("scalar", xh_use[hb])
                    m_xh = R.op("scalar", lambda E, xb=xb, hb=hb, nmr=nmr, rstd=rstd:
                                E.activation(xhat[hb][:, :], xb[:, :], AF.Identity,
                                             bias=nmr, scale=rstd),
                                tag="cs")
                    xr_use[bi] = m_xh
                    marks.append(m_xh)
                ln_marks[T] = marks

            def rec_tr(T):
                nonlocal xnt_use, v_done, m_wv, xnt_rdy
                with ExitStack() as pT:
                    ps_tr2 = [pT.enter_context(
                        nc.psum_tensor("p1tr%d_%d" % (T, i), [128, 1024], BF16))
                        for i in range(2)]
                    tr_evac = [Z, Z]
                    nbatch = 0
                    for s_g in range(8):
                        hb = s_g
                        R.wt("tensor", ln_marks[T][s_g])
                        R.wt("tensor", m_const)
                        R.wt("tensor", xnt_use)
                        R.wt("tensor", v_done[0])
                        R.wt("tensor", v_done[1])
                        for dcg in range(4):
                            tb = nbatch % 2
                            nbatch += 1
                            R.wt("tensor", tr_evac[tb])
                            for k in range(4):
                                dc = dcg * 4 + k
                                m_tr = R.op("tensor", lambda E, hb=hb, dc=dc, k=k, tb=tb:
                                            nc.tensor.transpose(
                                                ps_tr2[tb][:, k * 128:(k + 1) * 128],
                                                xhat[hb][:, dc * 128:(dc + 1) * 128],
                                                ident_b[:, :]),
                                            tag=("ts" if k == 3 else None))
                            R.wt("vector", m_tr)
                            tr_evac[tb] = R.op(
                                "vector", lambda E, dcg=dcg, s_g=s_g, tb=tb:
                                E.tensor_copy(
                                    xnT[:, 4 * dcg:4 * dcg + 4,
                                        s_g * 128:(s_g + 1) * 128],
                                    ps_tr2[tb][:, 0:512].rearrange(
                                        "p (d k) -> p d k", d=4)), tag="vs")
                        xh_use[hb] = ("ts", R.ch["ts"].n)
                    xnt_rdy = (tr_evac[0], tr_evac[1])
                    if T == 0:
                        m_wv = R.op("sync", lambda E: E.dma_start(
                            out=wv_sb[:, :, :],
                            in_=s_wv.rearrange("d p h -> p d h")), tag="wv")

            def rec_qkv(T):
                nonlocal xnt_use, v_done, m_wv, qk_done
                has_q = T < 4
                with ExitStack() as pQ:
                    ps_qk = [pQ.enter_context(nc.psum_tensor(
                        "p1qk%d_%d" % (T, i), [128, 1024], F32)) for i in range(2)]
                    psqk_use = [Z, Z]
                    oc_list = list(range(32)) if has_q else list(range(16, 32))
                    for idx, oc in enumerate(oc_list):
                        wb = idx % 2
                        pb = idx % 2
                        R.wt("sync", wqk_use[wb])
                        m_w = R.op("sync", lambda E, oc=oc, wb=wb: E.dma_start(
                            out=wqk[wb][:, :, :],
                            in_=s_wq[oc, :, :, :].rearrange("d p k -> p d k")),
                            tag="wqk%d" % wb)
                        R.wt("tensor", m_w)
                        R.wt("tensor", xnt_rdy[0])
                        R.wt("tensor", xnt_rdy[1])
                        R.wt("tensor", psqk_use[pb])
                        for dc in range(HC):
                            for h2 in range(2):
                                m_mm = R.op("tensor", lambda E, wb=wb, dc=dc, h2=h2, pb=pb:
                                            nc.tensor.matmul(
                                                ps_qk[pb][:, h2 * 512:(h2 + 1) * 512],
                                                wqk[wb][:, dc, :],
                                                xnT[:, dc, h2 * 512:(h2 + 1) * 512],
                                                start=(dc == 0), stop=(dc == HC - 1)),
                                            tag=("ts" if dc == HC - 1 and h2 == 1 else None))
                        wqk_use[wb] = ("ts", R.ch["ts"].n)
                        R.wt("scalar", m_mm)
                        R.wt("scalar", m_bwpp)
                        R.wt("scalar", qkstg_use[pb])
                        m_ev = R.op("scalar", lambda E, oc=oc, pb=pb: E.activation(
                            qkstg[pb][:, :], ps_qk[pb][:, :], AF.Identity,
                            bias=bw_pp[:, oc:oc + 1]), tag="cs")
                        psqk_use[pb] = m_ev
                        R.wt("gpsimd", m_ev)
                        if oc < 16:
                            m_st = R.op("gpsimd", lambda E, oc=oc, T=T, pb=pb: E.dma_start(
                                out=s_qT[2 * T:2 * T + 2, oc, :, :]
                                .rearrange("g p k -> p g k"),
                                in_=qkstg[pb][:, :].rearrange("p (g k) -> p g k", g=2)),
                                tag="qst%d" % pb)
                        else:
                            kb, hf = T >> 1, T & 1
                            m_st = R.op("gpsimd", lambda E, oc=oc, kb=kb, hf=hf, pb=pb:
                                        E.dma_start(
                                            out=s_kT[kb, 2 * hf:2 * hf + 2, oc - 16, :, :]
                                            .rearrange("qq p k -> p qq k"),
                                            in_=qkstg[pb][:, :].rearrange(
                                                "p (qq k) -> p qq k", qq=2)),
                                        tag="qst%d" % pb)
                        qkstg_use[pb] = m_st
                        if oc % 4 == 3:
                            emit_dq(1)
                    qk_done = (psqk_use[0], psqk_use[1])

                with ExitStack() as pV:
                    ps_v = [pV.enter_context(nc.psum_tensor(
                        "p1v%d_%d" % (T, i), [128, H], F32)) for i in range(2)]
                    psv_use = [Z, Z]
                    R.wt("tensor", m_wv)
                    R.wt("tensor", qk_done[0])
                    R.wt("tensor", qk_done[1])
                    for tc in range(8):
                        pb = tc % 2
                        R.wt("tensor", psv_use[pb])
                        for dc in range(HC):
                            for sl in range(4):
                                R.op("tensor", lambda E, dc=dc, sl=sl, tc=tc, pb=pb:
                                     nc.tensor.matmul(
                                         ps_v[pb][:, sl * 512:(sl + 1) * 512],
                                         xnT[:, dc, tc * 128:(tc + 1) * 128],
                                         wv_sb[:, dc, sl * 512:(sl + 1) * 512],
                                         start=(dc == 0), stop=False))
                        for sl in range(4):
                            m_mm = R.op("tensor", lambda E, sl=sl, pb=pb:
                                        nc.tensor.matmul(
                                            ps_v[pb][:, sl * 512:(sl + 1) * 512],
                                            ones_row[0:1, :],
                                            bwvrow[0:1, sl * 512:(sl + 1) * 512],
                                            start=False, stop=True),
                                        tag=("ts" if sl == 3 else None))
                        R.wt("scalar", m_mm)
                        R.wt("scalar", vstg_use[pb])
                        m_ev = R.op("scalar", lambda E, pb=pb: E.copy(
                            vstg[pb][:, :], ps_v[pb][:, :]), tag="cs")
                        psv_use[pb] = m_ev
                        R.wt("gpsimd", m_ev)
                        gtc = T * 8 + tc
                        kb, hf, tcl = gtc >> 4, (gtc >> 3) & 1, gtc & 7
                        m_st = R.op("gpsimd", lambda E, kb=kb, hf=hf, tcl=tcl, pb=pb:
                                    E.dma_start(out=s_v[kb, hf, tcl, :, :],
                                                in_=vstg[pb][:, :]), tag="vst%d" % pb)
                        vstg_use[pb] = m_st
                        if tc == 3:
                            emit_dq(1)
                    xnt_use = ("ts", R.ch["ts"].n)
                    v_done = (psv_use[0], psv_use[1])

            qk_done = (Z, Z)
            xnt_rdy = (Z, Z)
            if pipe:
                rec_ln(0)
                for T in range(8):
                    rec_tr(T)
                    if T < 7:
                        rec_ln(T + 1)
                    rec_qkv(T)
            else:
                for T in range(8):
                    rec_ln(T)
                    rec_tr(T)
                    rec_qkv(T)

        emit_dq(100)
        p01.close()
        R.barrier(R.marks())

        if upto >= 2:
            with ExitStack() as p2:
                psb = lambda *a: p2.enter_context(nc.sbuf_tensor(*a))
                mdiag = psb("mdiag", [128, 3968], BF16)
                qarena = psb("qarena", [128, HC, 512], BF16)
                kv = [psb("kv0", [128, 8192], BF16), psb("kv1", [128, 8192], BF16)]
                parena = psb("parena", [128, 4, 4096], BF16)
                ptar = psb("ptar", [128, 32, 512], BF16)
                osb = psb("osb", [128, 4, H], F32)
                msk = [psb("msk0", [128, 1024], F32), psb("msk1", [128, 1024], F32)]
                wost = [psb("wost0", [128, H], BF16), psb("wost1", [128, H], BF16)]
                xres = psb("xres", [128, H], F32)
                r2 = psb("r2", [128, H], F32)
                x2f = psb("x2f", [128, H], F32)
                x2b = psb("x2b", [128, H], BF16)
                x2stg = psb("x2stg", [128, 512], BF16)
                mrun = psb("mrun", [128, 4], F32)
                mprev = psb("mprev", [128, 4], F32)
                mbuf = psb("mbuf", [128, 4, 4], F32)
                lbuf = psb("lbuf", [128, 4, 4], F32)
                gbuf = psb("gbuf", [128, 4, 4], F32)
                gx = psb("gx", [128, 4], F32)
                lsb = psb("lsb", [128, 4], F32)
                lpass = psb("lpass", [128, 1], F32)
                negb = psb("negb", [128, 1], F32)
                negb2g = psb("negb2", [128, 4], F32)
                rl = psb("rl", [128, 1], F32)
                st2g = psb("st2", [128, 4], F32)
                bmax = psb("bmax", [128, 1], F32)
                st3g = psb("st3", [128, 4], F32)
                st4g = psb("st4", [128, 4], F32)

                q_use = Z
                ge_done = (Z, Z)
                x2ev_done = Z
                rl_use = Z
                x2f_use = Z
                x2tx_prev = Z
                osb_use = [Z] * 4
                kv_use = [Z, Z]
                msk_use = [Z, Z]
                wost_use = [Z, Z]
                xres_use = Z
                negb_use = Z
                x2stg_use = Z
                parena_use = [Z] * 4

                m0b = R.op("gpsimd", lambda E: E.memset(mdiag[:, :], 0.0), tag="gp")
                R.wt("gpsimd", m0b)
                m_mdiag = R.op("gpsimd", lambda E: E.affine_select(
                    out=mdiag[:, :], in_=mdiag[:, :], compare_op=ALU.is_ge,
                    fill=NEG, base=1920, pattern=[[-1, 3968]], channel_multiplier=1), tag="gp")
                R.wt("vector", m_mdiag)

                GRP_PASSES = [[[0, 2]]] * 4 + [[[0, 1], [2, 3]]] * 4

                def slot_type(g, kb):
                    if g < 4:
                        return "diag" if kb == 0 else "on0"
                    return {0: "full", 1: "diag", 2: "full", 3: "on1"}[kb]

                for g in range(8):
                    R.wt("sync", q_use)
                    m_q = R.op("sync", lambda E, g=g: E.dma_start(
                        out=qarena[:, :, :],
                        in_=s_qT[g, :, :, :].rearrange("d p k -> p d k")), tag="q2")
                    m_mi = R.op("vector", lambda E: E.memset(mrun[:, :], -3.0e38), tag="vs")
                    passes = GRP_PASSES[g]
                    pv_last = [Z] * 4
                    o_fold = [Z] * 4
                    for pidx, kbs in enumerate(passes):
                        if pidx == 0:
                            R.wt("tensor", ge_done[0])
                            R.wt("tensor", ge_done[1])
                        else:
                            for s4 in range(4):
                                R.wt("tensor", o_fold[s4])
                        with ExitStack() as pqk:
                            ps_qk = [pqk.enter_context(nc.psum_tensor(
                                "p2qk%d_%d_%d" % (g, pidx, i), [128, 1024], F32))
                                for i in range(2)]
                            ps_pt = pqk.enter_context(nc.psum_tensor(
                                "p2pt%d_%d" % (g, pidx), [128, 1024], BF16))
                            psqk_use = [Z, Z]
                            pt_evac = Z
                            for ki, kb in enumerate(kbs):
                                styp = slot_type(g, kb)
                                for hf in range(2):
                                    hidx = ki * 2 + hf
                                    m_k = [Z, Z]
                                    for q2 in range(2):
                                        kbuf = q2
                                        R.wt("sync", kv_use[kbuf])
                                        m_k[q2] = R.op("sync",
                                                       lambda E, kb=kb, hf=hf, q2=q2, kbuf=kbuf:
                                                       E.dma_start(
                                                           out=kv[kbuf].rearrange(
                                                               "p (d k) -> p d k", d=HC),
                                                           in_=s_kT[kb, 2 * hf + q2, :, :, :]
                                                           .rearrange("d p k -> p d k")),
                                                       tag="kv%d" % kbuf)
                                    for s in range(4):
                                        pb = s % 2
                                        R.wt("tensor", m_q)
                                        R.wt("tensor", psqk_use[pb])
                                        for q2 in range(2):
                                            kbuf = q2
                                            R.wt("tensor", m_k[q2])
                                            for dc in range(HC):
                                                m_mm = R.op(
                                                    "tensor",
                                                    lambda E, dc=dc, q2=q2, s=s, pb=pb, kbuf=kbuf:
                                                    nc.tensor.matmul(
                                                        ps_qk[pb][:, q2 * 512:(q2 + 1) * 512],
                                                        qarena[:, dc, s * 128:(s + 1) * 128],
                                                        kv[kbuf].rearrange(
                                                            "p (d k) -> p d k", d=HC)[:, dc, :],
                                                        start=(dc == 0), stop=(dc == HC - 1)),
                                                    tag=("ts" if dc == HC - 1 else None))
                                            kv_use[kbuf] = ("ts", R.ch["ts"].n)
                                        q_use = ("ts", R.ch["ts"].n)
                                        mb = s % 2
                                        R.wt("vector", m_mm)
                                        R.wt("vector", msk_use[mb])
                                        r_q = (g % 4) * 4 + s
                                        if styp == "diag":
                                            off = 1920 - 128 * r_q + hf * 1024
                                            m_msk = R.op("vector", lambda E, pb=pb, mb=mb, off=off:
                                                         E.tensor_add(
                                                             msk[mb][:, :], ps_qk[pb][:, :],
                                                             mdiag[:, off:off + 1024]), tag="vs")
                                        elif styp == "full":
                                            m_msk = R.op("vector", lambda E, pb=pb, mb=mb:
                                                         E.tensor_copy(
                                                             msk[mb][:, :], ps_qk[pb][:, :]),
                                                         tag="vs")
                                        else:
                                            col = 0 if styp == "on0" else 1
                                            m_msk = R.op("vector", lambda E, pb=pb, mb=mb, col=col:
                                                         E.tensor_scalar(
                                                             msk[mb][:, :], ps_qk[pb][:, :],
                                                             onoff[:, col:col + 1], None,
                                                             ALU.add), tag="vs")
                                        psqk_use[pb] = m_msk
                                        R.wt("vector", m_msk)
                                        if pidx == 0 and hidx == 0:
                                            R.wt("vector", m_mi)
                                        m_bm = R.op("vector", lambda E, mb=mb: E.reduce_max(
                                            bmax[:, 0:1], msk[mb][:, :], axis=AXX), tag="vs")
                                        R.wt("vector", m_bm)
                                        m_mx = R.op("vector", lambda E, s=s: E.tensor_max(
                                            mrun[:, s:s + 1], mrun[:, s:s + 1], bmax[:, 0:1]),
                                            tag="vs")
                                        R.wt("vector", m_mx)
                                        m_sv = R.op("vector", lambda E, s=s, hidx=hidx:
                                                    E.tensor_copy(mbuf[:, s, hidx:hidx + 1],
                                                                  mrun[:, s:s + 1]), tag="vs")
                                        R.wt("vector", m_sv)
                                        R.wt("vector", negb_use)
                                        m_nb = R.op("vector", lambda E, s=s: E.tensor_scalar(
                                            negb[:, :], mrun[:, s:s + 1], -SCALE, None,
                                            ALU.mult), tag="vs")
                                        R.wt("scalar", m_nb)
                                        if pidx == 0 and hidx == 0:
                                            R.wt("scalar", parena_use[s])
                                        m_exp = R.op("scalar", lambda E, mb=mb, s=s, hidx=hidx:
                                                     E.activation(
                                                         parena[:, s,
                                                                hidx * 1024:(hidx + 1) * 1024],
                                                         msk[mb][:, :], AF.Exp,
                                                         bias=negb[:, 0:1], scale=SCALE,
                                                         accum_out=lbuf[:, s, hidx:hidx + 1]),
                                                     tag="cs")
                                        msk_use[mb] = m_exp
                                        negb_use = m_exp
                            for s in range(4):
                                R.wt("vector", negb_use)
                                m_nb = R.op("vector", lambda E, s=s: E.tensor_scalar(
                                    negb[:, :], mrun[:, s:s + 1], -SCALE, None, ALU.mult),
                                    tag="vs")
                                R.wt("scalar", m_nb)
                                R.wt("scalar", msk_use[0])
                                R.wt("scalar", msk_use[1])
                                m_gb2 = R.op("scalar", lambda E, s=s: E.activation(
                                    gbuf[:, s, :], mbuf[:, s, :], AF.Exp, bias=negb[:, 0:1],
                                    scale=SCALE), tag="cs")
                                negb_use = m_gb2
                                if pidx > 0:
                                    m_gx = R.op("scalar", lambda E, s=s: E.activation(
                                        gx[:, s:s + 1], mprev[:, s:s + 1], AF.Exp,
                                        bias=negb[:, 0:1], scale=SCALE), tag="cs")
                                    negb_use = m_gx
                                R.wt("vector", m_gb2)
                                m_rs = Z
                                for hidx in range(4):
                                    m_rs = R.op("vector", lambda E, s=s, hidx=hidx:
                                                E.tensor_scalar(
                                                    parena[:, s, hidx * 1024:(hidx + 1) * 1024],
                                                    parena[:, s, hidx * 1024:(hidx + 1) * 1024],
                                                    gbuf[:, s, hidx:hidx + 1], None, ALU.mult),
                                                tag="vs")
                                R.wt("vector", m_rs)
                                m_lg = R.op("vector", lambda E, s=s: E.tensor_mul(
                                    gbuf[:, s, :], lbuf[:, s, :], gbuf[:, s, :]), tag="vs")
                                R.wt("vector", m_lg)
                                m_lp = R.op("vector", lambda E, s=s: E.reduce_sum(
                                    lpass[:, 0:1], gbuf[:, s, :],
                                    axis=AXX), tag="vs")
                                R.wt("vector", m_lp)
                                if pidx == 0:
                                    R.op("vector", lambda E, s=s: E.tensor_copy(
                                        lsb[:, s:s + 1], lpass[:, 0:1]), tag="vs")
                                else:
                                    R.wt("vector", m_gx)
                                    R.op("vector", lambda E, s=s:
                                         nc.vector.scalar_tensor_tensor(
                                             lsb[:, s:s + 1], lsb[:, s:s + 1], gx[:, s:s + 1],
                                             lpass[:, 0:1], ALU.mult, ALU.add), tag="vs")
                                R.op("vector", lambda E, s=s: E.tensor_copy(
                                    mprev[:, s:s + 1], mrun[:, s:s + 1]), tag="vs")
                                R.wt("tensor", m_rs)
                                for kcg in range(8):
                                    R.wt("tensor", pt_evac)
                                    for k in range(4):
                                        kc = kcg * 4 + k
                                        m_tp = R.op("tensor", lambda E, s=s, kc=kc, k=k:
                                                    nc.tensor.transpose(
                                                        ps_pt[:, k * 256:k * 256 + 128],
                                                        parena[:, s, kc * 128:(kc + 1) * 128],
                                                        ident_b[:, :]),
                                                    tag=("ts" if k == 3 else None))
                                    R.wt("vector", m_tp)
                                    pt_evac = R.op("vector", lambda E, s=s, kcg=kcg:
                                                   E.tensor_copy(
                                                       ptar[:, 4 * kcg:4 * kcg + 4,
                                                            s * 128:(s + 1) * 128],
                                                       ps_pt[:, :].rearrange(
                                                           "p (k q) -> p k q", k=4)[:, :, 0:128]),
                                                   tag="vs")
                        with ExitStack() as ppv:
                            ps_o = [ppv.enter_context(nc.psum_tensor(
                                "p2o%d_%d_%d" % (g, pidx, i), [128, H], F32)) for i in range(2)]
                            R.wt("tensor", x2ev_done)
                            pso_use = [o_fold[0], o_fold[1]]
                            for pair in ((0, 1), (2, 3)):
                                for kcg in range(8):
                                    vbuf = kcg % 2
                                    R.wt("sync", kv_use[vbuf])
                                    kb0 = kbs[kcg // 4]
                                    hf0 = (kcg % 4) // 2
                                    tc0 = (kcg % 2) * 4
                                    m_v = R.op("sync", lambda E, kb0=kb0, hf0=hf0, tc0=tc0,
                                               vbuf=vbuf: E.dma_start(
                                                   out=kv[vbuf].rearrange(
                                                       "p (c v) -> p c v", c=4),
                                                   in_=s_v[kb0, hf0, tc0:tc0 + 4, :, :]
                                                   .rearrange("c p v -> p c v")),
                                           tag="kv%d" % vbuf)
                                    for k in range(4):
                                        kc = kcg * 4 + k
                                        for s in pair:
                                            R.wt("tensor", m_v)
                                            R.wt("tensor", pt_evac)
                                            if kc == 0:
                                                R.wt("tensor", pso_use[s % 2])
                                            for sl in range(4):
                                                m_pv = R.op(
                                                    "tensor",
                                                    lambda E, kc=kc, s=s, sl=sl, k=k, vbuf=vbuf:
                                                    nc.tensor.matmul(
                                                        ps_o[s % 2][:, sl * 512:(sl + 1) * 512],
                                                        ptar[:, kc, s * 128:(s + 1) * 128],
                                                        kv[vbuf].rearrange(
                                                            "p (c v) -> p c v", c=4)
                                                        [:, k, sl * 512:(sl + 1) * 512],
                                                        start=(kc == 0), stop=(kc == 31)),
                                                    tag=("ts" if sl == 3 else None))
                                            pv_last[s] = m_pv
                                    kv_use[vbuf] = ("ts", R.ch["ts"].n)
                                for s in pair:
                                    R.wt("vector", pv_last[s])
                                    R.wt("vector", osb_use[s])
                                    if pidx == 0:
                                        m_of = R.op("vector", lambda E, s=s: E.tensor_copy(
                                            osb[:, s, :], ps_o[s % 2][:, :]), tag="vs")
                                    else:
                                        R.wt("vector", negb_use)
                                        m_of = R.op("vector", lambda E, s=s:
                                                    nc.vector.scalar_tensor_tensor(
                                                        osb[:, s, :], osb[:, s, :],
                                                        gx[:, s:s + 1], ps_o[s % 2][:, :],
                                                        ALU.mult, ALU.add), tag="vs")
                                    o_fold[s] = m_of
                                    pso_use[s % 2] = m_of

                    with ExitStack() as pop:
                        ps_at = pop.enter_context(nc.psum_tensor("p3at%d" % g, [128, H], F32))
                        ps_ot = pop.enter_context(nc.psum_tensor("p3ot%d" % g, [128, 1024], BF16))
                        ps_pad = pop.enter_context(nc.psum_tensor("p3pd%d" % g, [128, 512], F32))
                        ps_xt = pop.enter_context(nc.psum_tensor("p3xt%d" % g, [128, 1024], BF16))
                        ot_evac = Z
                        m_ob = [Z] * 4
                        for s in range(4):
                            R.wt("vector", o_fold[s])
                            R.wt("vector", rl_use)
                            m_rl = R.op("vector", lambda E, s=s: E.reciprocal(
                                rl[:, 0:1], lsb[:, s:s + 1]), tag="vs")
                            R.wt("scalar", m_rl)
                            R.wt("scalar", pt_evac)
                            m_ob[s] = R.op("scalar", lambda E, s=s: E.activation(
                                parena[:, s, 0:H], osb[:, s, :], AF.Copy,
                                scale=rl[:, 0:1]), tag="cs")
                            rl_use = m_ob[s]
                            osb_use[s] = m_ob[s]
                        for s in range(4):
                            R.wt("tensor", m_ob[s])
                            for vcg in range(4):
                                R.wt("tensor", ot_evac)
                                for k in range(4):
                                    vc = vcg * 4 + k
                                    m_to = R.op("tensor", lambda E, s=s, vc=vc, k=k:
                                                nc.tensor.transpose(
                                                    ps_ot[:, k * 256:k * 256 + 128],
                                                    parena[:, s, vc * 128:(vc + 1) * 128],
                                                    ident_b[:, :]),
                                                tag=("ts" if k == 3 else None))
                                R.wt("vector", m_to)
                                ot_evac = R.op("vector", lambda E, s=s, vcg=vcg: E.tensor_copy(
                                    ptar[:, 4 * vcg:4 * vcg + 4, s * 128:(s + 1) * 128],
                                    ps_ot[:, :].rearrange("p (k q) -> p k q", k=4)[:, :, 0:128]),
                                    tag="vs")
                            parena_use[s] = ("ts", R.ch["ts"].n)
                        m_r2 = Z
                        for s in range(4):
                            R.wt("sync", xres_use)
                            m_xr = R.op("sync", lambda E, g=g, s=s: E.dma_start(
                                out=xres[:, :],
                                in_=x_in[g * 512 + s * 128:g * 512 + (s + 1) * 128, :]),
                                tag="xr")
                            for vc in range(HC):
                                wb = vc % 2
                                R.wt("sync", wost_use[wb])
                                m_w = R.op("sync", lambda E, vc=vc, wb=wb: E.dma_start(
                                    out=wost[wb][:, :], in_=s_wo[vc, :, :]), tag="wo%d" % wb)
                                R.wt("tensor", m_w)
                                R.wt("tensor", ot_evac)
                                if vc == 0:
                                    R.wt("tensor", m_r2)
                                for sl in range(4):
                                    m_op = R.op("tensor", lambda E, vc=vc, s=s, sl=sl, wb=wb:
                                                nc.tensor.matmul(
                                                    ps_at[:, sl * 512:(sl + 1) * 512],
                                                    ptar[:, vc, s * 128:(s + 1) * 128],
                                                    wost[wb][:, sl * 512:(sl + 1) * 512],
                                                    start=(vc == 0), stop=(vc == HC - 1)),
                                                tag=("ts" if sl == 3 else None))
                                wost_use[wb] = ("ts", R.ch["ts"].n)
                            st2 = st2g[:, s:s + 1]
                            st3 = st3g[:, s:s + 1]
                            st4 = st4g[:, s:s + 1]
                            negb2 = negb2g[:, s:s + 1]
                            R.wt("vector", m_op)
                            R.wt("vector", m_xr)
                            m_r2 = R.op("vector", lambda E: nc.vector.scalar_tensor_tensor(
                                r2[:, :], xres[:, :], 2.0, ps_at[:, :],
                                ALU.mult, ALU.add), tag="vs")
                            xres_use = m_r2
                            R.wt("vector", m_r2)
                            m_sum = R.op("vector", lambda E, st2=st2: E.reduce_sum(
                                st2, r2[:, :], axis=AXX), tag="vs")
                            R.wt("scalar", m_r2)
                            m_ssq = R.op("scalar", lambda E, st3=st3: E.activation(
                                dump[:, :], r2[:, :], AF.Square, accum_out=st3),
                                tag="cs")
                            R.wt("vector", m_sum)
                            R.wt("vector", m_ssq)
                            m1 = R.op("vector", lambda E, st2=st2: E.tensor_scalar(
                                st2, st2, 1.0 / H, None, ALU.mult), tag="vs")
                            R.wt("vector", m1)
                            m2 = R.op("vector", lambda E, st3=st3: E.tensor_scalar(
                                st3, st3, 1.0 / H, None, ALU.mult), tag="vs")
                            R.wt("vector", m2)
                            m3 = R.op("vector", lambda E, st4=st4, st2=st2: E.tensor_mul(
                                st4, st2, st2), tag="vs")
                            R.wt("vector", m3)
                            m4 = R.op("vector", lambda E, st4=st4, st3=st3: nc.vector.scalar_tensor_tensor(
                                st4, st4, -1.0, st3, ALU.mult, ALU.add),
                                tag="vs")
                            R.wt("vector", m4)
                            m5 = R.op("vector", lambda E, st4=st4: E.tensor_scalar(
                                st4, st4, EPS2, None, ALU.add), tag="vs")
                            R.wt("scalar", m5)
                            m6 = R.op("scalar", lambda E, st4=st4: E.activation(
                                st4, st4, AF.Sqrt), tag="cs")
                            R.wt("vector", m6)
                            m7 = R.op("vector", lambda E, st4=st4: E.reciprocal(st4, st4),
                                      tag="vs")
                            R.wt("vector", m7)
                            m8 = R.op("vector", lambda E, negb2=negb2, st2=st2, st4=st4: nc.vector.scalar_tensor_tensor(
                                negb2, st2, -1.0, st4, ALU.mult, ALU.mult),
                                tag="vs")
                            R.wt("scalar", m8)
                            R.wt("scalar", x2f_use)
                            m_x2 = R.op("scalar", lambda E, negb2=negb2, st4=st4: E.activation(
                                x2f[:, :], r2[:, :], AF.Identity, bias=negb2,
                                scale=st4), tag="cs")
                            R.wt("gpsimd", m_x2)
                            m_sx2 = R.op("gpsimd", lambda E, g=g, s=s: E.dma_start(
                                out=s_x2[g * 512 + s * 128:g * 512 + (s + 1) * 128, :],
                                in_=x2f[:, :]), tag="x2st")
                            x2f_use = m_sx2
                            R.wt("vector", m_x2)
                            R.wt("vector", x2tx_prev)
                            m_sg = R.op("vector", lambda E: E.tensor_copy(
                                x2b[:, :].rearrange("p (t j q) -> p t j q", t=4, j=4),
                                x2f[:, :].rearrange("p (t q j) -> p t j q", t=4, j=4)), tag="vs")
                            R.wt("tensor", m_sg)
                            x2_evac = Z
                            for ng in range(4):
                                R.wt("tensor", x2_evac)
                                for k in range(4):
                                    n = ng * 4 + k
                                    m_tx = R.op("tensor", lambda E, n=n, k=k:
                                                nc.tensor.transpose(
                                                    ps_xt[:, k * 256:k * 256 + 128],
                                                    x2b[:, n * 128:(n + 1) * 128],
                                                    ident_b[:, :]),
                                                tag=("ts" if k == 3 else None))
                                R.wt("vector", m_tx)
                                R.wt("vector", x2stg_use)
                                x2_evac = R.op("vector", lambda E: E.tensor_copy(
                                    x2stg[:, :].rearrange("p (k q) -> p k q", k=4),
                                    ps_xt[:, :].rearrange("p (k q) -> p k q", k=4)[:, :, 0:128]),
                                    tag="vs")
                                R.wt("gpsimd", x2_evac)
                                tc = g * 4 + s
                                x2stg_use = R.op("gpsimd", lambda E, ng=ng, tc=tc: E.dma_start(
                                    out=s_x2t[4 * ng:4 * ng + 4, tc, :, :]
                                    .rearrange("n p k -> p n k"),
                                    in_=x2stg[:, :].rearrange("p (n k) -> p n k", n=4)),
                                    tag="x2tst")
                            x2tx_prev = ("ts", R.ch["ts"].n)
                            ge_done = (m_r2, ot_evac)
                            x2ev_done = x2_evac

            R.barrier(R.marks())

        if upto >= 3:
            with ExitStack() as p4:
                psb = lambda *a: p4.enter_context(nc.sbuf_tensor(*a))
                harena = psb("harena", [128, IC, 1024], BF16)
                x2t = psb("x2t", [128, HC, 1024], BF16)
                wstr = [psb("wstr%d" % i, [128, HC, 256], BF16) for i in range(4)]
                wdnb = [psb("wdnb%d" % i, [128, IC * 128], BF16) for i in range(2)]
                silu_t = psb("silu_t", [128, 1024], BF16)
                ysb = psb("ysb", [128, 1024], F32)
                ytst = psb("ytst", [128, 1024], F32)
                ytile = [psb("ytile0", [128, H], F32)]
                x2res = psb("x2res", [128, H], F32)
                s52 = psb("s52", [128, 1], F32)
                s53 = psb("s53", [128, 1], F32)
                s54 = psb("s54", [128, 1], F32)
                n5 = psb("n5", [128, 1], F32)

                wstr_use = [Z, Z, Z, Z]
                wdnb_use = [Z, Z]
                b_done = (Z, Z, Z)
                x2t_use = Z
                h_use = Z
                ysb_use = Z
                ytst_use = Z
                ytile_use = [Z]
                x2r_use = Z
                out_use = Z

                for Q in range(4):
                    R.wt("tensor", b_done[0])
                    R.wt("tensor", b_done[1])
                    R.wt("tensor", b_done[2])
                    R.wt("sync", x2t_use)
                    for n_ in range(HC):
                        m_x2t = R.op("sync", lambda E, Q=Q, n_=n_: E.dma_start(
                            out=x2t[:, n_, :].rearrange("p (tc k) -> p tc k", tc=8),
                            in_=s_x2t[n_, Q * 8:(Q + 1) * 8, :, :]
                            .rearrange("tc p k -> p tc k")), tag="x2t")
                    with ExitStack() as pA:
                        ps_g = [pA.enter_context(nc.psum_tensor(
                            "p4g%d_%d" % (Q, i), [128, 1024], F32)) for i in range(2)]
                        ps_u = [pA.enter_context(nc.psum_tensor(
                            "p4u%d_%d" % (Q, i), [128, 1024], F32)) for i in range(2)]
                        psg_use = [Z, Z]
                        psu_use = [Z, Z]
                        m_h = Z
                        for p in range(22):
                            gb, ub = 2 * (p % 2), 2 * (p % 2) + 1
                            R.wt("sync", wstr_use[gb])
                            m_wg = R.op("sync", lambda E, p=p, gb=gb: E.dma_start(
                                out=wstr[gb][:, :, :],
                                in_=s_wgu[p, :, :, :].rearrange("d p k -> p d k")),
                            tag="ws%d" % gb)
                            R.wt("sync", wstr_use[ub])
                            m_wu = R.op("sync", lambda E, p=p, ub=ub: E.dma_start(
                                out=wstr[ub][:, :, :],
                                in_=s_wgu[22 + p, :, :, :].rearrange("d p k -> p d k")),
                                tag="ws%d" % ub)
                            for jm in range(2):
                                m = p * 2 + jm
                                pb = m % 2
                                R.wt("tensor", m_wg)
                                R.wt("tensor", m_wu)
                                R.wt("tensor", m_x2t)
                                R.wt("tensor", psg_use[pb])
                                R.wt("tensor", psu_use[pb])
                                for dc in range(HC):
                                    for h2 in range(2):
                                        m_g = R.op("tensor",
                                                   lambda E, dc=dc, h2=h2, jm=jm, pb=pb, gb=gb:
                                                   nc.tensor.matmul(
                                                       ps_g[pb][:, h2 * 512:(h2 + 1) * 512],
                                                       wstr[gb][:, dc, jm * 128:(jm + 1) * 128],
                                                       x2t[:, dc, h2 * 512:(h2 + 1) * 512],
                                                       start=(dc == 0), stop=(dc == HC - 1)),
                                                   tag=("ts" if dc == HC - 1 and h2 == 1
                                                        else None))
                                for dc in range(HC):
                                    for h2 in range(2):
                                        m_u = R.op("tensor",
                                                   lambda E, dc=dc, h2=h2, jm=jm, pb=pb, ub=ub:
                                                   nc.tensor.matmul(
                                                       ps_u[pb][:, h2 * 512:(h2 + 1) * 512],
                                                       wstr[ub][:, dc, jm * 128:(jm + 1) * 128],
                                                       x2t[:, dc, h2 * 512:(h2 + 1) * 512],
                                                       start=(dc == 0), stop=(dc == HC - 1)),
                                                   tag=("ts" if dc == HC - 1 and h2 == 1
                                                        else None))
                                wstr_use[gb] = ("ts", R.ch["ts"].n)
                                wstr_use[ub] = ("ts", R.ch["ts"].n)
                                x2t_use = ("ts", R.ch["ts"].n)
                                R.wt("scalar", m_g)
                                R.wt("scalar", psg_use[pb])
                                R.wt("scalar", m_h)
                                m_si = R.op("scalar", lambda E, pb=pb: E.activation(
                                    silu_t[:, :], ps_g[pb][:, :], AF.Silu), tag="cs")
                                psg_use[pb] = m_si
                                R.wt("vector", m_si)
                                R.wt("vector", m_u)
                                R.wt("vector", h_use)
                                m_h = R.op("vector", lambda E, m=m, pb=pb: E.tensor_mul(
                                    harena[:, m, :], silu_t[:, :], ps_u[pb][:, :]), tag="vs")
                                psu_use[pb] = m_h
                        h_rdy = m_h
                    with ExitStack() as pB:
                        ps_y = [pB.enter_context(nc.psum_tensor(
                            "p4y%d_%d" % (Q, i), [128, 1024], F32)) for i in range(2)]
                        ps_yt = [pB.enter_context(nc.psum_tensor(
                            "p4t%d_%d" % (Q, i), [128, 512], F32)) for i in range(2)]
                        psy_use = [Z, Z]
                        yt_ev = Z
                        m_yst = Z
                        for hc in range(HC):
                            wb = hc % 2
                            pb = hc % 2
                            R.wt("sync", wdnb_use[wb])
                            m_wd = R.op("sync", lambda E, hc=hc, wb=wb: E.dma_start(
                                out=wdnb[wb][:, :].rearrange("p (mm k) -> p mm k", mm=IC),
                                in_=s_wdn[hc, :, :, :].rearrange("mm p k -> p mm k")),
                                tag="wd%d" % wb)
                            R.wt("tensor", m_wd)
                            R.wt("tensor", h_rdy)
                            R.wt("tensor", psy_use[pb])
                            wflat = wdnb[wb][:, :]
                            for m in range(IC):
                                for h2 in range(2):
                                    m_dn = R.op("tensor", lambda E, m=m, h2=h2, wflat=wflat,
                                                pb=pb: nc.tensor.matmul(
                                                    ps_y[pb][:, h2 * 512:(h2 + 1) * 512],
                                                    wflat[:, m * 128:(m + 1) * 128],
                                                    harena[:, m, h2 * 512:(h2 + 1) * 512],
                                                    start=(m == 0), stop=(m == IC - 1)),
                                                tag=("ts" if m == IC - 1 and h2 == 1 else None))
                            wdnb_use[wb] = ("ts", R.ch["ts"].n)
                            if hc == HC - 1:
                                h_use = ("ts", R.ch["ts"].n)
                            R.wt("vector", m_dn)
                            R.wt("vector", ysb_use)
                            m_ye = R.op("vector", lambda E, pb=pb: E.tensor_copy(
                                ysb[:, :], ps_y[pb][:, :]), tag="vs")
                            psy_use[pb] = m_ye
                            R.wt("tensor", m_ye)
                            for tg2 in range(2):
                                ptb = tg2 % 2
                                R.wt("tensor", yt_ev)
                                for k in range(4):
                                    tloc = tg2 * 4 + k
                                    m_yt = R.op("tensor", lambda E, tloc=tloc, k=k, ptb=ptb:
                                                nc.tensor.transpose(
                                                    ps_yt[ptb][:, k * 128:(k + 1) * 128],
                                                    ysb[:, tloc * 128:(tloc + 1) * 128],
                                                    ident_f[:, :]),
                                                tag=("ts" if k == 3 else None))
                                ysb_use = ("ts", R.ch["ts"].n)
                                R.wt("vector", m_yt)
                                R.wt("vector", ytst_use)
                                yt_ev = R.op("vector", lambda E, tg2=tg2, ptb=ptb: E.tensor_copy(
                                    ytst[:, tg2 * 512:(tg2 + 1) * 512], ps_yt[ptb][:, :]),
                                    tag="vs")
                                R.wt("gpsimd", yt_ev)
                                m_yst = R.op("gpsimd", lambda E, Q=Q, hc=hc, tg2=tg2:
                                             E.dma_start(
                                                 out=s_y[Q * 8 + tg2 * 4:Q * 8 + tg2 * 4 + 4,
                                                         hc, :, :].rearrange("t p k -> p t k"),
                                                 in_=ytst[:, tg2 * 512:(tg2 + 1) * 512]
                                                 .rearrange("p (t k) -> p t k", t=4)), tag="syst")
                                ytst_use = m_yst
                        b_done = (psy_use[0], psy_use[1], yt_ev)
                    for tl in range(8):
                        tsub = Q * 8 + tl
                        yb = 0
                        R.wt("gpsimd", ytile_use[yb])
                        R.wt("gpsimd", m_yst)
                        m_y = R.op("gpsimd", lambda E, tsub=tsub, yb=yb: E.dma_start(
                            out=ytile[yb][:, :].rearrange("p (c k) -> p c k", c=HC),
                            in_=s_y[tsub, :, :, :].rearrange("c p k -> p c k")), tag="yld")
                        R.wt("gpsimd", x2r_use)
                        R.wt("gpsimd", out_use)
                        m_x2r = R.op("gpsimd", lambda E, tsub=tsub: E.dma_start(
                            out=x2res[:, :], in_=s_x2[tsub * 128:(tsub + 1) * 128, :]),
                            tag="x2r")
                        R.wt("scalar", m_y)
                        R.wt("scalar", ("vs", R.ch["vs"].n))
                        R.wt("vector", ("cs", R.ch["cs"].n))
                        m_sq = R.op("scalar", lambda E, yb=yb: E.activation(
                            dump[:, :], ytile[yb][:, :], AF.Square, accum_out=s52[:, 0:1]),
                            tag="cs")
                        R.wt("vector", m_sq)
                        m1 = R.op("vector", lambda E: E.tensor_scalar(
                            s52[:, :], s52[:, :], 1.0 / H, EPSR, ALU.mult, ALU.add), tag="vs")
                        R.wt("scalar", m1)
                        m2 = R.op("scalar", lambda E: E.activation(
                            s52[:, :], s52[:, :], AF.Sqrt), tag="cs")
                        R.wt("vector", m2)
                        m3 = R.op("vector", lambda E: E.reciprocal(s52[:, :], s52[:, :]),
                                  tag="vs")
                        R.wt("vector", m3)
                        R.wt("vector", m_x2r)
                        m_r3 = R.op("vector", lambda E, yb=yb: nc.vector.scalar_tensor_tensor(
                            ytile[yb][:, :], ytile[yb][:, :], s52[:, 0:1], x2res[:, :],
                            ALU.mult, ALU.add), tag="vs")
                        x2r_use = m_r3
                        R.wt("vector", m_r3)
                        m_sum = R.op("vector", lambda E, yb=yb: E.reduce_sum(
                            s53[:, 0:1], ytile[yb][:, :], axis=AXX), tag="vs")
                        R.wt("scalar", m_r3)
                        m_ssq = R.op("scalar", lambda E, yb=yb: E.activation(
                            dump[:, :], ytile[yb][:, :], AF.Square, accum_out=s54[:, 0:1]),
                            tag="cs")
                        R.wt("vector", m_sum)
                        R.wt("vector", m_ssq)
                        m4 = R.op("vector", lambda E: E.tensor_scalar(
                            s53[:, :], s53[:, :], 1.0 / H, None, ALU.mult), tag="vs")
                        R.wt("vector", m4)
                        m5 = R.op("vector", lambda E: E.tensor_scalar(
                            s54[:, :], s54[:, :], 1.0 / H, None, ALU.mult), tag="vs")
                        R.wt("vector", m5)
                        m6 = R.op("vector", lambda E: E.tensor_mul(
                            n5[:, :], s53[:, :], s53[:, :]), tag="vs")
                        R.wt("vector", m6)
                        m7 = R.op("vector", lambda E: nc.vector.scalar_tensor_tensor(
                            n5[:, :], n5[:, :], -1.0, s54[:, :], ALU.mult, ALU.add), tag="vs")
                        R.wt("vector", m7)
                        m8 = R.op("vector", lambda E: E.tensor_scalar(
                            n5[:, :], n5[:, :], EPS2, None, ALU.add), tag="vs")
                        R.wt("scalar", m8)
                        m9 = R.op("scalar", lambda E: E.activation(
                            n5[:, :], n5[:, :], AF.Sqrt), tag="cs")
                        R.wt("vector", m9)
                        m10 = R.op("vector", lambda E: E.reciprocal(n5[:, :], n5[:, :]),
                                   tag="vs")
                        R.wt("vector", m10)
                        m11 = R.op("vector", lambda E: nc.vector.scalar_tensor_tensor(
                            s53[:, :], s53[:, :], -1.0, n5[:, :], ALU.mult, ALU.mult), tag="vs")
                        R.wt("scalar", m11)
                        R.wt("scalar", out_use)
                        m_out = R.op("scalar", lambda E, yb=yb: E.activation(
                            x2res[:, :], ytile[yb][:, :], AF.Identity, bias=s53[:, 0:1],
                            scale=n5[:, 0:1]), tag="cs")
                        ytile_use[yb] = m_out
                        R.wt("gpsimd", m_out)
                        out_use = R.op("gpsimd", lambda E, tsub=tsub: E.dma_start(
                            out=out[tsub * 128:(tsub + 1) * 128, :], in_=x2res[:, :]),
                            tag="oust")

        for name, c in list(R.ch.items()):
            if c.step == 16:
                R.wt("gpsimd", (name, c.n))

        with nc.Block() as block:
            for e in ENGINES:
                getattr(block, e)(lambda E, e=e: [f(E) for f in R.q[e]])
    return nc



def kernel(L0_attn_x, L0_attn_qkv, L0_attn_o_proj, L0_attn_gamma, L0_attn_beta,
           L0_ffn_gate_up_packed, L0_ffn_gate_up_scales, L0_ffn_down_packed,
           L0_ffn_down_scales):
    x = np.ascontiguousarray(np.asarray(L0_attn_x, dtype=np.float32))
    B = x.shape[0]
    wqkv = np.ascontiguousarray(np.asarray(L0_attn_qkv, dtype=np.float32))
    wo_ = np.ascontiguousarray(np.asarray(L0_attn_o_proj, dtype=np.float32))
    gamma = np.asarray(L0_attn_gamma, dtype=np.float32)
    beta = np.asarray(L0_attn_beta, dtype=np.float32)
    gup_ = np.ascontiguousarray(np.asarray(L0_ffn_gate_up_packed, dtype=np.int32))
    gus_ = np.ascontiguousarray(np.asarray(L0_ffn_gate_up_scales, dtype=np.float16))
    dnp_ = np.ascontiguousarray(np.asarray(L0_ffn_down_packed, dtype=np.int32))
    dns_ = np.ascontiguousarray(np.asarray(L0_ffn_down_scales, dtype=np.float16))

    gam_pp = np.ascontiguousarray(gamma.reshape(HC, 128).T)
    bet_pp = np.ascontiguousarray(beta.reshape(HC, 128).T)

    PERM = {0: [0, 3, 1, 2], 1: [1, 2, 0, 3]}
    in_maps = []
    for b in range(B):
        for t in range(2):
            xl = np.ascontiguousarray(x[b].reshape(4, 2048, H)[PERM[t]].reshape(S, H))
            onoff = np.zeros((128, 2), np.float32)
            onoff[:, 0] = NEG if t == 0 else 0.0
            onoff[:, 1] = 0.0 if t == 0 else NEG
            in_maps.append(dict(x=xl, wqkv=wqkv, wo=wo_, gam=gam_pp, bet=bet_pp,
                                onoff=onoff, gup=gup_, gus=gus_, dnp=dnp_, dns=dns_))

    nc = build()
    res = run_bass_kernel_spmd(nc, in_maps, core_ids=list(range(8)))
    outp = np.empty((B, 4, 2048, H), np.float32)
    for b in range(B):
        oa = res.results[2 * b]["out"]
        ob = res.results[2 * b + 1]["out"]
        outp[b, 0] = oa[0:2048]
        outp[b, 3] = oa[2048:4096]
        outp[b, 1] = ob[0:2048]
        outp[b, 2] = ob[2048:4096]
    return outp.reshape(B, S, H)


# version 11
# speedup vs baseline: 1.2462x; 1.0226x over previous
import numpy as np
from contextlib import ExitStack

import concourse.bass as bass
import concourse.mybir as mybir
from concourse.bass_utils import run_bass_kernel_spmd

F32 = mybir.dt.float32
BF16 = mybir.dt.bfloat16
F16 = mybir.dt.float16
U16 = mybir.dt.uint16
I32 = mybir.dt.int32
AF = mybir.ActivationFunctionType
ALU = mybir.AluOpType
AXX = mybir.AxisListType.X

H = 2048
HC = 16
S = 8192
NQ = 4096
SCALE = 1.0 / float(np.sqrt(2048.0))
NEG = -1.0e30
EPS1 = 1e-5
EPS2 = 1e-5
EPSR = 1e-6
I2 = 11264
IC = 44
ENGINES = ("sync", "gpsimd", "vector", "scalar", "tensor")
Z = ("vs", 0)


class Ch:
    def __init__(self, ctx, nc, name, step):
        self.sem = ctx.enter_context(nc.semaphore(name))
        self.name = name
        self.n = 0
        self.step = step


class Rec:
    COMPUTE = ("gp", "vs", "cs", "ts")

    def __init__(self, ctx, nc):
        self.nc = nc
        self.ctx = ctx
        self.q = {e: [] for e in ENGINES}
        self.ch = {n: Ch(ctx, nc, n, 1) for n in self.COMPUTE}

    def chan(self, tag):
        if tag not in self.ch:
            self.ch[tag] = Ch(self.ctx, self.nc, tag, 16)
        return self.ch[tag]

    AUTOCH = {"vector": "vs", "scalar": "cs"}

    def op(self, eng, fn, tag=None, auto=True):
        if eng in ("vector", "scalar") and auto:
            ch = self.AUTOCH[eng]
            assert tag in (None, ch), (eng, tag)
            c = self.ch[ch]
            prev = c.n
            c.n += c.step
            mark = (ch, c.n)
            sem, step = c.sem, c.step

            def emit(E, fn=fn, sem=sem, step=step, prev=prev):
                if prev > 0:
                    E.wait_ge(sem, prev)
                fn(E).then_inc(sem, step)
            self.q[eng].append(emit)
            return mark
        if tag is None:
            self.q[eng].append(fn)
            return None
        c = self.chan(tag)
        c.n += c.step
        mark = (tag, c.n)
        sem, step = c.sem, c.step

        def emit(E, fn=fn, sem=sem, step=step):
            fn(E).then_inc(sem, step)
        self.q[eng].append(emit)
        return mark

    def wt(self, eng, mark):
        if mark is None:
            return
        tag, v = mark
        if v <= 0:
            return
        sem = self.chan(tag).sem
        self.q[eng].append(lambda E, sem=sem, v=v: E.wait_ge(sem, v))

    def marks(self):
        return [(n, c.n) for n, c in self.ch.items()]

    def barrier(self, mks):
        for e in ENGINES:
            for m in mks:
                self.wt(e, m)


def build(dbg=False, upto=3, pipe=True):
    nc = bass.Bass("TRN2", target_bir_lowering=False)
    x_in = nc.dram_tensor("x", [S, H], F32, kind="ExternalInput").ap()
    wqkv = nc.dram_tensor("wqkv", [H, 3 * H], F32, kind="ExternalInput").ap()
    wo = nc.dram_tensor("wo", [H, H], F32, kind="ExternalInput").ap()
    gam = nc.dram_tensor("gam", [128, HC], F32, kind="ExternalInput").ap()
    bet = nc.dram_tensor("bet", [128, HC], F32, kind="ExternalInput").ap()
    onoff_in = nc.dram_tensor("onoff", [128, 2], F32, kind="ExternalInput").ap()
    gup = nc.dram_tensor("gup", [I2, 256], I32, kind="ExternalInput").ap()
    gus = nc.dram_tensor("gus", [16, I2], F16, kind="ExternalInput").ap()
    dnp = nc.dram_tensor("dnp", [H, 704], I32, kind="ExternalInput").ap()
    dns = nc.dram_tensor("dns", [IC, H], F16, kind="ExternalInput").ap()
    out = nc.dram_tensor("out", [NQ, H], F32, kind="ExternalOutput").ap()

    kind = "ExternalOutput" if dbg else "Internal"
    s_wq = nc.dram_tensor("s_wq", [32, HC, 128, 128], BF16, kind=kind).ap()
    s_wv = nc.dram_tensor("s_wv", [HC, 128, H], BF16, kind=kind).ap()
    s_wo = nc.dram_tensor("s_wo", [HC, 128, H], BF16, kind=kind).ap()
    s_qT = nc.dram_tensor("s_qT", [8, HC, 128, 512], BF16, kind=kind).ap()
    s_kT = nc.dram_tensor("s_kT", [4, 4, HC, 128, 512], BF16, kind=kind).ap()
    s_v = nc.dram_tensor("s_v", [4, 2, 8, 128, H], BF16, kind=kind).ap()
    s_x2 = nc.dram_tensor("s_x2", [NQ, H], F32, kind=kind).ap()
    s_x2t = nc.dram_tensor("s_x2t", [HC, 32, 128, 128], BF16, kind=kind).ap()
    s_wgu = nc.dram_tensor("s_wgu", [44, HC, 128, 256], BF16, kind=kind).ap()
    s_wdn = nc.dram_tensor("s_wdn", [HC, IC, 128, 128], BF16, kind=kind).ap()
    s_y = nc.dram_tensor("s_y", [32, HC, 128, 128], F32, kind=kind).ap()

    with ExitStack() as ctx:
        sb = lambda *a: ctx.enter_context(nc.sbuf_tensor(*a))
        R = Rec(ctx, nc)

        ident_b = sb("ident_b", [128, 128], BF16)
        ident_f = sb("ident_f", [128, 128], F32)
        gam_sb = sb("gam_sb", [128, HC], F32)
        bet_sb = sb("bet_sb", [128, HC], F32)
        bet_bf = sb("bet_bf", [128, HC], BF16)
        onoff = sb("onoff_sb", [128, 2], F32)
        bw_pp = sb("bw_pp", [128, 32], F32)
        dump = sb("dump", [128, H], BF16)

        m0 = R.op("gpsimd", lambda E: E.memset(ident_b[:, :], 0.0), tag="gp")
        R.wt("gpsimd", m0)
        m0 = R.op("gpsimd", lambda E: E.affine_select(
            out=ident_b[:, :], in_=ident_b[:, :], compare_op=ALU.not_equal,
            fill=1.0, base=0, pattern=[[-1, 128]], channel_multiplier=1), tag="gp")
        R.wt("gpsimd", m0)
        m0 = R.op("gpsimd", lambda E: E.memset(ident_f[:, :], 0.0), tag="gp")
        R.wt("gpsimd", m0)
        m_const = R.op("gpsimd", lambda E: E.affine_select(
            out=ident_f[:, :], in_=ident_f[:, :], compare_op=ALU.not_equal,
            fill=1.0, base=0, pattern=[[-1, 128]], channel_multiplier=1), tag="gp")

        R.op("sync", lambda E: E.dma_start(out=gam_sb[:, :], in_=gam), tag="cst")
        R.op("sync", lambda E: E.dma_start(out=onoff[:, :], in_=onoff_in), tag="cst")
        m_gb = R.op("sync", lambda E: E.dma_start(out=bet_sb[:, :], in_=bet), tag="cst")
        R.wt("vector", m_gb)
        m_bb = R.op("vector", lambda E: E.tensor_copy(bet_bf[:, :], bet_sb[:, :]), tag="vs")

        p01 = ExitStack()
        bwvrow = p01.enter_context(nc.sbuf_tensor("bwvrow", [1, H], BF16))
        ones_row = p01.enter_context(nc.sbuf_tensor("ones_row", [1, 128], BF16))
        with ExitStack() as p0:
            wtile2 = [p0.enter_context(nc.sbuf_tensor("wtile%d" % i, [128, 3 * H], BF16))
                      for i in range(2)]
            wotile = p0.enter_context(nc.sbuf_tensor("wotile", [128, H], BF16))
            ps_bp = p0.enter_context(nc.psum_tensor("ps_bp", [128, 32], F32))
            ps_bv = p0.enter_context(nc.psum_tensor("ps_bv", [1, H], F32))

            m_wst = [Z, Z]
            m_mm = Z
            m_bv = Z
            for dc in range(HC):
                wbi = dc % 2
                wtile = wtile2[wbi]
                R.wt("gpsimd", m_wst[wbi])
                m_wl = R.op("gpsimd", lambda E, dc=dc, wtile=wtile: E.dma_start(
                    out=wtile[:, :], in_=wqkv[dc * 128:(dc + 1) * 128, :]),
                    tag="p0w%d" % wbi)
                R.wt("tensor", m_wl)
                R.wt("tensor", m_bb)
                for oc in range(32):
                    m_mm = R.op("tensor", lambda E, dc=dc, oc=oc, wtile=wtile:
                                nc.tensor.matmul(
                                    ps_bp[:, oc:oc + 1],
                                    wtile[:, oc * 128:(oc + 1) * 128],
                                    bet_bf[:, dc:dc + 1],
                                    start=(dc == 0 and oc == 0),
                                    stop=(dc == HC - 1 and oc == 31)),
                                tag=("ts" if oc == 31 else None))
                for sl in range(4):
                    m_bv = R.op("tensor", lambda E, dc=dc, sl=sl, wtile=wtile:
                                nc.tensor.matmul(
                                    ps_bv[:, sl * 512:(sl + 1) * 512],
                                    bet_bf[:, dc:dc + 1],
                                    wtile[:, 2 * H + sl * 512:2 * H + (sl + 1) * 512],
                                    start=(dc == 0), stop=(dc == HC - 1)),
                                tag=("ts" if sl == 3 else None))
                R.wt("vector", m_mm)
                R.wt("vector", m_bv)
                m_f1 = R.op("vector", lambda E, dc=dc, wtile=wtile: E.tensor_scalar(
                    wtile[:, :], wtile[:, :], gam_sb[:, dc:dc + 1], None, ALU.mult),
                    tag="vs")
                R.wt("gpsimd", m_f1)
                R.op("gpsimd", lambda E, dc=dc, wtile=wtile: E.dma_start(
                    out=s_wq[:, dc, :, :].rearrange("oc p k -> p oc k"),
                    in_=wtile[:, 0:2 * H].rearrange("p (oc k) -> p oc k", oc=32)),
                    tag="p0st%d" % wbi)
                m_wst[wbi] = R.op("gpsimd", lambda E, dc=dc, wtile=wtile: E.dma_start(
                    out=s_wv[dc, :, :], in_=wtile[:, 2 * H:3 * H]), tag="p0st%d" % wbi)

            R.wt("vector", m_mm)
            m_bwpp = R.op("vector", lambda E: E.tensor_copy(bw_pp[:, :], ps_bp[:, :]),
                          tag="vs")
            R.wt("scalar", m_bv)
            m_bvrow = R.op("scalar", lambda E: E.copy(bwvrow[:, :], ps_bv[:, :]), tag="cs")
            R.op("vector", lambda E: E.memset(ones_row[:, :], 1.0), tag="vs")

            m_wost = Z
            for vc in range(HC):
                R.wt("gpsimd", m_wost)
                m_wol = R.op("gpsimd", lambda E, vc=vc: E.dma_start(
                    out=wotile[:, :], in_=wo[vc * 128:(vc + 1) * 128, :]), tag="p0wo")
                R.wt("gpsimd", m_wol)
                m_wost = R.op("gpsimd", lambda E, vc=vc: E.dma_start(
                    out=s_wo[vc, :, :], in_=wotile[:, :]), tag="p0wo")

        R.barrier(R.marks())

        dq_holder = {}
        dqs = {"u": 0, "pk_use": [Z] * 6, "scl_use": [Z] * 3, "ob_use": [Z, Z]}

        def dq_unit(kindu, t, slc):
            dq_pk, dq_scl = dq_holder["pk"], dq_holder["scl"]
            dq_nib, dq_mag = dq_holder["nib"], dq_holder["mag"]
            dq_bits, dq_sb, dq_pred = dq_holder["bits"], dq_holder["sb"], dq_holder["pred"]
            dq_obf = dq_holder["obf"]
            u = dqs["u"]
            dqs["u"] += 1
            bi = u % 6
            si = u % 3
            pk = dq_pk[bi]
            scl = dq_scl[si]
            W = 1024
            R.wt("sync", dqs["pk_use"][bi])
            if kindu == "gu":
                src = gup.bitcast(U16)[slc * 1024:(slc + 1) * 1024, t * 128:(t + 1) * 128]
            else:
                src = dnp.bitcast(U16)[slc * 1024:(slc + 1) * 1024, t * 128:(t + 1) * 128]
            m_pk = R.op("sync", lambda E, src=src, pk=pk, W=W:
                        E.dma_start_transpose(pk[:, 0:W], src), tag="pk%d" % bi)
            R.wt("gpsimd", dqs["scl_use"][si])
            if kindu == "gu":
                ssrc = gus[4 * t:4 * t + 4, slc * 1024:(slc + 1) * 1024]
            else:
                ssrc = dns[4 * t:4 * t + 4, slc * 1024:(slc + 1) * 1024]
            m_scl = R.op("gpsimd", lambda E, ssrc=ssrc, scl=scl, W=W:
                         E.dma_start(out=scl[:, 0:W],
                                     in_=ssrc.unsqueeze(1).broadcast_to([4, 32, W])),
                         tag="scl%d" % si)
            R.wt("vector", m_pk)
            R.wt("vector", m_scl)
            for j in range(4):
                ob = dq_obf[j % 2]
                m1 = R.op("vector", lambda E, j=j, pk=pk, W=W: E.tensor_scalar(
                    dq_nib[:, 0:W], pk[:, 0:W], 4 * j, 15,
                    ALU.logical_shift_right, ALU.bitwise_and), tag="vs")
                R.op("vector", lambda E, W=W: E.tensor_scalar(
                    dq_mag[:, 0:W], dq_nib[:, 0:W], 7, None, ALU.bitwise_and))
                R.op("vector", lambda E, W=W: E.tensor_scalar(
                    dq_sb[:, 0:W], dq_nib[:, 0:W], 8, 12,
                    ALU.bitwise_and, ALU.logical_shift_left))
                R.op("vector", lambda E, W=W: E.tensor_scalar(
                    dq_bits[:, 0:W], dq_mag[:, 0:W], 64, 16128, ALU.mult, ALU.add))
                R.op("vector", lambda E, W=W: E.tensor_scalar(
                    dq_pred[:, 0:W], dq_mag[:, 0:W], 1, None, ALU.is_le))
                R.op("vector", lambda E, W=W: E.tensor_scalar(
                    dq_mag[:, 0:W], dq_mag[:, 0:W], 16128, None, ALU.mult))
                R.op("vector", lambda E, W=W: E.copy_predicated(
                    dq_bits[:, 0:W], dq_pred[:, 0:W], dq_mag[:, 0:W]))
                R.op("vector", lambda E, W=W: E.tensor_tensor(
                    dq_nib[:, 0:W], dq_bits[:, 0:W], dq_sb[:, 0:W], ALU.bitwise_or))
                R.wt("vector", dqs["ob_use"][j % 2])
                if kindu == "gu":
                    m7 = R.op("vector", lambda E, W=W, scl=scl, ob=ob: E.tensor_tensor(
                        ob[:, 0:W]
                        .rearrange("p (qh j q7) -> p qh j q7", qh=2, j=4),
                        dq_nib.bitcast(BF16)[:, 0:W]
                        .rearrange("p (qh q7 j) -> p qh j q7", qh=2, q7=128),
                        scl[:, 0:W]
                        .rearrange("p (qh q7 j) -> p qh j q7", qh=2, q7=128),
                        ALU.mult), tag="vs")
                else:
                    m7 = R.op("vector", lambda E, W=W, scl=scl, ob=ob: E.tensor_tensor(
                        ob[:, 0:W], dq_nib.bitcast(BF16)[:, 0:W],
                        scl[:, 0:W], ALU.mult), tag="vs")
                R.wt("gpsimd", m7)
                if kindu == "gu":
                    n = t * 4 + j
                    m_st = R.op("gpsimd", lambda E, n=n, slc=slc, ob=ob: E.dma_start(
                        out=s_wgu[4 * slc:4 * slc + 4, n, :, :]
                        .rearrange("pp p k -> p pp k"),
                        in_=ob[:, 0:1024]
                        .rearrange("p (pp k) -> p pp k", pp=4)), tag="dqst%d" % (j % 2))
                else:
                    mm = t * 4 + j
                    m_st = R.op("gpsimd", lambda E, mm=mm, slc=slc, ob=ob: E.dma_start(
                        out=s_wdn[8 * slc:8 * slc + 8, mm, :, :]
                        .rearrange("hc p k -> p hc k"),
                        in_=ob[:, 0:1024]
                        .rearrange("p (hc k) -> p hc k", hc=8)), tag="dqst%d" % (j % 2))
                dqs["ob_use"][j % 2] = m_st
            dqs["pk_use"][bi] = ("vs", R.ch["vs"].n)
            dqs["scl_use"][si] = ("vs", R.ch["vs"].n)
            return m_st

        dq_list = ([("gu", t, s_) for t in range(4) for s_ in range(11)] +
                   [("dn", t, s_) for t in range(11) for s_ in range(2)])
        dq_i = [0]
        dq_laststore = [Z]

        def emit_dq(nun):
            while nun > 0 and dq_i[0] < len(dq_list):
                dq_laststore[0] = dq_unit(*dq_list[dq_i[0]])
                dq_i[0] += 1
                nun -= 1

        with ExitStack() as p1:
            psb = lambda *a: p1.enter_context(nc.sbuf_tensor(*a))
            dq_holder["pk"] = [psb("dq_pk%d" % i, [128, 1024], U16) for i in range(6)]
            dq_holder["nib"] = psb("dq_nib", [128, 1024], U16)
            dq_holder["mag"] = psb("dq_mag", [128, 1024], U16)
            dq_holder["bits"] = psb("dq_bits", [128, 1024], U16)
            dq_holder["sb"] = psb("dq_sb", [128, 1024], U16)
            dq_holder["pred"] = psb("dq_pred", [128, 1024], U16)
            dq_holder["scl"] = [psb("dq_scl%d" % i, [128, 1024], BF16) for i in range(3)]
            dq_holder["obf"] = [psb("dq_ob%d" % i, [128, 1024], BF16) for i in range(2)]
            x_raw = [psb("x_raw0", [128, H], F32), psb("x_raw1", [128, H], F32)]
            xhat = [psb("xhat%d" % i, [128, H], BF16) for i in range(8)]
            xnT = psb("xnT", [128, HC, 1024], BF16)
            wqk = [psb("wqk0", [128, HC, 128], BF16), psb("wqk1", [128, HC, 128], BF16)]
            wv_sb = psb("wv_sb", [128, HC, H], BF16)
            qkstg = [psb("qkstg0", [128, 1024], BF16), psb("qkstg1", [128, 1024], BF16)]
            vstg = [psb("vstg0", [128, H], BF16), psb("vstg1", [128, H], BF16)]
            sums8 = psb("p1su", [128, 8], F32)
            ssq8 = psb("p1sq", [128, 8], F32)
            mu8 = psb("p1mu", [128, 8], F32)
            var8 = psb("p1va", [128, 8], F32)
            rstd8 = psb("p1rs", [128, 8], F32)
            nmr8 = psb("p1nm", [128, 8], F32)
            t18 = psb("p1t1", [128, 8], F32)

            xr_use = [Z, Z]
            xh_use = [Z] * 8
            wqk_use = [Z, Z]
            qkstg_use = [Z, Z]
            vstg_use = [Z, Z]
            xnt_use = Z
            v_done = (Z, Z)
            m_wv = Z
            ln_marks = {}

            def rec_ln(T):
                marks = []
                for s_g in range(8):
                    bi = s_g % 2
                    hb = s_g
                    R.wt("sync", xr_use[bi])
                    rows = T * 1024 + s_g * 128
                    m_x = R.op("sync", lambda E, bi=bi, rows=rows: E.dma_start(
                        out=x_raw[bi][:, :],
                        in_=x_in[rows:rows + 128, :]), tag="x%d" % bi)
                    xb = x_raw[bi]
                    sg = s_g
                    sums = sums8[:, sg:sg + 1]
                    ssq = ssq8[:, sg:sg + 1]
                    mu = mu8[:, sg:sg + 1]
                    var = var8[:, sg:sg + 1]
                    rstd = rstd8[:, sg:sg + 1]
                    nmr = nmr8[:, sg:sg + 1]
                    t1 = t18[:, sg:sg + 1]
                    R.wt("vector", m_x)
                    R.wt("scalar", m_x)
                    m_sum = R.op("vector", lambda E, xb=xb, sums=sums: E.reduce_sum(
                        sums, xb[:, :], axis=AXX), tag="vs")
                    m_ssq = R.op("scalar", lambda E, xb=xb, ssq=ssq: E.activation(
                        dump[:, :], xb[:, :], AF.Square,
                        accum_out=ssq), tag="cs")
                    R.wt("vector", m_ssq)
                    R.wt("vector", m_sum)
                    m1 = R.op("vector", lambda E, mu=mu, sums=sums: E.tensor_scalar(
                        mu, sums, 1.0 / H, None, ALU.mult), tag="vs")
                    m2 = R.op("vector", lambda E, t1=t1, ssq=ssq: E.tensor_scalar(
                        t1, ssq, 1.0 / H, None, ALU.mult), tag="vs")
                    m3 = R.op("vector", lambda E, var=var, mu=mu: E.tensor_mul(
                        var, mu, mu), tag="vs")
                    m4 = R.op("vector", lambda E, var=var, t1=t1:
                              nc.vector.scalar_tensor_tensor(
                                  var, var, -1.0, t1, ALU.mult, ALU.add), tag="vs")
                    m5 = R.op("vector", lambda E, var=var: E.tensor_scalar(
                        var, var, EPS1, None, ALU.add), tag="vs")
                    R.wt("scalar", m5)
                    m6 = R.op("scalar", lambda E, rstd=rstd, var=var: E.activation(
                        rstd, var, AF.Sqrt), tag="cs")
                    R.wt("vector", m6)
                    m7 = R.op("vector", lambda E, rstd=rstd: E.reciprocal(
                        rstd, rstd), tag="vs")
                    m8 = R.op("vector", lambda E, nmr=nmr, mu=mu, rstd=rstd:
                              nc.vector.scalar_tensor_tensor(
                                  nmr, mu, -1.0, rstd, ALU.mult, ALU.mult), tag="vs")
                    R.wt("scalar", m8)
                    R.wt("scalar", xh_use[hb])
                    m_xh = R.op("scalar", lambda E, xb=xb, hb=hb, nmr=nmr, rstd=rstd:
                                E.activation(xhat[hb][:, :], xb[:, :], AF.Identity,
                                             bias=nmr, scale=rstd),
                                tag="cs")
                    xr_use[bi] = m_xh
                    marks.append(m_xh)
                ln_marks[T] = marks

            def rec_tr(T):
                nonlocal xnt_use, v_done, m_wv, xnt_rdy
                with ExitStack() as pT:
                    ps_tr2 = [pT.enter_context(
                        nc.psum_tensor("p1tr%d_%d" % (T, i), [128, 1024], BF16))
                        for i in range(2)]
                    tr_evac = [Z, Z]
                    nbatch = 0
                    for s_g in range(8):
                        hb = s_g
                        R.wt("tensor", ln_marks[T][s_g])
                        R.wt("tensor", m_const)
                        R.wt("tensor", xnt_use)
                        R.wt("tensor", v_done[0])
                        R.wt("tensor", v_done[1])
                        for dcg in range(4):
                            tb = nbatch % 2
                            nbatch += 1
                            R.wt("tensor", tr_evac[tb])
                            for k in range(4):
                                dc = dcg * 4 + k
                                m_tr = R.op("tensor", lambda E, hb=hb, dc=dc, k=k, tb=tb:
                                            nc.tensor.transpose(
                                                ps_tr2[tb][:, k * 128:(k + 1) * 128],
                                                xhat[hb][:, dc * 128:(dc + 1) * 128],
                                                ident_b[:, :]),
                                            tag=("ts" if k == 3 else None))
                            R.wt("vector", m_tr)
                            tr_evac[tb] = R.op(
                                "vector", lambda E, dcg=dcg, s_g=s_g, tb=tb:
                                E.tensor_copy(
                                    xnT[:, 4 * dcg:4 * dcg + 4,
                                        s_g * 128:(s_g + 1) * 128],
                                    ps_tr2[tb][:, 0:512].rearrange(
                                        "p (d k) -> p d k", d=4)), tag="vs")
                        xh_use[hb] = ("ts", R.ch["ts"].n)
                    xnt_rdy = (tr_evac[0], tr_evac[1])
                    if T == 0:
                        m_wv = R.op("sync", lambda E: E.dma_start(
                            out=wv_sb[:, :, :],
                            in_=s_wv.rearrange("d p h -> p d h")), tag="wv")

            def rec_qkv(T):
                nonlocal xnt_use, v_done, m_wv, qk_done
                has_q = T < 4
                with ExitStack() as pQ:
                    ps_qk = [pQ.enter_context(nc.psum_tensor(
                        "p1qk%d_%d" % (T, i), [128, 1024], F32)) for i in range(2)]
                    psqk_use = [Z, Z]
                    oc_list = list(range(32)) if has_q else list(range(16, 32))
                    for idx, oc in enumerate(oc_list):
                        wb = idx % 2
                        pb = idx % 2
                        R.wt("sync", wqk_use[wb])
                        m_w = R.op("sync", lambda E, oc=oc, wb=wb: E.dma_start(
                            out=wqk[wb][:, :, :],
                            in_=s_wq[oc, :, :, :].rearrange("d p k -> p d k")),
                            tag="wqk%d" % wb)
                        R.wt("tensor", m_w)
                        R.wt("tensor", xnt_rdy[0])
                        R.wt("tensor", xnt_rdy[1])
                        R.wt("tensor", psqk_use[pb])
                        for dc in range(HC):
                            for h2 in range(2):
                                m_mm = R.op("tensor", lambda E, wb=wb, dc=dc, h2=h2, pb=pb:
                                            nc.tensor.matmul(
                                                ps_qk[pb][:, h2 * 512:(h2 + 1) * 512],
                                                wqk[wb][:, dc, :],
                                                xnT[:, dc, h2 * 512:(h2 + 1) * 512],
                                                start=(dc == 0), stop=(dc == HC - 1)),
                                            tag=("ts" if dc == HC - 1 and h2 == 1 else None))
                        wqk_use[wb] = ("ts", R.ch["ts"].n)
                        R.wt("scalar", m_mm)
                        R.wt("scalar", m_bwpp)
                        R.wt("scalar", qkstg_use[pb])
                        m_ev = R.op("scalar", lambda E, oc=oc, pb=pb: E.activation(
                            qkstg[pb][:, :], ps_qk[pb][:, :], AF.Identity,
                            bias=bw_pp[:, oc:oc + 1]), tag="cs")
                        psqk_use[pb] = m_ev
                        R.wt("gpsimd", m_ev)
                        if oc < 16:
                            m_st = R.op("gpsimd", lambda E, oc=oc, T=T, pb=pb: E.dma_start(
                                out=s_qT[2 * T:2 * T + 2, oc, :, :]
                                .rearrange("g p k -> p g k"),
                                in_=qkstg[pb][:, :].rearrange("p (g k) -> p g k", g=2)),
                                tag="qst%d" % pb)
                        else:
                            kb, hf = T >> 1, T & 1
                            m_st = R.op("gpsimd", lambda E, oc=oc, kb=kb, hf=hf, pb=pb:
                                        E.dma_start(
                                            out=s_kT[kb, 2 * hf:2 * hf + 2, oc - 16, :, :]
                                            .rearrange("qq p k -> p qq k"),
                                            in_=qkstg[pb][:, :].rearrange(
                                                "p (qq k) -> p qq k", qq=2)),
                                        tag="qst%d" % pb)
                        qkstg_use[pb] = m_st
                        if oc % 4 == 3:
                            emit_dq(1)
                    qk_done = (psqk_use[0], psqk_use[1])

                with ExitStack() as pV:
                    ps_v = [pV.enter_context(nc.psum_tensor(
                        "p1v%d_%d" % (T, i), [128, H], F32)) for i in range(2)]
                    psv_use = [Z, Z]
                    R.wt("tensor", m_wv)
                    R.wt("tensor", qk_done[0])
                    R.wt("tensor", qk_done[1])
                    for tc in range(8):
                        pb = tc % 2
                        R.wt("tensor", psv_use[pb])
                        for dc in range(HC):
                            for sl in range(4):
                                R.op("tensor", lambda E, dc=dc, sl=sl, tc=tc, pb=pb:
                                     nc.tensor.matmul(
                                         ps_v[pb][:, sl * 512:(sl + 1) * 512],
                                         xnT[:, dc, tc * 128:(tc + 1) * 128],
                                         wv_sb[:, dc, sl * 512:(sl + 1) * 512],
                                         start=(dc == 0), stop=False))
                        for sl in range(4):
                            m_mm = R.op("tensor", lambda E, sl=sl, pb=pb:
                                        nc.tensor.matmul(
                                            ps_v[pb][:, sl * 512:(sl + 1) * 512],
                                            ones_row[0:1, :],
                                            bwvrow[0:1, sl * 512:(sl + 1) * 512],
                                            start=False, stop=True),
                                        tag=("ts" if sl == 3 else None))
                        R.wt("scalar", m_mm)
                        R.wt("scalar", vstg_use[pb])
                        m_ev = R.op("scalar", lambda E, pb=pb: E.copy(
                            vstg[pb][:, :], ps_v[pb][:, :]), tag="cs")
                        psv_use[pb] = m_ev
                        R.wt("gpsimd", m_ev)
                        gtc = T * 8 + tc
                        kb, hf, tcl = gtc >> 4, (gtc >> 3) & 1, gtc & 7
                        m_st = R.op("gpsimd", lambda E, kb=kb, hf=hf, tcl=tcl, pb=pb:
                                    E.dma_start(out=s_v[kb, hf, tcl, :, :],
                                                in_=vstg[pb][:, :]), tag="vst%d" % pb)
                        vstg_use[pb] = m_st
                        if tc == 3:
                            emit_dq(1)
                    xnt_use = ("ts", R.ch["ts"].n)
                    v_done = (psv_use[0], psv_use[1])

            qk_done = (Z, Z)
            xnt_rdy = (Z, Z)
            if pipe:
                rec_ln(0)
                for T in range(8):
                    rec_tr(T)
                    if T < 7:
                        rec_ln(T + 1)
                    rec_qkv(T)
            else:
                for T in range(8):
                    rec_ln(T)
                    rec_tr(T)
                    rec_qkv(T)

        emit_dq(100)
        p01.close()
        R.barrier(R.marks())

        if upto >= 2:
            with ExitStack() as p2:
                psb = lambda *a: p2.enter_context(nc.sbuf_tensor(*a))
                mdiag = psb("mdiag", [128, 3968], BF16)
                qarena = psb("qarena", [128, HC, 512], BF16)
                kv = [psb("kv0", [128, 8192], BF16), psb("kv1", [128, 8192], BF16)]
                parena = psb("parena", [128, 4, 4096], BF16)
                ptar = psb("ptar", [128, 32, 512], BF16)
                osb = psb("osb", [128, 4, H], F32)
                msk = [psb("msk0", [128, 1024], F32), psb("msk1", [128, 1024], F32)]
                wost = [psb("wost0", [128, H], BF16), psb("wost1", [128, H], BF16)]
                xres = psb("xres", [128, H], F32)
                x2f = psb("x2f", [128, H], F32)
                x2b = psb("x2b", [128, H], BF16)
                x2stg = psb("x2stg", [128, 512], BF16)
                mrun = psb("mrun", [128, 4], F32)
                mprev = psb("mprev", [128, 4], F32)
                mbuf = psb("mbuf", [128, 4, 4], F32)
                lbuf = psb("lbuf", [128, 4, 4], F32)
                gbuf = psb("gbuf", [128, 4, 4], F32)
                gx = psb("gx", [128, 4], F32)
                lsb = psb("lsb", [128, 4], F32)
                lpass = psb("lpass", [128, 1], F32)
                negb = psb("negb", [128, 1], F32)
                negb2g = psb("negb2", [128, 4], F32)
                rl = psb("rl", [128, 1], F32)
                st2g = psb("st2", [128, 4], F32)
                bmax = psb("bmax", [128, 1], F32)
                st3g = psb("st3", [128, 4], F32)
                st4g = psb("st4", [128, 4], F32)

                q_use = Z
                ge_done = (Z, Z)
                x2ev_done = Z
                rl_use = Z
                x2f_use = Z
                x2tx_prev = Z
                osb_use = [Z] * 4
                kv_use = [Z, Z]
                msk_use = [Z, Z]
                wost_use = [Z, Z]
                xres_use = Z
                negb_use = Z
                x2stg_use = Z
                parena_use = [Z] * 4

                m0b = R.op("gpsimd", lambda E: E.memset(mdiag[:, :], 0.0), tag="gp")
                R.wt("gpsimd", m0b)
                m_mdiag = R.op("gpsimd", lambda E: E.affine_select(
                    out=mdiag[:, :], in_=mdiag[:, :], compare_op=ALU.is_ge,
                    fill=NEG, base=1920, pattern=[[-1, 3968]], channel_multiplier=1), tag="gp")
                R.wt("vector", m_mdiag)

                GRP_PASSES = [[[0, 2]]] * 4 + [[[0, 1], [2, 3]]] * 4

                def slot_type(g, kb):
                    if g < 4:
                        return "diag" if kb == 0 else "on0"
                    return {0: "full", 1: "diag", 2: "full", 3: "on1"}[kb]

                for g in range(8):
                    R.wt("sync", q_use)
                    m_q = R.op("sync", lambda E, g=g: E.dma_start(
                        out=qarena[:, :, :],
                        in_=s_qT[g, :, :, :].rearrange("d p k -> p d k")), tag="q2")
                    m_mi = R.op("vector", lambda E: E.memset(mrun[:, :], -3.0e38), tag="vs")
                    passes = GRP_PASSES[g]
                    pv_last = [Z] * 4
                    o_fold = [Z] * 4
                    for pidx, kbs in enumerate(passes):
                        if pidx == 0:
                            R.wt("tensor", ge_done[0])
                            R.wt("tensor", ge_done[1])
                        else:
                            for s4 in range(4):
                                R.wt("tensor", o_fold[s4])
                        with ExitStack() as pqk:
                            ps_qk = [pqk.enter_context(nc.psum_tensor(
                                "p2qk%d_%d_%d" % (g, pidx, i), [128, 1024], F32))
                                for i in range(2)]
                            ps_pt = pqk.enter_context(nc.psum_tensor(
                                "p2pt%d_%d" % (g, pidx), [128, 1024], BF16))
                            psqk_use = [Z, Z]
                            pt_evac = Z
                            for ki, kb in enumerate(kbs):
                                styp = slot_type(g, kb)
                                for hf in range(2):
                                    hidx = ki * 2 + hf
                                    m_k = [Z, Z]
                                    for q2 in range(2):
                                        kbuf = q2
                                        R.wt("sync", kv_use[kbuf])
                                        m_k[q2] = R.op("sync",
                                                       lambda E, kb=kb, hf=hf, q2=q2, kbuf=kbuf:
                                                       E.dma_start(
                                                           out=kv[kbuf].rearrange(
                                                               "p (d k) -> p d k", d=HC),
                                                           in_=s_kT[kb, 2 * hf + q2, :, :, :]
                                                           .rearrange("d p k -> p d k")),
                                                       tag="kv%d" % kbuf)
                                    for s in range(4):
                                        pb = s % 2
                                        R.wt("tensor", m_q)
                                        R.wt("tensor", psqk_use[pb])
                                        for q2 in range(2):
                                            kbuf = q2
                                            R.wt("tensor", m_k[q2])
                                            for dc in range(HC):
                                                m_mm = R.op(
                                                    "tensor",
                                                    lambda E, dc=dc, q2=q2, s=s, pb=pb, kbuf=kbuf:
                                                    nc.tensor.matmul(
                                                        ps_qk[pb][:, q2 * 512:(q2 + 1) * 512],
                                                        qarena[:, dc, s * 128:(s + 1) * 128],
                                                        kv[kbuf].rearrange(
                                                            "p (d k) -> p d k", d=HC)[:, dc, :],
                                                        start=(dc == 0), stop=(dc == HC - 1)),
                                                    tag=("ts" if dc == HC - 1 else None))
                                            kv_use[kbuf] = ("ts", R.ch["ts"].n)
                                        q_use = ("ts", R.ch["ts"].n)
                                        mb = s % 2
                                        R.wt("vector", m_mm)
                                        R.wt("vector", msk_use[mb])
                                        r_q = (g % 4) * 4 + s
                                        if styp == "diag":
                                            off = 1920 - 128 * r_q + hf * 1024
                                            m_msk = R.op("vector", lambda E, pb=pb, mb=mb, off=off:
                                                         E.tensor_add(
                                                             msk[mb][:, :], ps_qk[pb][:, :],
                                                             mdiag[:, off:off + 1024]), tag="vs")
                                        elif styp == "full":
                                            m_msk = R.op("vector", lambda E, pb=pb, mb=mb:
                                                         E.tensor_copy(
                                                             msk[mb][:, :], ps_qk[pb][:, :]),
                                                         tag="vs")
                                        else:
                                            col = 0 if styp == "on0" else 1
                                            m_msk = R.op("vector", lambda E, pb=pb, mb=mb, col=col:
                                                         E.tensor_scalar(
                                                             msk[mb][:, :], ps_qk[pb][:, :],
                                                             onoff[:, col:col + 1], None,
                                                             ALU.add), tag="vs")
                                        psqk_use[pb] = m_msk
                                        R.wt("vector", m_msk)
                                        if pidx == 0 and hidx == 0:
                                            R.wt("vector", m_mi)
                                        m_bm = R.op("vector", lambda E, mb=mb: E.reduce_max(
                                            bmax[:, 0:1], msk[mb][:, :], axis=AXX), tag="vs")
                                        R.wt("vector", m_bm)
                                        m_mx = R.op("vector", lambda E, s=s: E.tensor_max(
                                            mrun[:, s:s + 1], mrun[:, s:s + 1], bmax[:, 0:1]),
                                            tag="vs")
                                        R.wt("vector", m_mx)
                                        m_sv = R.op("vector", lambda E, s=s, hidx=hidx:
                                                    E.tensor_copy(mbuf[:, s, hidx:hidx + 1],
                                                                  mrun[:, s:s + 1]), tag="vs")
                                        R.wt("vector", m_sv)
                                        R.wt("vector", negb_use)
                                        m_nb = R.op("vector", lambda E, s=s: E.tensor_scalar(
                                            negb[:, :], mrun[:, s:s + 1], -SCALE, None,
                                            ALU.mult), tag="vs")
                                        R.wt("scalar", m_nb)
                                        if pidx == 0 and hidx == 0:
                                            R.wt("scalar", parena_use[s])
                                        m_exp = R.op("scalar", lambda E, mb=mb, s=s, hidx=hidx:
                                                     E.activation(
                                                         parena[:, s,
                                                                hidx * 1024:(hidx + 1) * 1024],
                                                         msk[mb][:, :], AF.Exp,
                                                         bias=negb[:, 0:1], scale=SCALE,
                                                         accum_out=lbuf[:, s, hidx:hidx + 1]),
                                                     tag="cs")
                                        msk_use[mb] = m_exp
                                        negb_use = m_exp
                            for s in range(4):
                                R.wt("vector", negb_use)
                                m_nb = R.op("vector", lambda E, s=s: E.tensor_scalar(
                                    negb[:, :], mrun[:, s:s + 1], -SCALE, None, ALU.mult),
                                    tag="vs")
                                R.wt("scalar", m_nb)
                                R.wt("scalar", msk_use[0])
                                R.wt("scalar", msk_use[1])
                                m_gb2 = R.op("scalar", lambda E, s=s: E.activation(
                                    gbuf[:, s, :], mbuf[:, s, :], AF.Exp, bias=negb[:, 0:1],
                                    scale=SCALE), tag="cs")
                                negb_use = m_gb2
                                if pidx > 0:
                                    m_gx = R.op("scalar", lambda E, s=s: E.activation(
                                        gx[:, s:s + 1], mprev[:, s:s + 1], AF.Exp,
                                        bias=negb[:, 0:1], scale=SCALE), tag="cs")
                                    negb_use = m_gx
                                R.wt("vector", m_gb2)
                                m_rs = Z
                                for hidx in range(4):
                                    m_rs = R.op("vector", lambda E, s=s, hidx=hidx:
                                                E.tensor_scalar(
                                                    parena[:, s, hidx * 1024:(hidx + 1) * 1024],
                                                    parena[:, s, hidx * 1024:(hidx + 1) * 1024],
                                                    gbuf[:, s, hidx:hidx + 1], None, ALU.mult),
                                                tag="vs")
                                R.wt("vector", m_rs)
                                m_lg = R.op("vector", lambda E, s=s: E.tensor_mul(
                                    gbuf[:, s, :], lbuf[:, s, :], gbuf[:, s, :]), tag="vs")
                                R.wt("vector", m_lg)
                                m_lp = R.op("vector", lambda E, s=s: E.reduce_sum(
                                    lpass[:, 0:1], gbuf[:, s, :],
                                    axis=AXX), tag="vs")
                                R.wt("vector", m_lp)
                                if pidx == 0:
                                    R.op("vector", lambda E, s=s: E.tensor_copy(
                                        lsb[:, s:s + 1], lpass[:, 0:1]), tag="vs")
                                else:
                                    R.wt("vector", m_gx)
                                    R.op("vector", lambda E, s=s:
                                         nc.vector.scalar_tensor_tensor(
                                             lsb[:, s:s + 1], lsb[:, s:s + 1], gx[:, s:s + 1],
                                             lpass[:, 0:1], ALU.mult, ALU.add), tag="vs")
                                R.op("vector", lambda E, s=s: E.tensor_copy(
                                    mprev[:, s:s + 1], mrun[:, s:s + 1]), tag="vs")
                                R.wt("tensor", m_rs)
                                for kcg in range(8):
                                    R.wt("tensor", pt_evac)
                                    for k in range(4):
                                        kc = kcg * 4 + k
                                        m_tp = R.op("tensor", lambda E, s=s, kc=kc, k=k:
                                                    nc.tensor.transpose(
                                                        ps_pt[:, k * 256:k * 256 + 128],
                                                        parena[:, s, kc * 128:(kc + 1) * 128],
                                                        ident_b[:, :]),
                                                    tag=("ts" if k == 3 else None))
                                    R.wt("vector", m_tp)
                                    pt_evac = R.op("vector", lambda E, s=s, kcg=kcg:
                                                   E.tensor_copy(
                                                       ptar[:, 4 * kcg:4 * kcg + 4,
                                                            s * 128:(s + 1) * 128],
                                                       ps_pt[:, :].rearrange(
                                                           "p (k q) -> p k q", k=4)[:, :, 0:128]),
                                                   tag="vs")
                        with ExitStack() as ppv:
                            ps_o = [ppv.enter_context(nc.psum_tensor(
                                "p2o%d_%d_%d" % (g, pidx, i), [128, H], F32)) for i in range(2)]
                            R.wt("tensor", x2ev_done)
                            pso_use = [o_fold[0], o_fold[1]]
                            for pair in ((0, 1), (2, 3)):
                                for kcg in range(8):
                                    vbuf = kcg % 2
                                    R.wt("sync", kv_use[vbuf])
                                    kb0 = kbs[kcg // 4]
                                    hf0 = (kcg % 4) // 2
                                    tc0 = (kcg % 2) * 4
                                    m_v = R.op("sync", lambda E, kb0=kb0, hf0=hf0, tc0=tc0,
                                               vbuf=vbuf: E.dma_start(
                                                   out=kv[vbuf].rearrange(
                                                       "p (c v) -> p c v", c=4),
                                                   in_=s_v[kb0, hf0, tc0:tc0 + 4, :, :]
                                                   .rearrange("c p v -> p c v")),
                                           tag="kv%d" % vbuf)
                                    for k in range(4):
                                        kc = kcg * 4 + k
                                        for s in pair:
                                            R.wt("tensor", m_v)
                                            R.wt("tensor", pt_evac)
                                            if kc == 0:
                                                R.wt("tensor", pso_use[s % 2])
                                            for sl in range(4):
                                                m_pv = R.op(
                                                    "tensor",
                                                    lambda E, kc=kc, s=s, sl=sl, k=k, vbuf=vbuf:
                                                    nc.tensor.matmul(
                                                        ps_o[s % 2][:, sl * 512:(sl + 1) * 512],
                                                        ptar[:, kc, s * 128:(s + 1) * 128],
                                                        kv[vbuf].rearrange(
                                                            "p (c v) -> p c v", c=4)
                                                        [:, k, sl * 512:(sl + 1) * 512],
                                                        start=(kc == 0), stop=(kc == 31)),
                                                    tag=("ts" if sl == 3 else None))
                                            pv_last[s] = m_pv
                                    kv_use[vbuf] = ("ts", R.ch["ts"].n)
                                for s in pair:
                                    R.wt("vector", pv_last[s])
                                    R.wt("vector", osb_use[s])
                                    if pidx == 0:
                                        m_of = R.op("vector", lambda E, s=s: E.tensor_copy(
                                            osb[:, s, :], ps_o[s % 2][:, :]), tag="vs")
                                    else:
                                        R.wt("vector", negb_use)
                                        m_of = R.op("vector", lambda E, s=s:
                                                    nc.vector.scalar_tensor_tensor(
                                                        osb[:, s, :], osb[:, s, :],
                                                        gx[:, s:s + 1], ps_o[s % 2][:, :],
                                                        ALU.mult, ALU.add), tag="vs")
                                    o_fold[s] = m_of
                                    pso_use[s % 2] = m_of

                    with ExitStack() as pop:
                        ps_at4 = [pop.enter_context(nc.psum_tensor(
                            "p3at%d_%d" % (g, i), [128, 512], F32)) for i in range(4)]
                        ps_ot = pop.enter_context(nc.psum_tensor("p3ot%d" % g, [128, 1024], BF16))
                        ps_pad = pop.enter_context(nc.psum_tensor("p3pd%d" % g, [128, 512], F32))
                        ps_xt = pop.enter_context(nc.psum_tensor("p3xt%d" % g, [128, 1024], BF16))
                        ot_evac = Z
                        m_ob = [Z] * 4
                        for s in range(4):
                            R.wt("vector", o_fold[s])
                            R.wt("vector", rl_use)
                            m_rl = R.op("vector", lambda E, s=s: E.reciprocal(
                                rl[:, 0:1], lsb[:, s:s + 1]), tag="vs")
                            R.wt("scalar", m_rl)
                            R.wt("scalar", pt_evac)
                            m_ob[s] = R.op("scalar", lambda E, s=s: E.activation(
                                parena[:, s, 0:H], osb[:, s, :], AF.Copy,
                                scale=rl[:, 0:1]), tag="cs")
                            rl_use = m_ob[s]
                            osb_use[s] = m_ob[s]
                        for s in range(4):
                            R.wt("tensor", m_ob[s])
                            for vcg in range(4):
                                R.wt("tensor", ot_evac)
                                for k in range(4):
                                    vc = vcg * 4 + k
                                    m_to = R.op("tensor", lambda E, s=s, vc=vc, k=k:
                                                nc.tensor.transpose(
                                                    ps_ot[:, k * 256:k * 256 + 128],
                                                    parena[:, s, vc * 128:(vc + 1) * 128],
                                                    ident_b[:, :]),
                                                tag=("ts" if k == 3 else None))
                                R.wt("vector", m_to)
                                ot_evac = R.op("vector", lambda E, s=s, vcg=vcg: E.tensor_copy(
                                    ptar[:, 4 * vcg:4 * vcg + 4, s * 128:(s + 1) * 128],
                                    ps_ot[:, :].rearrange("p (k q) -> p k q", k=4)[:, :, 0:128]),
                                    tag="vs")
                            parena_use[s] = ("ts", R.ch["ts"].n)
                        m_stt = [[Z] * 4 for _ in range(4)]
                        for sl in range(4):
                            R.wt("sync", xres_use)
                            m_xr = R.op("sync", lambda E, g=g, sl=sl: E.dma_start(
                                out=xres[:, :].rearrange("p (s k) -> p s k", s=4),
                                in_=x_in[g * 512:(g + 1) * 512,
                                         sl * 512:(sl + 1) * 512]
                                .rearrange("(s p) k -> p s k", s=4)), tag="xr")
                            m_stop = [Z] * 4
                            for vc in range(HC):
                                wb = vc % 2
                                R.wt("sync", wost_use[wb])
                                m_w = R.op("sync", lambda E, vc=vc, wb=wb, sl=sl: E.dma_start(
                                    out=wost[wb][:, 0:512],
                                    in_=s_wo[vc, :, sl * 512:(sl + 1) * 512]),
                                    tag="wo%d" % wb)
                                R.wt("tensor", m_w)
                                if vc == 0:
                                    R.wt("tensor", ot_evac)
                                for s in range(4):
                                    if vc == 0 and sl > 0:
                                        R.wt("tensor", m_stt[sl - 1][s])
                                    tagd = "ts" if (s == 3 or vc == HC - 1) else None
                                    m_op = R.op("tensor", lambda E, vc=vc, s=s, wb=wb:
                                                nc.tensor.matmul(
                                                    ps_at4[s][:, :],
                                                    ptar[:, vc, s * 128:(s + 1) * 128],
                                                    wost[wb][:, 0:512],
                                                    start=(vc == 0), stop=(vc == HC - 1)),
                                                tag=tagd)
                                    if vc == HC - 1:
                                        m_stop[s] = m_op
                                wost_use[wb] = ("ts", R.ch["ts"].n)
                            for s in range(4):
                                R.wt("vector", m_stop[s])
                                R.wt("vector", m_xr)
                                if sl == 0:
                                    R.wt("vector", m_ob[s])
                                m_stt[sl][s] = R.op(
                                    "vector", lambda E, s=s, sl=sl:
                                    nc.vector.scalar_tensor_tensor(
                                        osb[:, s, sl * 512:(sl + 1) * 512],
                                        xres[:, s * 512:(s + 1) * 512], 2.0,
                                        ps_at4[s][:, :], ALU.mult, ALU.add), tag="vs")
                            xres_use = m_stt[sl][3]
                        for s in range(4):
                            st2 = st2g[:, s:s + 1]
                            st3 = st3g[:, s:s + 1]
                            st4 = st4g[:, s:s + 1]
                            negb2 = negb2g[:, s:s + 1]
                            m_sum = R.op("vector", lambda E, st2=st2, s=s: E.reduce_sum(
                                st2, osb[:, s, :], axis=AXX), tag="vs")
                            R.wt("scalar", m_stt[3][s])
                            m_ssq = R.op("scalar", lambda E, st3=st3, s=s: E.activation(
                                dump[:, :], osb[:, s, :], AF.Square, accum_out=st3),
                                tag="cs")
                            R.wt("vector", m_sum)
                            R.wt("vector", m_ssq)
                            m1 = R.op("vector", lambda E, st2=st2: E.tensor_scalar(
                                st2, st2, 1.0 / H, None, ALU.mult), tag="vs")
                            R.wt("vector", m1)
                            m2 = R.op("vector", lambda E, st3=st3: E.tensor_scalar(
                                st3, st3, 1.0 / H, None, ALU.mult), tag="vs")
                            R.wt("vector", m2)
                            m3 = R.op("vector", lambda E, st4=st4, st2=st2: E.tensor_mul(
                                st4, st2, st2), tag="vs")
                            R.wt("vector", m3)
                            m4 = R.op("vector", lambda E, st4=st4, st3=st3: nc.vector.scalar_tensor_tensor(
                                st4, st4, -1.0, st3, ALU.mult, ALU.add),
                                tag="vs")
                            R.wt("vector", m4)
                            m5 = R.op("vector", lambda E, st4=st4: E.tensor_scalar(
                                st4, st4, EPS2, None, ALU.add), tag="vs")
                            R.wt("scalar", m5)
                            m6 = R.op("scalar", lambda E, st4=st4: E.activation(
                                st4, st4, AF.Sqrt), tag="cs")
                            R.wt("vector", m6)
                            m7 = R.op("vector", lambda E, st4=st4: E.reciprocal(st4, st4),
                                      tag="vs")
                            R.wt("vector", m7)
                            m8 = R.op("vector", lambda E, negb2=negb2, st2=st2, st4=st4: nc.vector.scalar_tensor_tensor(
                                negb2, st2, -1.0, st4, ALU.mult, ALU.mult),
                                tag="vs")
                            R.wt("scalar", m8)
                            R.wt("scalar", x2f_use)
                            m_x2 = R.op("scalar", lambda E, negb2=negb2, st4=st4, s=s:
                                        E.activation(
                                            x2f[:, :], osb[:, s, :], AF.Identity,
                                            bias=negb2, scale=st4), tag="cs")
                            osb_use[s] = m_x2
                            R.wt("gpsimd", m_x2)
                            m_sx2 = R.op("gpsimd", lambda E, g=g, s=s: E.dma_start(
                                out=s_x2[g * 512 + s * 128:g * 512 + (s + 1) * 128, :],
                                in_=x2f[:, :]), tag="x2st")
                            x2f_use = m_sx2
                            R.wt("vector", m_x2)
                            R.wt("vector", x2tx_prev)
                            m_sg = R.op("vector", lambda E: E.tensor_copy(
                                x2b[:, :].rearrange("p (t j q) -> p t j q", t=4, j=4),
                                x2f[:, :].rearrange("p (t q j) -> p t j q", t=4, j=4)), tag="vs")
                            R.wt("tensor", m_sg)
                            x2_evac = Z
                            for ng in range(4):
                                R.wt("tensor", x2_evac)
                                for k in range(4):
                                    n = ng * 4 + k
                                    m_tx = R.op("tensor", lambda E, n=n, k=k:
                                                nc.tensor.transpose(
                                                    ps_xt[:, k * 256:k * 256 + 128],
                                                    x2b[:, n * 128:(n + 1) * 128],
                                                    ident_b[:, :]),
                                                tag=("ts" if k == 3 else None))
                                R.wt("vector", m_tx)
                                R.wt("vector", x2stg_use)
                                x2_evac = R.op("vector", lambda E: E.tensor_copy(
                                    x2stg[:, :].rearrange("p (k q) -> p k q", k=4),
                                    ps_xt[:, :].rearrange("p (k q) -> p k q", k=4)[:, :, 0:128]),
                                    tag="vs")
                                R.wt("gpsimd", x2_evac)
                                tc = g * 4 + s
                                x2stg_use = R.op("gpsimd", lambda E, ng=ng, tc=tc: E.dma_start(
                                    out=s_x2t[4 * ng:4 * ng + 4, tc, :, :]
                                    .rearrange("n p k -> p n k"),
                                    in_=x2stg[:, :].rearrange("p (n k) -> p n k", n=4)),
                                    tag="x2tst")
                            x2tx_prev = ("ts", R.ch["ts"].n)
                            ge_done = (m_stt[3][3], ot_evac)
                            x2ev_done = x2_evac

            R.barrier(R.marks())

        if upto >= 3:
            with ExitStack() as p4:
                psb = lambda *a: p4.enter_context(nc.sbuf_tensor(*a))
                harena = psb("harena", [128, IC, 1024], BF16)
                x2t = psb("x2t", [128, HC, 1024], BF16)
                wstr = [psb("wstr%d" % i, [128, HC, 256], BF16) for i in range(4)]
                wdnb = [psb("wdnb%d" % i, [128, IC * 128], BF16) for i in range(2)]
                silu_t = psb("silu_t", [128, 1024], BF16)
                ysb = psb("ysb", [128, 1024], F32)
                ytst = psb("ytst", [128, 1024], F32)
                ytile = [psb("ytile0", [128, H], F32)]
                x2res = psb("x2res", [128, H], F32)
                s52 = psb("s52", [128, 1], F32)
                s53 = psb("s53", [128, 1], F32)
                s54 = psb("s54", [128, 1], F32)
                n5 = psb("n5", [128, 1], F32)

                wstr_use = [Z, Z, Z, Z]
                wdnb_use = [Z, Z]
                b_done = (Z, Z, Z)
                x2t_use = Z
                h_use = Z
                ysb_use = Z
                ytst_use = Z
                ytile_use = [Z]
                x2r_use = Z
                out_use = Z

                for Q in range(4):
                    R.wt("tensor", b_done[0])
                    R.wt("tensor", b_done[1])
                    R.wt("tensor", b_done[2])
                    R.wt("sync", x2t_use)
                    for n_ in range(HC):
                        m_x2t = R.op("sync", lambda E, Q=Q, n_=n_: E.dma_start(
                            out=x2t[:, n_, :].rearrange("p (tc k) -> p tc k", tc=8),
                            in_=s_x2t[n_, Q * 8:(Q + 1) * 8, :, :]
                            .rearrange("tc p k -> p tc k")), tag="x2t")
                    with ExitStack() as pA:
                        ps_g = [pA.enter_context(nc.psum_tensor(
                            "p4g%d_%d" % (Q, i), [128, 1024], F32)) for i in range(2)]
                        ps_u = [pA.enter_context(nc.psum_tensor(
                            "p4u%d_%d" % (Q, i), [128, 1024], F32)) for i in range(2)]
                        psg_use = [Z, Z]
                        psu_use = [Z, Z]
                        m_h = Z
                        for p in range(22):
                            gb, ub = 2 * (p % 2), 2 * (p % 2) + 1
                            R.wt("sync", wstr_use[gb])
                            m_wg = R.op("sync", lambda E, p=p, gb=gb: E.dma_start(
                                out=wstr[gb][:, :, :],
                                in_=s_wgu[p, :, :, :].rearrange("d p k -> p d k")),
                            tag="ws%d" % gb)
                            R.wt("sync", wstr_use[ub])
                            m_wu = R.op("sync", lambda E, p=p, ub=ub: E.dma_start(
                                out=wstr[ub][:, :, :],
                                in_=s_wgu[22 + p, :, :, :].rearrange("d p k -> p d k")),
                                tag="ws%d" % ub)
                            for jm in range(2):
                                m = p * 2 + jm
                                pb = m % 2
                                R.wt("tensor", m_wg)
                                R.wt("tensor", m_wu)
                                R.wt("tensor", m_x2t)
                                R.wt("tensor", psg_use[pb])
                                R.wt("tensor", psu_use[pb])
                                for dc in range(HC):
                                    for h2 in range(2):
                                        m_g = R.op("tensor",
                                                   lambda E, dc=dc, h2=h2, jm=jm, pb=pb, gb=gb:
                                                   nc.tensor.matmul(
                                                       ps_g[pb][:, h2 * 512:(h2 + 1) * 512],
                                                       wstr[gb][:, dc, jm * 128:(jm + 1) * 128],
                                                       x2t[:, dc, h2 * 512:(h2 + 1) * 512],
                                                       start=(dc == 0), stop=(dc == HC - 1)),
                                                   tag=("ts" if dc == HC - 1 and h2 == 1
                                                        else None))
                                for dc in range(HC):
                                    for h2 in range(2):
                                        m_u = R.op("tensor",
                                                   lambda E, dc=dc, h2=h2, jm=jm, pb=pb, ub=ub:
                                                   nc.tensor.matmul(
                                                       ps_u[pb][:, h2 * 512:(h2 + 1) * 512],
                                                       wstr[ub][:, dc, jm * 128:(jm + 1) * 128],
                                                       x2t[:, dc, h2 * 512:(h2 + 1) * 512],
                                                       start=(dc == 0), stop=(dc == HC - 1)),
                                                   tag=("ts" if dc == HC - 1 and h2 == 1
                                                        else None))
                                wstr_use[gb] = ("ts", R.ch["ts"].n)
                                wstr_use[ub] = ("ts", R.ch["ts"].n)
                                x2t_use = ("ts", R.ch["ts"].n)
                                R.wt("scalar", m_g)
                                R.wt("scalar", psg_use[pb])
                                R.wt("scalar", m_h)
                                m_si = R.op("scalar", lambda E, pb=pb: E.activation(
                                    silu_t[:, :], ps_g[pb][:, :], AF.Silu), tag="cs")
                                psg_use[pb] = m_si
                                R.wt("vector", m_si)
                                R.wt("vector", m_u)
                                R.wt("vector", h_use)
                                m_h = R.op("vector", lambda E, m=m, pb=pb: E.tensor_mul(
                                    harena[:, m, :], silu_t[:, :], ps_u[pb][:, :]), tag="vs")
                                psu_use[pb] = m_h
                        h_rdy = m_h
                    with ExitStack() as pB:
                        ps_y = [pB.enter_context(nc.psum_tensor(
                            "p4y%d_%d" % (Q, i), [128, 1024], F32)) for i in range(2)]
                        ps_yt = [pB.enter_context(nc.psum_tensor(
                            "p4t%d_%d" % (Q, i), [128, 512], F32)) for i in range(2)]
                        psy_use = [Z, Z]
                        yt_ev = Z
                        m_yst = Z
                        for hc in range(HC):
                            wb = hc % 2
                            pb = hc % 2
                            R.wt("sync", wdnb_use[wb])
                            m_wd = R.op("sync", lambda E, hc=hc, wb=wb: E.dma_start(
                                out=wdnb[wb][:, :].rearrange("p (mm k) -> p mm k", mm=IC),
                                in_=s_wdn[hc, :, :, :].rearrange("mm p k -> p mm k")),
                                tag="wd%d" % wb)
                            R.wt("tensor", m_wd)
                            R.wt("tensor", h_rdy)
                            R.wt("tensor", psy_use[pb])
                            wflat = wdnb[wb][:, :]
                            for m in range(IC):
                                for h2 in range(2):
                                    m_dn = R.op("tensor", lambda E, m=m, h2=h2, wflat=wflat,
                                                pb=pb: nc.tensor.matmul(
                                                    ps_y[pb][:, h2 * 512:(h2 + 1) * 512],
                                                    wflat[:, m * 128:(m + 1) * 128],
                                                    harena[:, m, h2 * 512:(h2 + 1) * 512],
                                                    start=(m == 0), stop=(m == IC - 1)),
                                                tag=("ts" if m == IC - 1 and h2 == 1 else None))
                            wdnb_use[wb] = ("ts", R.ch["ts"].n)
                            if hc == HC - 1:
                                h_use = ("ts", R.ch["ts"].n)
                            R.wt("vector", m_dn)
                            R.wt("vector", ysb_use)
                            m_ye = R.op("vector", lambda E, pb=pb: E.tensor_copy(
                                ysb[:, :], ps_y[pb][:, :]), tag="vs")
                            psy_use[pb] = m_ye
                            R.wt("tensor", m_ye)
                            for tg2 in range(2):
                                ptb = tg2 % 2
                                R.wt("tensor", yt_ev)
                                for k in range(4):
                                    tloc = tg2 * 4 + k
                                    m_yt = R.op("tensor", lambda E, tloc=tloc, k=k, ptb=ptb:
                                                nc.tensor.transpose(
                                                    ps_yt[ptb][:, k * 128:(k + 1) * 128],
                                                    ysb[:, tloc * 128:(tloc + 1) * 128],
                                                    ident_f[:, :]),
                                                tag=("ts" if k == 3 else None))
                                ysb_use = ("ts", R.ch["ts"].n)
                                R.wt("vector", m_yt)
                                R.wt("vector", ytst_use)
                                yt_ev = R.op("vector", lambda E, tg2=tg2, ptb=ptb: E.tensor_copy(
                                    ytst[:, tg2 * 512:(tg2 + 1) * 512], ps_yt[ptb][:, :]),
                                    tag="vs")
                                R.wt("gpsimd", yt_ev)
                                m_yst = R.op("gpsimd", lambda E, Q=Q, hc=hc, tg2=tg2:
                                             E.dma_start(
                                                 out=s_y[Q * 8 + tg2 * 4:Q * 8 + tg2 * 4 + 4,
                                                         hc, :, :].rearrange("t p k -> p t k"),
                                                 in_=ytst[:, tg2 * 512:(tg2 + 1) * 512]
                                                 .rearrange("p (t k) -> p t k", t=4)), tag="syst")
                                ytst_use = m_yst
                        b_done = (psy_use[0], psy_use[1], yt_ev)
                    for tl in range(8):
                        tsub = Q * 8 + tl
                        yb = 0
                        R.wt("gpsimd", ytile_use[yb])
                        R.wt("gpsimd", m_yst)
                        m_y = R.op("gpsimd", lambda E, tsub=tsub, yb=yb: E.dma_start(
                            out=ytile[yb][:, :].rearrange("p (c k) -> p c k", c=HC),
                            in_=s_y[tsub, :, :, :].rearrange("c p k -> p c k")), tag="yld")
                        R.wt("gpsimd", x2r_use)
                        R.wt("gpsimd", out_use)
                        m_x2r = R.op("gpsimd", lambda E, tsub=tsub: E.dma_start(
                            out=x2res[:, :], in_=s_x2[tsub * 128:(tsub + 1) * 128, :]),
                            tag="x2r")
                        R.wt("scalar", m_y)
                        R.wt("scalar", ("vs", R.ch["vs"].n))
                        R.wt("vector", ("cs", R.ch["cs"].n))
                        m_sq = R.op("scalar", lambda E, yb=yb: E.activation(
                            dump[:, :], ytile[yb][:, :], AF.Square, accum_out=s52[:, 0:1]),
                            tag="cs")
                        R.wt("vector", m_sq)
                        m1 = R.op("vector", lambda E: E.tensor_scalar(
                            s52[:, :], s52[:, :], 1.0 / H, EPSR, ALU.mult, ALU.add), tag="vs")
                        R.wt("scalar", m1)
                        m2 = R.op("scalar", lambda E: E.activation(
                            s52[:, :], s52[:, :], AF.Sqrt), tag="cs")
                        R.wt("vector", m2)
                        m3 = R.op("vector", lambda E: E.reciprocal(s52[:, :], s52[:, :]),
                                  tag="vs")
                        R.wt("vector", m3)
                        R.wt("vector", m_x2r)
                        m_r3 = R.op("vector", lambda E, yb=yb: nc.vector.scalar_tensor_tensor(
                            ytile[yb][:, :], ytile[yb][:, :], s52[:, 0:1], x2res[:, :],
                            ALU.mult, ALU.add), tag="vs")
                        x2r_use = m_r3
                        R.wt("vector", m_r3)
                        m_sum = R.op("vector", lambda E, yb=yb: E.reduce_sum(
                            s53[:, 0:1], ytile[yb][:, :], axis=AXX), tag="vs")
                        R.wt("scalar", m_r3)
                        m_ssq = R.op("scalar", lambda E, yb=yb: E.activation(
                            dump[:, :], ytile[yb][:, :], AF.Square, accum_out=s54[:, 0:1]),
                            tag="cs")
                        R.wt("vector", m_sum)
                        R.wt("vector", m_ssq)
                        m4 = R.op("vector", lambda E: E.tensor_scalar(
                            s53[:, :], s53[:, :], 1.0 / H, None, ALU.mult), tag="vs")
                        R.wt("vector", m4)
                        m5 = R.op("vector", lambda E: E.tensor_scalar(
                            s54[:, :], s54[:, :], 1.0 / H, None, ALU.mult), tag="vs")
                        R.wt("vector", m5)
                        m6 = R.op("vector", lambda E: E.tensor_mul(
                            n5[:, :], s53[:, :], s53[:, :]), tag="vs")
                        R.wt("vector", m6)
                        m7 = R.op("vector", lambda E: nc.vector.scalar_tensor_tensor(
                            n5[:, :], n5[:, :], -1.0, s54[:, :], ALU.mult, ALU.add), tag="vs")
                        R.wt("vector", m7)
                        m8 = R.op("vector", lambda E: E.tensor_scalar(
                            n5[:, :], n5[:, :], EPS2, None, ALU.add), tag="vs")
                        R.wt("scalar", m8)
                        m9 = R.op("scalar", lambda E: E.activation(
                            n5[:, :], n5[:, :], AF.Sqrt), tag="cs")
                        R.wt("vector", m9)
                        m10 = R.op("vector", lambda E: E.reciprocal(n5[:, :], n5[:, :]),
                                   tag="vs")
                        R.wt("vector", m10)
                        m11 = R.op("vector", lambda E: nc.vector.scalar_tensor_tensor(
                            s53[:, :], s53[:, :], -1.0, n5[:, :], ALU.mult, ALU.mult), tag="vs")
                        R.wt("scalar", m11)
                        R.wt("scalar", out_use)
                        m_out = R.op("scalar", lambda E, yb=yb: E.activation(
                            x2res[:, :], ytile[yb][:, :], AF.Identity, bias=s53[:, 0:1],
                            scale=n5[:, 0:1]), tag="cs")
                        ytile_use[yb] = m_out
                        R.wt("gpsimd", m_out)
                        out_use = R.op("gpsimd", lambda E, tsub=tsub: E.dma_start(
                            out=out[tsub * 128:(tsub + 1) * 128, :], in_=x2res[:, :]),
                            tag="oust")

        for name, c in list(R.ch.items()):
            if c.step == 16:
                R.wt("gpsimd", (name, c.n))

        with nc.Block() as block:
            for e in ENGINES:
                getattr(block, e)(lambda E, e=e: [f(E) for f in R.q[e]])
    return nc



def kernel(L0_attn_x, L0_attn_qkv, L0_attn_o_proj, L0_attn_gamma, L0_attn_beta,
           L0_ffn_gate_up_packed, L0_ffn_gate_up_scales, L0_ffn_down_packed,
           L0_ffn_down_scales):
    x = np.ascontiguousarray(np.asarray(L0_attn_x, dtype=np.float32))
    B = x.shape[0]
    wqkv = np.ascontiguousarray(np.asarray(L0_attn_qkv, dtype=np.float32))
    wo_ = np.ascontiguousarray(np.asarray(L0_attn_o_proj, dtype=np.float32))
    gamma = np.asarray(L0_attn_gamma, dtype=np.float32)
    beta = np.asarray(L0_attn_beta, dtype=np.float32)
    gup_ = np.ascontiguousarray(np.asarray(L0_ffn_gate_up_packed, dtype=np.int32))
    gus_ = np.ascontiguousarray(np.asarray(L0_ffn_gate_up_scales, dtype=np.float16))
    dnp_ = np.ascontiguousarray(np.asarray(L0_ffn_down_packed, dtype=np.int32))
    dns_ = np.ascontiguousarray(np.asarray(L0_ffn_down_scales, dtype=np.float16))

    gam_pp = np.ascontiguousarray(gamma.reshape(HC, 128).T)
    bet_pp = np.ascontiguousarray(beta.reshape(HC, 128).T)

    PERM = {0: [0, 3, 1, 2], 1: [1, 2, 0, 3]}
    in_maps = []
    for b in range(B):
        for t in range(2):
            xl = np.ascontiguousarray(x[b].reshape(4, 2048, H)[PERM[t]].reshape(S, H))
            onoff = np.zeros((128, 2), np.float32)
            onoff[:, 0] = NEG if t == 0 else 0.0
            onoff[:, 1] = 0.0 if t == 0 else NEG
            in_maps.append(dict(x=xl, wqkv=wqkv, wo=wo_, gam=gam_pp, bet=bet_pp,
                                onoff=onoff, gup=gup_, gus=gus_, dnp=dnp_, dns=dns_))

    nc = build()
    res = run_bass_kernel_spmd(nc, in_maps, core_ids=list(range(8)))
    outp = np.empty((B, 4, 2048, H), np.float32)
    for b in range(B):
        oa = res.results[2 * b]["out"]
        ob = res.results[2 * b + 1]["out"]
        outp[b, 0] = oa[0:2048]
        outp[b, 3] = oa[2048:4096]
        outp[b, 1] = ob[0:2048]
        outp[b, 2] = ob[2048:4096]
    return outp.reshape(B, S, H)
